# version 13
# baseline (speedup 1.0000x reference)
import contextlib
import numpy as np
import ml_dtypes
import concourse.bass as bass
import concourse.mybir as mybir
from concourse.bass_utils import run_bass_kernel_spmd
from concourse.bass_core_rust import IndirectOffsetOnAxis

F32 = mybir.dt.float32
BF16 = mybir.dt.bfloat16
F16 = mybir.dt.float16
I32 = mybir.dt.int32
U16 = mybir.dt.uint16
ALU = mybir.AluOpType
AF = mybir.ActivationFunctionType
AX = mybir.AxisListType.X

D = 2048
T = 4096
DT = D // 128
PN = 512
NPAN = T // PN
I_FF = 5632
IT_FF = I_FF // 128
GU_OUT = 2 * I_FF
EPS_LN = 1e-5
EPS_RMS = 1e-6
NH = 16
HD = 128
BLK = 256
NBLK = T // BLK
NS = 16384
NE = 8
NA = 64

bf = ml_dtypes.bfloat16
FCONSTS = [EPS_LN, EPS_RMS, -0.5, 0.5, 1.0, 0.0, 2.0, -1.0, 1.0 / D, -256.0,
           0.08838834764831845, 1e-30, 3.0, 4.0, 5.0, 6.0]


class Prog:

    def __init__(self, name):
        self.nc = bass.Bass("TRN2", target_bir_lowering=False)
        self.nc.detect_race_conditions = False
        self.sems = {}
        self.cnt = {}
        self.consts = {}
        self.stack = contextlib.ExitStack()
        self.scopes = []

    def sem(self, name):
        if name not in self.sems:
            self.sems[name] = self.nc.alloc_semaphore(name)
            self.cnt[name] = 0
        return self.sems[name]

    def inc(self, name, instr, k=1):
        instr.then_inc(self.sem(name), k)
        self.cnt[name] += k
        return self.cnt[name]

    def n(self, name):
        self.sem(name)
        return self.cnt[name]

    def sbuf(self, name, shape, dtype):
        st = self.scopes[-1] if self.scopes else self.stack
        return st.enter_context(self.nc.sbuf_tensor(name, shape, dtype))

    def psum(self, name, shape, dtype=F32):
        st = self.scopes[-1] if self.scopes else self.stack
        return st.enter_context(self.nc.psum_tensor(name, shape, dtype))

    def open_scope(self):
        self.scopes.append(contextlib.ExitStack())
        self.scope_sems = getattr(self, "scope_sems", [])
        self.scope_sems.append(set(self.sems.keys()))

    def close_scope(self, recycle=False):
        self.scopes.pop().close()
        base = self.scope_sems.pop()
        if recycle:
            new = [n for n in self.sems if n not in base and n != "bar"]
            emit_barrier(self, "scx")
            self.nc.clear_and_free_semaphores([self.sems[n] for n in new])
            emit_barrier(self, "scy")
            for n in new:
                del self.sems[n]
                del self.cnt[n]

    def close(self):
        while self.scopes:
            self.close_scope()
        self.stack.close()


def emit_barrier(P, tag):
    nc = P.nc
    s = P.sem("bar")
    base = P.cnt["bar"]
    target = base + 5
    P.cnt["bar"] = target
    with nc.Block() as blk:
        for eng_name in ("sync", "tensor", "vector", "scalar", "gpsimd"):
            def body(eng):
                try:
                    eng.drain()
                except Exception:
                    pass
                eng.nop().then_inc(s, 1)
                eng.wait_ge(s, target)
            getattr(blk, eng_name)(body)


def emit_consts(P, dram):
    nc = P.nc
    C = {}
    C["ones_bf"] = P.stack.enter_context(nc.sbuf_tensor("s_ones_bf", [128, 128], BF16))
    C["ident_bf"] = P.stack.enter_context(nc.sbuf_tensor("s_ident_bf", [128, 128], BF16))
    C["ident_f32"] = P.stack.enter_context(nc.sbuf_tensor("s_ident_f32", [128, 128], F32))
    C["ones1_bf"] = P.stack.enter_context(nc.sbuf_tensor("s_ones1_bf", [1, 128], BF16))
    C["iota128"] = P.stack.enter_context(nc.sbuf_tensor("s_iota128", [128, 128], F32))
    C["arr256"] = P.stack.enter_context(nc.sbuf_tensor("s_arr256", [1, 256], F32))
    C["rarr256"] = P.stack.enter_context(nc.sbuf_tensor("s_rarr256", [1, 256], F32))
    C["idxhi"] = P.stack.enter_context(nc.sbuf_tensor("s_idxhi", [128, 256], F32))
    C["idxlo"] = P.stack.enter_context(nc.sbuf_tensor("s_idxlo", [128, 256], F32))
    C["mskhi"] = P.stack.enter_context(nc.sbuf_tensor("s_mskhi", [128, 256], F32))
    C["msklo"] = P.stack.enter_context(nc.sbuf_tensor("s_msklo", [128, 256], F32))
    C["iota8"] = P.stack.enter_context(nc.sbuf_tensor("s_iota8", [128, 8], F32))
    C["cmadd"] = P.stack.enter_context(nc.sbuf_tensor("s_cmadd", [128, 4, 512], BF16))
    C["ones_f32"] = P.stack.enter_context(nc.sbuf_tensor("s_ones_f32", [128, 1], F32))
    C["fconst"] = P.stack.enter_context(nc.sbuf_tensor("s_fconst", [128, len(FCONSTS)], F32))
    P.consts = C
    for i, v in enumerate(FCONSTS):
        nc.const_aps.register(F32, v, C["fconst"][:, i:i + 1])
    with P.nc.Block() as blk:
        def body(g):
            n0 = P.n("cst")
            for key in ("ones_bf", "ident_bf", "ident_f32", "ones1_bf", "iota128",
                        "arr256", "rarr256", "idxhi", "idxlo", "mskhi", "msklo", "iota8",
                        "cmadd", "ones_f32", "fconst"):
                P.inc("cst", g.dma_start(C[key][:], dram["c_" + key][:]), 16)
            g.wait_ge(P.sem("cst"), P.n("cst"))
        blk.gpsimd(body)
    emit_barrier(P, "consts")


def host_consts():
    out = {}
    out["c_ones_bf"] = np.ones((128, 128), dtype=bf)
    out["c_ident_bf"] = np.eye(128, dtype=bf)
    out["c_ident_f32"] = np.eye(128, dtype=np.float32)
    out["c_ones1_bf"] = np.ones((1, 128), dtype=bf)
    out["c_iota128"] = np.broadcast_to(np.arange(128, dtype=np.float32), (128, 128)).copy()
    arr = np.arange(1, 257, dtype=np.float32)
    out["c_arr256"] = arr.reshape(1, 256).copy()
    out["c_rarr256"] = (256.0 - arr).reshape(1, 256).copy()
    m = np.arange(256, dtype=np.float32)[None, :]
    nhi = np.arange(128, dtype=np.float32)[:, None]
    nlo = nhi + 128
    out["c_idxhi"] = (m - nhi).astype(np.float32)
    out["c_idxlo"] = (m - nlo).astype(np.float32)
    out["c_mskhi"] = (m >= nhi).astype(np.float32)
    out["c_msklo"] = (m >= nlo).astype(np.float32)
    out["c_iota8"] = np.broadcast_to(np.arange(8, dtype=np.float32), (128, 8)).copy()
    r_ = np.arange(128)[:, None]
    c_ = np.arange(512)[None, :]
    cm = np.where(np.stack([(c_ >= j * 128 + r_) for j in range(4)], axis=1),
                  0.0, -10000.0).astype(bf)
    out["c_cmadd"] = np.ascontiguousarray(cm)
    out["c_ones_f32"] = np.ones((128, 1), dtype=np.float32)
    out["c_fconst"] = np.broadcast_to(np.array(FCONSTS, dtype=np.float32), (128, len(FCONSTS))).copy()
    return out




def emit_dequant(P, tag, pk_ap, sc_ap, w_ap, IN, OUT, fslice=512):
    nc = P.nc
    IPT = IN // 8
    n_ip = (IPT + 127) // 128
    n_f = OUT // fslice
    sb = P.sbuf
    pks = [sb(f"{tag}_pk{b}", [128, fslice], I32) for b in range(2)]
    scs = [sb(f"{tag}_sc{b}", [128, fslice], F32) for b in range(2)]
    wt = [sb(f"{tag}_wt{b}", [128, 8, fslice], BF16) for b in range(2)]
    nib = sb(f"{tag}_nib", [128, fslice], I32)
    mm_ = sb(f"{tag}_m", [128, fslice], I32)
    t2 = sb(f"{tag}_t2", [128, fslice], I32)
    t3 = sb(f"{tag}_t3", [128, fslice], I32)
    cnd = sb(f"{tag}_cnd", [128, fslice], I32)
    bits = sb(f"{tag}_bits", [128, fslice], I32)
    sgn = sb(f"{tag}_sgn", [128, fslice], I32)
    u16 = sb(f"{tag}_u16", [128, fslice], U16)

    tiles = []
    for it in range(n_ip):
        prt = min(128, IPT - it * 128)
        for fi in range(n_f):
            tiles.append((it, fi, prt))
    NTL = len(tiles)

    ld = [f"{tag}_ld0", f"{tag}_ld1"]
    stq = [f"{tag}_st0", f"{tag}_st1"]
    vv = f"{tag}_vv"
    ld_tot = [32 * (i // 2 + 1) for i in range(NTL)]
    st_tot = [8 * 16 * (i // 2 + 1) for i in range(NTL)]

    with nc.Block() as blk:
        def g_body(g):
            for i, (it, fi, prt) in enumerate(tiles):
                b = i % 2
                if i >= 2:
                    g.wait_ge(P.sem(stq[b]), st_tot[i - 2])
                P.inc(ld[b], g.dma_start(
                    pks[b][0:prt, :], pk_ap[it * 128:it * 128 + prt,
                                            fi * fslice:(fi + 1) * fslice]), 16)
                g0 = (it * 128) // 16
                ngr = (prt + 15) // 16
                src = bass.AP(sc_ap.tensor, sc_ap.offset + g0 * OUT + fi * fslice,
                              [[OUT, ngr], [0, 16], [1, fslice]])
                P.inc(ld[b], g.dma_start(scs[b][0:prt, :], src), 16)
                assert P.n(ld[b]) == ld_tot[i]
        blk.gpsimd(g_body)

        def v_body(v):
            for i, (it, fi, prt) in enumerate(tiles):
                b = i % 2
                v.wait_ge(P.sem(ld[b]), ld_tot[i])
                if i >= 2:
                    v.wait_ge(P.sem(stq[b]), st_tot[i - 2])
                pt = pks[b][0:prt, :]
                for j in range(8):
                    v.tensor_scalar(nib[0:prt, :], pt, 4 * j, 15,
                                    ALU.logical_shift_right, ALU.bitwise_and)
                    v.tensor_single_scalar(mm_[0:prt, :], nib[0:prt, :], 7, ALU.bitwise_and)
                    v.tensor_scalar(t2[0:prt, :], mm_[0:prt, :], 64, 0x3F00, ALU.mult, ALU.add)
                    v.tensor_single_scalar(t3[0:prt, :], mm_[0:prt, :], 0x3F00, ALU.mult)
                    v.tensor_single_scalar(cnd[0:prt, :], mm_[0:prt, :], 2, ALU.is_lt)
                    v.select(bits[0:prt, :], cnd[0:prt, :], t3[0:prt, :], t2[0:prt, :])
                    v.tensor_scalar(sgn[0:prt, :], nib[0:prt, :], 8, 12,
                                    ALU.bitwise_and, ALU.logical_shift_left)
                    v.tensor_tensor(bits[0:prt, :], bits[0:prt, :], sgn[0:prt, :], ALU.add)
                    v.tensor_copy(u16[0:prt, :], bits[0:prt, :])
                    P.inc(vv, v.tensor_tensor(wt[b][0:prt, j, :],
                                              u16[0:prt, :].bitcast(BF16),
                                              scs[b][0:prt, :], ALU.mult))
        blk.vector(v_body)

        def s_body(sy):
            k = 0
            for i, (it, fi, prt) in enumerate(tiles):
                b = i % 2
                k += 8
                sy.wait_ge(P.sem(vv), k)
                for j in range(8):
                    dst = bass.AP(w_ap.tensor,
                                  w_ap.offset + (it * 1024 + j) * OUT + fi * fslice,
                                  [[8 * OUT, prt], [1, fslice]])
                    P.inc(stq[b], sy.dma_start(dst, wt[b][0:prt, j, :]), 16)
                assert P.n(stq[b]) == st_tot[i]
            sy.wait_ge(P.sem(stq[0]), P.n(stq[0]))
            sy.wait_ge(P.sem(stq[1]), P.n(stq[1]))
        blk.sync(s_body)


def emit_ln(P, tag, x_ap, xln_ap):
    nc = P.nc
    sb = P.sbuf
    C = P.consts
    xp = [sb(f"{tag}_xp{b}", [128, DT, PN], F32) for b in range(2)]
    bfp = [sb(f"{tag}_bfp{b}", [128, DT, 2 * PN], BF16) for b in range(2)]
    ms = sb(f"{tag}_ms", [128, PN], F32)
    vq = sb(f"{tag}_vq", [128, PN], F32)
    var = sb(f"{tag}_var", [128, PN], F32)
    rstd = sb(f"{tag}_rstd", [128, PN], F32)
    mr = sb(f"{tag}_mr", [128, PN], F32)
    tacc = sb(f"{tag}_tacc", [128, PN], F32)
    xst = [sb(f"{tag}_xst{b}", [128, DT, PN], BF16) for b in range(2)]
    pss = [P.psum(f"{tag}_pss{b}", [128, PN], F32) for b in range(2)]
    psq = [P.psum(f"{tag}_psq{b}", [128, PN], F32) for b in range(2)]

    ld = [f"{tag}_ld0", f"{tag}_ld1"]
    od = [f"{tag}_od0", f"{tag}_od1"]
    sqd, bfd, smd, smv, rsd, appd, pfree, xfree = (
        f"{tag}_sq", f"{tag}_bf", f"{tag}_sm", f"{tag}_smv", f"{tag}_rs",
        f"{tag}_ap", f"{tag}_pf", f"{tag}_xf")
    ld_tot = [DT * 16 * (p // 2 + 1) for p in range(NPAN)]
    od_tot = [DT * 16 * (p // 2 + 1) for p in range(NPAN)]

    with nc.Block() as blk:
        def g_body(g):
            for p in range(NPAN):
                b = p % 2
                if p >= 2:
                    g.wait_ge(P.sem(appd), p - 1)
                for k in range(DT):
                    P.inc(ld[b], g.dma_start(xp[b][:, k, :],
                                             x_ap[k * 128:(k + 1) * 128,
                                                  p * PN:(p + 1) * PN]), 16)
                assert P.n(ld[b]) == ld_tot[p]
        blk.gpsimd(g_body)

        def sc_body(s):
            for p in range(NPAN):
                b = p % 2
                if p >= 2:
                    s.wait_ge(P.sem(smd), (p - 1) * 2)
                s.wait_ge(P.sem(ld[b]), ld_tot[p])
                for k in range(DT):
                    P.inc(sqd, s.activation(bfp[b][:, k, PN:2 * PN], xp[b][:, k, :],
                                            AF.Square))
                s.wait_ge(P.sem(smv), p + 1)
                s.activation(var[:], var[:], AF.Ln, bias=EPS_LN)
                P.inc(rsd, s.activation(rstd[:], var[:], AF.Exp, scale=-0.5))
        blk.scalar(sc_body)

        def v_body(v):
            for p in range(NPAN):
                b = p % 2
                if p >= 2:
                    v.wait_ge(P.sem(smd), (p - 1) * 2)
                v.wait_ge(P.sem(ld[b]), ld_tot[p])
                for k in range(DT):
                    P.inc(bfd, v.tensor_copy(bfp[b][:, k, 0:PN], xp[b][:, k, :]))
                v.wait_ge(P.sem(smd), p * 2 + 2)
                v.tensor_single_scalar(ms[:], pss[b][:], 1.0 / D, ALU.mult)
                P.inc(pfree, v.tensor_single_scalar(vq[:], psq[b][:], 1.0 / D, ALU.mult))
                v.tensor_tensor(tacc[:], ms[:], ms[:], ALU.mult)
                P.inc(smv, v.tensor_tensor(var[:], vq[:], tacc[:], ALU.subtract))
                v.wait_ge(P.sem(rsd), p + 1)
                if p >= 2:
                    v.wait_ge(P.sem(od[b]), od_tot[p - 2])
                v.tensor_tensor(mr[:], ms[:], rstd[:], ALU.mult)
                for k in range(DT):
                    v.tensor_tensor(tacc[:], xp[b][:, k, :], rstd[:], ALU.mult)
                    last = v.tensor_tensor(xst[b][:, k, :], tacc[:], mr[:], ALU.subtract)
                P.inc(appd, last)
        blk.vector(v_body)

        def t_body(t):
            for p in range(NPAN):
                b = p % 2
                if p >= 2:
                    t.wait_ge(P.sem(pfree), p - 1)
                t.wait_ge(P.sem(sqd), (p + 1) * DT)
                t.wait_ge(P.sem(bfd), (p + 1) * DT)
                for k in range(DT):
                    t.matmul(pss[b][:], C["ones_bf"][:], bfp[b][:, k, 0:PN],
                             start=(k == 0), stop=(k == DT - 1))
                    last = t.matmul(psq[b][:], C["ones_bf"][:], bfp[b][:, k, PN:2 * PN],
                                    start=(k == 0), stop=(k == DT - 1))
                P.inc(smd, last, 2)
        blk.tensor(t_body)

        def s_out(sy):
            for p in range(NPAN):
                b = p % 2
                sy.wait_ge(P.sem(appd), p + 1)
                for k in range(DT):
                    P.inc(od[b], sy.dma_start(xln_ap[k * 128:(k + 1) * 128,
                                                     p * PN:(p + 1) * PN],
                                              xst[b][:, k, :]), 16)
                assert P.n(od[b]) == od_tot[p]
            sy.wait_ge(P.sem(od[0]), P.n(od[0]))
            sy.wait_ge(P.sem(od[1]), P.n(od[1]))
        blk.sync(s_out)


def emit_l0ffn_gateup(P, tag, xln_ap, wgu_ap, h_ap, wgu_tiled=True):
    nc = P.nc
    sb = P.sbuf
    xr = sb(f"{tag}_xr", [128, DT, T], BF16)
    wg = [sb(f"{tag}_wg{b}", [128, DT, 256], BF16) for b in range(2)]
    wu = [sb(f"{tag}_wu{b}", [128, DT, 256], BF16) for b in range(2)]
    sg = sb(f"{tag}_sg", [128, 2, PN], F32)
    ht = sb(f"{tag}_ht", [128, 2, PN], BF16)
    psg = [P.psum(f"{tag}_psg{b}", [128, PN], F32) for b in range(2)]
    psu = [P.psum(f"{tag}_psu{b}", [128, PN], F32) for b in range(2)]

    ldx = f"{tag}_ldx"
    ldw = [f"{tag}_ldw0", f"{tag}_ldw1"]
    hst = [f"{tag}_hs0", f"{tag}_hs1"]
    mmg, mmu, sgd, hdone, wfree, pfree = (
        f"{tag}_mmg", f"{tag}_mmu", f"{tag}_sgd", f"{tag}_hd", f"{tag}_wf",
        f"{tag}_pf")

    NI2 = IT_FF // 2
    NIDX = IT_FF * NPAN
    nwl = 2 if wgu_tiled else 2 * DT
    ldw_tot = [nwl * 16 * (s2 // 2 + 1) for s2 in range(NI2)]
    hst_tot = [16 * (i // 2 + 1) for i in range(NIDX)]

    with nc.Block() as blk:
        def g_body(g):
            for k in range(DT):
                P.inc(ldx, g.dma_start(xr[:, k, :], xln_ap[k * 128:(k + 1) * 128, :]), 16)
            for s2 in range(NI2):
                b = s2 % 2
                if s2 >= 2:
                    g.wait_ge(P.sem(mmu), (s2 - 1) * 2 * NPAN)
                if wgu_tiled:
                    P.inc(ldw[b], g.dma_start(wg[b][:, :, :], bass.AP(
                        wgu_ap.tensor, wgu_ap.offset + s2 * (128 * DT * 256),
                        [[DT * 256, 128], [1, DT * 256]])), 16)
                    P.inc(ldw[b], g.dma_start(wu[b][:, :, :], bass.AP(
                        wgu_ap.tensor, wgu_ap.offset + (22 + s2) * (128 * DT * 256),
                        [[DT * 256, 128], [1, DT * 256]])), 16)
                else:
                    for k in range(DT):
                        P.inc(ldw[b], g.dma_start(
                            wg[b][:, k, :],
                            wgu_ap[k * 128:(k + 1) * 128,
                                   s2 * 256:(s2 + 1) * 256]), 16)
                        P.inc(ldw[b], g.dma_start(
                            wu[b][:, k, :],
                            wgu_ap[k * 128:(k + 1) * 128,
                                   I_FF + s2 * 256:I_FF + (s2 + 1) * 256]), 16)
                assert P.n(ldw[b]) == ldw_tot[s2]
        blk.gpsimd(g_body)

        def t_body(t):
            t.wait_ge(P.sem(ldx), DT * 16)
            for s2 in range(NI2):
                b = s2 % 2
                t.wait_ge(P.sem(ldw[b]), ldw_tot[s2])
                for ii in range(2):
                    for p in range(NPAN):
                        idx = (s2 * 2 + ii) * NPAN + p
                        pb = idx % 2
                        if idx >= 2:
                            t.wait_ge(P.sem(hdone), idx - 1)
                        for k in range(DT):
                            last = t.matmul(psg[pb][:], wg[b][:, k, ii * 128:(ii + 1) * 128],
                                            xr[:, k, p * PN:(p + 1) * PN],
                                            start=(k == 0), stop=(k == DT - 1))
                        P.inc(mmg, last)
                        for k in range(DT):
                            last = t.matmul(psu[pb][:], wu[b][:, k, ii * 128:(ii + 1) * 128],
                                            xr[:, k, p * PN:(p + 1) * PN],
                                            start=(k == 0), stop=(k == DT - 1))
                        P.inc(mmu, last)
        blk.tensor(t_body)

        def sc_body(s):
            for idx in range(NIDX):
                pb = idx % 2
                if idx >= 2:
                    s.wait_ge(P.sem(hdone), idx - 1)
                s.wait_ge(P.sem(mmg), idx + 1)
                P.inc(sgd, s.activation(sg[:, pb, :], psg[pb][:], AF.Silu))
        blk.scalar(sc_body)

        def v_body(v):
            for idx in range(NIDX):
                pb = idx % 2
                if idx >= 2:
                    v.wait_ge(P.sem(hst[pb]), hst_tot[idx - 2])
                v.wait_ge(P.sem(sgd), idx + 1)
                v.wait_ge(P.sem(mmu), idx + 1)
                P.inc(hdone, v.tensor_tensor(ht[:, pb, :], sg[:, pb, :], psu[pb][:],
                                             ALU.mult))
        blk.vector(v_body)

        def s_out(sy):
            for idx in range(NIDX):
                i = idx // NPAN
                p = idx % NPAN
                pb = idx % 2
                sy.wait_ge(P.sem(hdone), idx + 1)
                dst = bass.AP(h_ap.tensor, h_ap.offset + p * 128 * IT_FF * PN
                              + i * PN, [[IT_FF * PN, 128], [1, PN]])
                P.inc(hst[pb], sy.dma_start(dst, ht[:, pb, :]), 16)
                assert P.n(hst[pb]) == hst_tot[idx]
            sy.wait_ge(P.sem(hst[0]), P.n(hst[0]))
            sy.wait_ge(P.sem(hst[1]), P.n(hst[1]))
        blk.sync(s_out)


def emit_gateup_dq(P, tag, xln_ap, pkg_ap, scg_ap, pkd_ap, scd_ap, wgu_ap,
                   wdn_ap, h_ap):
    nc = P.nc
    sb = P.sbuf
    C = P.consts
    FS = 512
    tiles = []
    for fp in range(11):
        for fi in (fp, 11 + fp):
            for it in range(2):
                tiles.append(("g", it, fi, 128))
    for it in range(6):
        prt = 128 if it < 5 else 64
        for fi in range(4):
            tiles.append(("d", it, fi, prt))
    NTL = len(tiles)
    NGT = 44

    pks = [sb(f"{tag}_pk{b}", [128, FS], I32) for b in range(2)]
    scs = [sb(f"{tag}_sc{b}", [128, FS], F32) for b in range(2)]
    wt = sb(f"{tag}_wt", [128, 8, FS], BF16)
    nib = sb(f"{tag}_nib", [128, FS], I32)
    mm_ = sb(f"{tag}_m", [128, FS], I32)
    t2 = sb(f"{tag}_t2", [128, FS], I32)
    bits = sb(f"{tag}_bits", [128, FS], I32)
    sgn = sb(f"{tag}_sgn", [128, FS], I32)
    u16 = sb(f"{tag}_u16", [128, FS], U16)

    xr = sb(f"{tag}_xr", [128, DT, T], BF16)
    wg = [sb(f"{tag}_wg{b}", [128, DT, 128], BF16) for b in range(2)]
    wu = [sb(f"{tag}_wu{b}", [128, DT, 128], BF16) for b in range(2)]
    sg = sb(f"{tag}_sg", [128, 2, PN], BF16)
    ht = sb(f"{tag}_ht", [128, 2, PN], BF16)
    psg = [P.psum(f"{tag}_psg{b}", [128, PN], F32) for b in range(2)]
    psu = [P.psum(f"{tag}_psu{b}", [128, PN], F32) for b in range(2)]

    ld = [f"{tag}_ld0", f"{tag}_ld1"]
    ldw = [f"{tag}_lw0", f"{tag}_lw1"]
    hst = [f"{tag}_hs0", f"{tag}_hs1"]
    (stq, vv, ldx, mmg, mmu, sgd, hdone) = [
        f"{tag}_{x}" for x in ("stq", "vv", "ldx", "mmg", "mmu", "sg", "hd")]
    NSL = 44
    NIDX = NSL * NPAN
    hst_tot = [16 * (i // 2 + 1) for i in range(NIDX)]


    ndma = [16 for _ in tiles]
    st_cum = []
    acc = 0
    for x in ndma:
        acc += 16 * x
        st_cum.append(acc)

    with nc.Block() as blk:
        def g_body(g):
            def store_tile(i):
                kind, it, fi, prt = tiles[i]
                g.wait_ge(P.sem(vv), 8 * (i + 1))
                if kind == "g":
                    for j in range(8):
                        for half in range(2):
                            s2 = 2 * fi + half if fi < 11 else \
                                22 + 2 * (fi - 11) + half
                            dst = bass.AP(
                                wgu_ap.tensor,
                                wgu_ap.offset + s2 * (128 * DT * 256)
                                + j * DT * 256 + 8 * it * 256,
                                [[256, 8], [8 * DT * 256, 16], [1, 256]])
                            P.inc(stq, g.dma_start(
                                dst, wt[0:prt, j,
                                        half * 256:(half + 1) * 256]), 16)
                else:
                    for j in range(8):
                        for half in range(2):
                            fp2 = 2 * fi + half
                            dst = bass.AP(
                                wdn_ap.tensor,
                                wdn_ap.offset + fp2 * 128 * IT_FF * 256
                                + j * IT_FF * 256 + 8 * it * 256,
                                [[256, prt // 16], [8 * IT_FF * 256, 16],
                                 [1, 256]])
                            P.inc(stq, g.dma_start(
                                dst, wt[0:prt, j,
                                        half * 256:(half + 1) * 256]), 16)
            for i, (kind, it, fi, prt) in enumerate(tiles):
                b = i % 2
                if i >= 2:
                    g.wait_ge(P.sem(stq), st_cum[i - 2])
                pk_ap = pkg_ap if kind == "g" else pkd_ap
                sc_ap = scg_ap if kind == "g" else scd_ap
                OUT = GU_OUT if kind == "g" else D
                P.inc(ld[b], g.dma_start(
                    pks[b][0:prt, :], pk_ap[it * 128:it * 128 + prt,
                                            fi * FS:(fi + 1) * FS]), 16)
                g0 = (it * 128) // 16
                ngr = (prt + 15) // 16
                src = bass.AP(sc_ap.tensor, sc_ap.offset + g0 * OUT + fi * FS,
                              [[OUT, ngr], [0, 16], [1, FS]])
                P.inc(ld[b], g.dma_start(scs[b][0:prt, :], src), 16)
                if i >= 1:
                    store_tile(i - 1)
            store_tile(NTL - 1)
            g.wait_ge(P.sem(stq), st_cum[-1])
        blk.gpsimd(g_body)

        def s_body(sy):
            for k in range(DT):
                P.inc(ldx, sy.dma_start(xr[:, k, :],
                                        xln_ap[k * 128:(k + 1) * 128, :]), 16)
            for fp in range(11):
                sy.wait_ge(P.sem(stq), st_cum[4 * fp + 3])
                for s2 in range(4 * fp, 4 * fp + 4):
                    b = s2 % 2
                    if s2 >= 2:
                        sy.wait_ge(P.sem(mmu), (s2 - 1) * NPAN)
                    half = s2 % 2
                    P.inc(ldw[b], sy.dma_start(wg[b][:, :, :], bass.AP(
                        wgu_ap.tensor,
                        wgu_ap.offset + (s2 // 2) * (128 * DT * 256) + half * 128,
                        [[DT * 256, 128], [256, DT], [1, 128]])), 16)
                    P.inc(ldw[b], sy.dma_start(wu[b][:, :, :], bass.AP(
                        wgu_ap.tensor,
                        wgu_ap.offset + (22 + s2 // 2) * (128 * DT * 256)
                        + half * 128,
                        [[DT * 256, 128], [256, DT], [1, 128]])), 16)
        blk.sync(s_body)

        def t_body(t):
            t.wait_ge(P.sem(ldx), DT * 16)
            for s2 in range(NSL):
                b = s2 % 2
                t.wait_ge(P.sem(ldw[b]), 2 * 16 * (s2 // 2 + 1))
                for p in range(NPAN):
                    idx = s2 * NPAN + p
                    pb = idx % 2
                    if idx >= 2:
                        t.wait_ge(P.sem(hdone), idx - 1)
                    for k in range(DT):
                        last = t.matmul(psg[pb][:], wg[b][:, k, :],
                                        xr[:, k, p * PN:(p + 1) * PN],
                                        start=(k == 0), stop=(k == DT - 1))
                    P.inc(mmg, last)
                    for k in range(DT):
                        last = t.matmul(psu[pb][:], wu[b][:, k, :],
                                        xr[:, k, p * PN:(p + 1) * PN],
                                        start=(k == 0), stop=(k == DT - 1))
                    P.inc(mmu, last)
        blk.tensor(t_body)

        def sc_body(s):
            def hstore(j):
                pb = j % 2
                s.wait_ge(P.sem(hdone), j + 1)
                i_t = j // NPAN
                p = j % NPAN
                dst = bass.AP(h_ap.tensor, h_ap.offset + p * 128 * IT_FF * PN
                              + i_t * PN, [[IT_FF * PN, 128], [1, PN]])
                P.inc(hst[pb], s.dma_start(dst, ht[:, pb, :]), 16)
            for idx in range(NIDX):
                pb = idx % 2
                if idx >= 2:
                    s.wait_ge(P.sem(hdone), idx - 1)
                s.wait_ge(P.sem(mmg), idx + 1)
                P.inc(sgd, s.activation(sg[:, pb, :], psg[pb][:], AF.Silu))
                if idx >= 1:
                    hstore(idx - 1)
            hstore(NIDX - 1)
            s.wait_ge(P.sem(hst[0]), P.n(hst[0]))
            s.wait_ge(P.sem(hst[1]), P.n(hst[1]))
        blk.scalar(sc_body)

        def v_body(v):
            def plane(pl):
                i, j = pl // 8, pl % 8
                kind, it, fi, prt = tiles[i]
                b = i % 2
                if j == 0:
                    v.wait_ge(P.sem(ld[b]), 32 * (i // 2 + 1))
                    if i >= 1:
                        v.wait_ge(P.sem(stq), st_cum[i - 1])
                pt = pks[b][0:prt, :]
                v.tensor_scalar(nib[0:prt, :], pt, 4 * j, 15,
                                ALU.logical_shift_right, ALU.bitwise_and)
                v.tensor_single_scalar(mm_[0:prt, :], nib[0:prt, :], 7, ALU.bitwise_and)
                v.tensor_scalar(t2[0:prt, :], mm_[0:prt, :], 0x3EC0, 0x3F00,
                                ALU.mult, ALU.min)
                v.scalar_tensor_tensor(bits[0:prt, :], mm_[0:prt, :], 64,
                                       t2[0:prt, :], ALU.mult, ALU.add)
                v.tensor_scalar(sgn[0:prt, :], nib[0:prt, :], 8, 12,
                                ALU.bitwise_and, ALU.logical_shift_left)
                v.tensor_tensor(bits[0:prt, :], bits[0:prt, :], sgn[0:prt, :], ALU.add)
                v.tensor_copy(u16[0:prt, :], bits[0:prt, :])
                P.inc(vv, v.tensor_tensor(wt[0:prt, j, :],
                                          u16[0:prt, :].bitcast(BF16),
                                          scs[b][0:prt, :], ALU.mult))
            pl = 0
            NPL = NTL * 8
            for idx in range(NIDX):
                pace = min(NPL, 32 + (idx * 512) // 352 + 2)
                while pl < pace:
                    plane(pl)
                    pl += 1
                pb = idx % 2
                if idx >= 2:
                    v.wait_ge(P.sem(hst[pb]), hst_tot[idx - 2])
                v.wait_ge(P.sem(sgd), idx + 1)
                v.wait_ge(P.sem(mmu), idx + 1)
                P.inc(hdone, v.tensor_tensor(ht[:, pb, :], sg[:, pb, :],
                                             psu[pb][:], ALU.mult))
            while pl < NPL:
                plane(pl)
                pl += 1
        blk.vector(v_body)


def emit_l0ffn_down(P, tag, h_ap, wdn_ap, x_ap, xout_ap, stage=3,
                    wdn_tiled=True):
    nc = P.nc
    sb = P.sbuf
    C = P.consts
    hp = [sb(f"{tag}_hp{b}", [128, IT_FF, PN], BF16) for b in range(2)]
    wd = [sb(f"{tag}_wd{b}", [128, IT_FF, 256], BF16) for b in range(2)]
    ysb = sb(f"{tag}_ysb", [128, DT, PN], F32)
    ysq = sb(f"{tag}_ysq", [128, DT, PN], BF16)
    rec = sb(f"{tag}_rec", [128, PN], F32)
    tmp = sb(f"{tag}_tmp", [128, PN], F32)
    xt = sb(f"{tag}_xt", [128, 2, PN], F32)
    xo = sb(f"{tag}_xo", [128, 2, PN], F32)
    psd = [P.psum(f"{tag}_psd{b}", [128, PN], F32) for b in range(4)]
    pss = [P.psum(f"{tag}_pss0", [128, PN], F32)]

    ldh = [f"{tag}_lh0", f"{tag}_lh1"]
    ldw = [f"{tag}_lw0", f"{tag}_lw1"]
    ldx = [f"{tag}_lx0", f"{tag}_lx1"]
    od = [f"{tag}_od0", f"{tag}_od1"]
    mmd, sqd, smd, smv, recd, appd, wfr, pfree, hfree = (
        f"{tag}_mmd", f"{tag}_sq", f"{tag}_sm", f"{tag}_smv", f"{tag}_rc",
        f"{tag}_ap", f"{tag}_wfr", f"{tag}_pf", f"{tag}_hf")

    NFP = DT // 2
    NW = NPAN * NFP
    NT = NPAN * DT
    ldh_tot = [16 * (p // 2 + 1) for p in range(NPAN)]
    nwd = 1 if wdn_tiled else IT_FF
    ldw_tot = [nwd * 16 * (ns // 2 + 1) for ns in range(NW)]
    ldx_tot = [16 * (nt // 2 + 1) for nt in range(NT)]
    od_tot = [16 * (nt // 2 + 1) for nt in range(NT)]

    with nc.Block() as blk:
        def g_body(g):
            def xtxo(p):
                for f in range(DT):
                    nt = p * DT + f
                    xb = nt % 2
                    if nt >= 2:
                        g.wait_ge(P.sem(appd), nt - 1)
                    cs = slice(p * PN, (p + 1) * PN)
                    if stage >= 3:
                        P.inc(ldx[xb], g.dma_start(xt[:, xb, :],
                                                   x_ap[f * 128:(f + 1) * 128, cs]), 16)
                    g.wait_ge(P.sem(appd), nt + 1)
                    P.inc(od[xb], g.dma_start(xout_ap[f * 128:(f + 1) * 128, cs],
                                              xo[:, xb, :]), 16)
            for p in range(NPAN):
                b = p % 2
                if p >= 2:
                    g.wait_ge(P.sem(mmd), (p - 1) * NFP)
                src = bass.AP(h_ap.tensor, h_ap.offset + p * 128 * IT_FF * PN,
                              [[IT_FF * PN, 128], [1, IT_FF * PN]])
                P.inc(ldh[b], g.dma_start(hp[b][:, :, :], src), 16)
                assert P.n(ldh[b]) == ldh_tot[p]
                if p >= 1:
                    xtxo(p - 1)
            xtxo(NPAN - 1)
            g.wait_ge(P.sem(od[0]), P.n(od[0]))
            g.wait_ge(P.sem(od[1]), P.n(od[1]))
        blk.gpsimd(g_body)

        def w_body(sy2):
            ns = 0
            for p in range(NPAN):
                for fp_ in range(NFP):
                    b = ns % 2
                    if ns >= 2:
                        sy2.wait_ge(P.sem(mmd), ns - 1)
                    if wdn_tiled:
                        src = bass.AP(wdn_ap.tensor,
                                      wdn_ap.offset + fp_ * 128 * IT_FF * 256,
                                      [[IT_FF * 256, 128], [1, IT_FF * 256]])
                        P.inc(ldw[b], sy2.dma_start(wd[b][:, :, :], src), 16)
                    else:
                        for i in range(IT_FF):
                            P.inc(ldw[b], sy2.dma_start(
                                wd[b][:, i, :],
                                wdn_ap[i * 128:(i + 1) * 128,
                                       fp_ * 256:(fp_ + 1) * 256]), 16)
                    ns += 1
        blk.sync(w_body)

        def t_body(t):
            ns = 0
            for p in range(NPAN):
                b = p % 2
                t.wait_ge(P.sem(ldh[b]), ldh_tot[p])
                for fp_ in range(NFP):
                    pb = (fp_ % 2) * 2
                    idx = p * NFP + fp_
                    wb = ns % 2
                    t.wait_ge(P.sem(ldw[wb]), ldw_tot[ns])
                    if idx >= 2:
                        t.wait_ge(P.sem(sqd), 2 * (idx - 1))
                    for i in range(IT_FF):
                        t.matmul(psd[pb][:], wd[wb][:, i, 0:128], hp[b][:, i, :],
                                 start=(i == 0), stop=(i == IT_FF - 1))
                        last = t.matmul(psd[pb + 1][:], wd[wb][:, i, 128:256], hp[b][:, i, :],
                                        start=(i == 0), stop=(i == IT_FF - 1))
                    P.inc(mmd, last)
                    ns += 1
                    if stage >= 2 and fp_ >= 1:
                        if fp_ == 1 and p >= 1:
                            t.wait_ge(P.sem(smv), p)
                        for q in range(2):
                            f = fp_ * 2 - 2 + q
                            t.wait_ge(P.sem(sqd), p * DT + f + 1)
                            last = t.matmul(pss[0][:], C["ones_bf"][:], ysq[:, f, :],
                                            start=(f == 0), stop=False)
                        P.inc(smd, last)
                if stage >= 2:
                    for q in range(2):
                        f = DT - 2 + q
                        t.wait_ge(P.sem(sqd), p * DT + f + 1)
                        last = t.matmul(pss[0][:], C["ones_bf"][:], ysq[:, f, :],
                                        start=False, stop=(f == DT - 1))
                    P.inc(smd, last)
                P.inc(hfree, t.nop())
        blk.tensor(t_body)

        def sc_body(s):
            for p in range(NPAN):
                for fp_ in range(NFP):
                    idx = p * NFP + fp_
                    pb = (fp_ % 2) * 2
                    s.wait_ge(P.sem(mmd), idx + 1)
                    for q in range(2):
                        f = fp_ * 2 + q
                        s.activation(ysb[:, f, :], psd[pb + q][:], AF.Identity)
                        P.inc(sqd, s.activation(ysq[:, f, :], ysb[:, f, :], AF.Square))
                if stage >= 2:
                    s.wait_ge(P.sem(smv), p + 1)
                    s.activation(rec[:], rec[:], AF.Ln, bias=EPS_RMS)
                    P.inc(recd, s.activation(rec[:], rec[:], AF.Exp, scale=-0.5))
        blk.scalar(sc_body)

        def v_body(v):
            nt = 0
            for p in range(NPAN):
                if stage >= 2:
                    v.wait_ge(P.sem(smd), (p + 1) * NFP)
                    P.inc(smv, v.tensor_single_scalar(rec[:], pss[0][:], 1.0 / D, ALU.mult))
                    v.wait_ge(P.sem(recd), p + 1)
                else:
                    v.wait_ge(P.sem(sqd), (p + 1) * DT)
                for f in range(DT):
                    xb = nt % 2
                    if nt >= 2:
                        v.wait_ge(P.sem(od[xb]), od_tot[nt - 2])
                    if stage >= 3:
                        v.wait_ge(P.sem(ldx[xb]), ldx_tot[nt])
                        v.tensor_tensor(tmp[:], ysb[:, f, :], rec[:], ALU.mult)
                        P.inc(appd, v.scalar_tensor_tensor(
                            xo[:, xb, :], tmp[:], 0.5, xt[:, xb, :], ALU.mult, ALU.add))
                    else:
                        P.inc(appd, v.tensor_copy(xo[:, xb, :], ysb[:, f, :]))
                    nt += 1
        blk.vector(v_body)


CONST_NAMES = ["c_ones_bf", "c_ident_bf", "c_ident_f32", "c_ones1_bf", "c_iota128",
               "c_arr256", "c_rarr256", "c_idxhi", "c_idxlo", "c_mskhi", "c_msklo",
               "c_iota8", "c_cmadd", "c_ones_f32", "c_fconst"]
CONST_SHAPES = {
    "c_ones_bf": ([128, 128], BF16), "c_ident_bf": ([128, 128], BF16),
    "c_ident_f32": ([128, 128], F32), "c_ones1_bf": ([1, 128], BF16),
    "c_iota128": ([128, 128], F32), "c_arr256": ([1, 256], F32),
    "c_rarr256": ([1, 256], F32), "c_idxhi": ([128, 256], F32),
    "c_idxlo": ([128, 256], F32), "c_mskhi": ([128, 256], F32),
    "c_msklo": ([128, 256], F32), "c_iota8": ([128, 8], F32),
    "c_cmadd": ([128, 4, 512], BF16), "c_ones_f32": ([128, 1], F32),
    "c_fconst": ([128, 16], F32),
}


def decl_consts(P):
    dram = {}
    for n in CONST_NAMES:
        shape, dt_ = CONST_SHAPES[n]
        dram[n] = P.nc.dram_tensor(n, shape, dt_, kind="ExternalInput").ap()
    return dram


def build_dequant():
    P = Prog("dequant")
    nc = P.nc
    dram = decl_consts(P)
    pk_gu = nc.dram_tensor("pk_gu", [256, GU_OUT], I32, kind="ExternalInput").ap()
    sc_gu = nc.dram_tensor("sc_gu", [16, GU_OUT], F16, kind="ExternalInput").ap()
    pk_dn = nc.dram_tensor("pk_dn", [704, D], I32, kind="ExternalInput").ap()
    sc_dn = nc.dram_tensor("sc_dn", [44, D], F16, kind="ExternalInput").ap()
    wgu = nc.dram_tensor("wgu", [D, GU_OUT], BF16, kind="ExternalOutput").ap()
    wdn = nc.dram_tensor("wdn", [I_FF, D], BF16, kind="ExternalOutput").ap()
    emit_consts(P, dram)
    P.open_scope()
    emit_dequant(P, "dqg", pk_gu, sc_gu, wgu, D, GU_OUT, 512)
    emit_barrier(P, "dq1")
    P.close_scope()
    P.open_scope()
    emit_dequant(P, "dqd", pk_dn, sc_dn, wdn, I_FF, D, 512)
    emit_barrier(P, "dq2")
    P.close_scope()
    P.close()
    ins = CONST_NAMES + ["pk_gu", "sc_gu", "pk_dn", "sc_dn"]
    return P.nc, ins, ["wgu", "wdn"]


def build_l0ffn(xsrc_name, xdst_name):
    P = Prog("l0ffn")
    nc = P.nc
    dram = decl_consts(P)
    x_in = nc.dram_tensor(xsrc_name, [D, T], F32, kind="ExternalInput").ap()
    wgu = nc.dram_tensor("wgu", [D, GU_OUT], BF16, kind="ExternalInput").ap()
    wdn = nc.dram_tensor("wdn", [I_FF, D], BF16, kind="ExternalInput").ap()
    x_out = nc.dram_tensor(xdst_name, [D, T], F32, kind="ExternalOutput").ap()
    xln = nc.dram_tensor("xln", [D, T], BF16, kind="Internal").ap()
    hbuf = nc.dram_tensor("hbuf", [I_FF, T], BF16, kind="Internal").ap()
    emit_consts(P, dram)
    P.open_scope()
    emit_ln(P, "ln", x_in, xln)
    emit_barrier(P, "b1")
    P.close_scope()
    P.open_scope()
    emit_l0ffn_gateup(P, "fa", xln, wgu, hbuf, wgu_tiled=False)
    emit_barrier(P, "b2")
    P.close_scope()
    P.open_scope()
    emit_l0ffn_down(P, "fb", hbuf, wdn, x_in, x_out, wdn_tiled=False)
    emit_barrier(P, "b3")
    P.close_scope()
    P.close()
    ins = CONST_NAMES + [xsrc_name, "wgu", "wdn"]
    return P.nc, ins, [xdst_name]


def build_l0attn(xsrc_name, xdst_name):
    P = Prog("l0attn")
    nc = P.nc
    dram = decl_consts(P)
    x_in = nc.dram_tensor(xsrc_name, [D, T], F32, kind="ExternalInput").ap()
    wqkv = nc.dram_tensor("wqkvT", [D, 3 * D], F32, kind="ExternalInput").ap()
    slope = nc.dram_tensor("slope_row", [1, 16], F32, kind="ExternalInput").ap()
    wgate = nc.dram_tensor("wgateT", [D, D], F32, kind="ExternalInput").ap()
    wout = nc.dram_tensor("woutT", [D, D], F32, kind="ExternalInput").ap()
    normw = nc.dram_tensor("normw2d", [128, 16], F32, kind="ExternalInput").ap()
    x_out = nc.dram_tensor(xdst_name, [D, T], F32, kind="ExternalOutput").ap()
    xln = nc.dram_tensor("xln", [D, T], BF16, kind="Internal").ap()
    hbuf = nc.dram_tensor("hbuf", [2 * D, T], BF16, kind="Internal").ap()
    var = nc.dram_tensor("varbuf", [1, T], F32, kind="Internal").ap()
    qkvb = nc.dram_tensor("qkvbuf", [3 * D, T], BF16, kind="Internal").ap()
    outh = hbuf[0:D, :]
    z = hbuf[D:2 * D, :]
    emit_consts(P, dram)
    P.open_scope()
    emit_ln(P, "ln", x_in, xln)
    emit_barrier(P, "b1")
    P.close_scope()
    P.open_scope()
    emit_lightning_proj(P, "lp", xln, wqkv, qkvb)
    emit_barrier(P, "b2a")
    P.close_scope()
    P.open_scope()
    emit_lightning_scan(P, "ls", qkvb, slope, outh)
    emit_barrier(P, "b2")
    P.close_scope()
    P.open_scope()
    emit_lightning_out(P, "lo", xln, outh, var, wgate, normw, z)
    emit_barrier(P, "b3")
    P.close_scope()
    P.open_scope()
    emit_matmul_residual(P, "wo", z, wout, x_in, x_out, D, alpha=1.0)
    emit_barrier(P, "b4")
    P.close_scope()
    P.close()
    ins = CONST_NAMES + [xsrc_name, "wqkvT", "slope_row", "wgateT", "woutT", "normw2d"]
    return P.nc, ins, [xdst_name]


def host_prep(inputs):
    shared = dict(host_consts())
    shared["pk_gu"] = np.ascontiguousarray(np.asarray(inputs["L0_ffn_gate_up_packed"]).T)
    shared["sc_gu"] = np.ascontiguousarray(np.asarray(inputs["L0_ffn_gate_up_scales"]))
    shared["pk_dn"] = np.ascontiguousarray(np.asarray(inputs["L0_ffn_down_packed"]).T)
    shared["sc_dn"] = np.ascontiguousarray(np.asarray(inputs["L0_ffn_down_scales"]))
    shared["wqkvT"] = np.ascontiguousarray(np.asarray(inputs["L0_attn_w_qkv"]).T)
    shared["slope_row"] = np.ascontiguousarray(
        np.asarray(inputs["L0_attn_slope_rate"]).reshape(1, 16))
    def tile_w(wT):
        KI, KO = wT.shape[0] // 128, wT.shape[1] // 128
        return np.ascontiguousarray(
            wT.reshape(KI, 128, KO, 128).transpose(2, 1, 0, 3)).reshape(wT.shape)
    shared["wgateT"] = np.ascontiguousarray(np.asarray(inputs["L0_attn_w_gate"]).T)
    shared["woutT"] = tile_w(np.asarray(inputs["L0_attn_w_out"]).T)
    shared["normw2d"] = np.ascontiguousarray(
        np.asarray(inputs["L0_attn_norm_w"]).reshape(16, 128).T)
    order, midx2d, tile_blocks = l1ffn_host_plan(inputs["L1_ffn_mask_idx"])
    shared["gate_wT"] = np.ascontiguousarray(np.asarray(inputs["L1_ffn_gate_w"]).T)
    shared["WT"] = tile_w(np.asarray(inputs["L1_ffn_W"]).T)
    shared["eaw"] = np.ascontiguousarray(np.asarray(inputs["L1_ffn_expert_atom_weights"]))
    shared["atoms_s"] = np.ascontiguousarray(np.asarray(inputs["L1_ffn_atoms"])[:, order])
    shared["imp_s"] = np.ascontiguousarray(np.asarray(inputs["L1_ffn_importance"])[:, order])
    shared["midx2d"] = midx2d
    shared["_tile_blocks"] = tile_blocks
    shared["Wq"] = tile_w(np.asarray(inputs["L1_attn_Wq"]))
    shared["Wk"] = tile_w(np.asarray(inputs["L1_attn_Wk"]))
    shared["Wv"] = tile_w(np.asarray(inputs["L1_attn_Wv"]))
    shared["Wo"] = tile_w(np.asarray(inputs["L1_attn_Wo"]))
    shared["bo2d"] = np.ascontiguousarray(
        np.asarray(inputs["L1_attn_bo"]).reshape(16, 128).T)
    x = np.asarray(inputs["L0_attn_hidden_states"])
    percore = [{"x_in": np.ascontiguousarray(x[c].T)} for c in range(8)]
    return shared, percore


def run_launch(builder_out, state):
    nc, innames, outnames = builder_out
    in_maps = [{k: state[c][k] for k in innames} for c in range(len(state))]
    res = run_bass_kernel_spmd(nc, in_maps, core_ids=list(range(len(state))))
    for c in range(len(state)):
        for k in outnames:
            state[c][k] = np.asarray(res.results[c][k])
    return state


def kernel(**inputs):
    import os as _os
    if _os.environ.get("MKK_UNFUSED"):
        return kernel_unfused(**inputs)
    return kernel_fused(**inputs)


def kernel_unfused(**inputs):
    shared, percore = host_prep(inputs)
    tile_blocks = shared.pop("_tile_blocks")
    state = [dict(shared, **percore[c]) for c in range(8)]
    for c in range(8):
        state[c]["xa"] = state[c]["x_in"]
    pg_dq = build_dequant()
    pg_ffn0 = build_l0ffn("xa", "xb")
    pg_attn0 = build_l0attn("xa", "xb")
    pg_ffn1 = build_l1ffn("xa", "xb", tile_blocks)
    pg_attn1 = build_l1attn("xa", "xb")
    run_launch(pg_dq, state)
    for prog in (pg_ffn0, pg_attn0, pg_ffn0, pg_ffn1, pg_attn1, pg_ffn1):
        run_launch(prog, state)
        for c in range(8):
            state[c]["xa"] = state[c]["xb"]
    y = np.stack([np.asarray(state[c]["xb"], dtype=np.float32).T for c in range(8)])
    return y


def emit_lightning(P, tag, xln_ap, wqkv_ap, slope_ap, outh_ap, var_ap):
    nc = P.nc
    sb = P.sbuf
    C = P.consts
    xr = sb(f"{tag}_xr", [128, DT, T], BF16)
    wqk = sb(f"{tag}_wq", [128, DT, 384], BF16)
    qb_ = sb(f"{tag}_qb", [128, T], BF16)
    kb_ = sb(f"{tag}_kb", [128, T], BF16)
    vb_ = sb(f"{tag}_vb", [128, T], BF16)
    slope = sb(f"{tag}_slope", [128, 16], F32)
    negs = sb(f"{tag}_negs", [128, 16], F32)
    qdrow = sb(f"{tag}_qdr", [1, 256], BF16)
    kdrow = sb(f"{tag}_kdr", [1, 256], BF16)
    bdcol = sb(f"{tag}_bd", [128, 1], F32)
    qdb = sb(f"{tag}_qdb", [128, 256], BF16)
    kdb = sb(f"{tag}_kdb", [128, 256], BF16)
    idxc = sb(f"{tag}_idxc", [128, 2, 256], F32)
    ddh = sb(f"{tag}_ddh", [128, 256], BF16)
    ddl = sb(f"{tag}_ddl", [128, 256], BF16)
    qqd = sb(f"{tag}_qqd", [128, 256], BF16)
    kkd = sb(f"{tag}_kkd", [128, 256], BF16)
    qkdd = sb(f"{tag}_qkdd", [128, 2, 256], BF16)
    trv = sb(f"{tag}_trv", [128, 2, 128], BF16)
    trk = sb(f"{tag}_trk", [128, 2, 128], BF16)
    kvf = sb(f"{tag}_kvf", [128, 128], F32)
    kvt = sb(f"{tag}_kvt", [128, 128], F32)
    kvb = sb(f"{tag}_kvb", [128, 128], BF16)
    ostg = sb(f"{tag}_ostg", [128, 2, 256], BF16)
    sqst = sb(f"{tag}_sq", [128, 256], BF16)
    vacc = sb(f"{tag}_vacc", [1, T], F32)
    ppj = [P.psum(f"{tag}_ppj{i}", [128, PN], F32) for i in range(3)]
    pqk = P.psum(f"{tag}_pqk", [128, 512], F32)
    pA = P.psum(f"{tag}_pA", [128, 512], F32)
    po = pA[:, 0:256]
    pkvd = pA[:, 256:384]
    ptr = P.psum(f"{tag}_ptr", [128, 4, 128], BF16)
    pdec = P.psum(f"{tag}_pdec", [128, 512], F32)
    pvar = pdec[0:1, 0:256]

    (ldx, lds, neg, decA, decP, decE, ddA, ddV, pj, pev, qkS, qqdS, kkdS, qkddS,
     trS, trE, oS, oev, sqS, kvdS, kvfS, kvbS, varM, varS) = [
        f"{tag}_{x}" for x in
        ("ldx", "lds", "neg", "dA", "dP", "dE", "ddA", "ddV", "pj", "pev", "qk",
         "qqd", "kkd", "qkdd", "tr", "trE", "o", "oev", "sqS", "kvd", "kvf",
         "kvb", "vM", "vS")]
    ldw = [f"{tag}_lw0", f"{tag}_lw1"]
    ost = [f"{tag}_os0", f"{tag}_os1"]
    ldw_tot = [DT * 16 * (h + 1) for h in range(NH)]
    ost_tot = [16 * (m // 2 + 1) for m in range(NH * NBLK)]

    with nc.Block() as blk:
        def g_body(g):
            for k in range(DT):
                P.inc(ldx, g.dma_start(xr[:, k, :], xln_ap[k * 128:(k + 1) * 128, :]), 16)
            src = bass.AP(slope_ap.tensor, slope_ap.offset, [[0, 128], [1, 16]])
            P.inc(lds, g.dma_start(slope[:], src), 16)
            for h in range(NH):
                if h >= 1:
                    g.wait_ge(P.sem(pj), h * 24)
                for k in range(DT):
                    P.inc(ldw[0], g.dma_start(
                        wqk[:, k, :],
                        wqkv_ap[k * 128:(k + 1) * 128, h * 384:(h + 1) * 384]), 16)
            g.wait_ge(P.sem(varS), NH * NBLK)
            P.inc(f"{tag}_vout", g.dma_start(var_ap[:], vacc[:]), 16)
            g.wait_ge(P.sem(f"{tag}_vout"), 16)
        blk.gpsimd(g_body)

        def v_body(v):
            v.wait_ge(P.sem(lds), 16)
            P.inc(neg, v.tensor_single_scalar(negs[:], slope[:], -1.0, ALU.mult))
            v.tensor_scalar_max(idxc[:, 0, :], C["idxhi"][:], 0.0)
            P.inc(f"{tag}_idx", v.tensor_scalar_max(idxc[:, 1, :], C["idxlo"][:], 0.0))
            v.memset(vacc[:], 0.0)
            for h in range(NH):
                hb = h % 2
                v.wait_ge(P.sem(ddA), h * 2 + 2)
                v.tensor_tensor(ddh[:], ddh[:], C["mskhi"][:], ALU.mult)
                P.inc(ddV, v.tensor_tensor(ddl[:], ddl[:], C["msklo"][:], ALU.mult))
                v.memset(kvf[:], 0.0)
                P.inc(kvbS, v.memset(kvb[:], 0.0))
                for blki in range(NBLK):
                    m = h * NBLK + blki
                    cs = slice(blki * 256, (blki + 1) * 256)
                    v.wait_ge(P.sem(pev), h * 24 + 24)
                    v.wait_ge(P.sem(decE), h + 1)
                    P.inc(qqdS, v.tensor_tensor(qqd[:], qb_[:, cs], qdb[:], ALU.mult))
                    P.inc(kkdS, v.tensor_tensor(kkd[:], kb_[:, cs], kdb[:], ALU.mult))
                    v.wait_ge(P.sem(qkS), m + 1)
                    v.tensor_tensor(qkdd[:, 0, :], pqk[:, 0:256], ddh[:], ALU.mult)
                    P.inc(qkddS, v.tensor_tensor(qkdd[:, 1, :], pqk[:, 256:512], ddl[:], ALU.mult))
                    v.wait_ge(P.sem(f"{tag}_kvsc"), m + 1)
                    P.inc(kvfS, v.tensor_tensor(kvf[:], kvt[:], pkvd, ALU.add))
                    P.inc(kvbS, v.tensor_copy(kvb[:], kvf[:]))
                    v.wait_ge(P.sem(varM), m + 1)
                    P.inc(varS, v.tensor_tensor(vacc[0:1, cs], vacc[0:1, cs], pvar, ALU.add))
        blk.vector(v_body)

        def sc_body(s):
            s.wait_ge(P.sem(lds), 16)
            for h in range(NH):
                hb = h % 2
                s.wait_ge(P.sem(neg), 1)
                if h >= 1:
                    s.wait_ge(P.sem(qkddS), h * NBLK)
                    s.wait_ge(P.sem(qqdS), h * NBLK)
                    s.wait_ge(P.sem(kkdS), h * NBLK)
                nsc = negs[:, h:h + 1]
                s.activation(qdrow[:], C["arr256"][:], AF.Exp, scale=negs[0:1, h:h + 1])
                s.activation(kdrow[:], C["rarr256"][:], AF.Exp, scale=negs[0:1, h:h + 1])
                P.inc(decA, s.activation(bdcol[:], slope[:, h:h + 1], AF.Exp, scale=-256.0))
                if h == 0:
                    s.wait_ge(P.sem(f"{tag}_idx"), 1)
                s.activation(ddh[:], idxc[:, 0, :], AF.Exp, scale=nsc)
                P.inc(ddA, s.activation(ddl[:], idxc[:, 1, :], AF.Exp, scale=nsc), 2)
                s.wait_ge(P.sem(decP), h + 1)
                s.activation(qdb[:], pdec[:, 0:256], AF.Copy)
                P.inc(decE, s.activation(kdb[:], pdec[:, 256:512], AF.Copy))
                for pn in range(NPAN):
                    for o in range(3):
                        i = h * 24 + pn * 3 + o
                        s.wait_ge(P.sem(pj), i + 1)
                        dst = (qb_, kb_, vb_)[o][:, pn * PN:(pn + 1) * PN]
                        if h >= 1 and pn == 0 and o == 0:
                            s.wait_ge(P.sem(varS), h * NBLK)
                        P.inc(pev, s.activation(dst, ppj[o][:], AF.Silu))
                for blki in range(NBLK):
                    m = h * NBLK + blki
                    s.wait_ge(P.sem(trS), m + 1)
                    s.activation(trk[:, 0, :], ptr[:, 0, :], AF.Copy)
                    s.activation(trk[:, 1, :], ptr[:, 1, :], AF.Copy)
                    s.activation(trv[:, 0, :], ptr[:, 2, :], AF.Copy)
                    P.inc(trE, s.activation(trv[:, 1, :], ptr[:, 3, :], AF.Copy))
                    s.wait_ge(P.sem(oS), m + 1)
                    if m >= 2:
                        s.wait_ge(P.sem(ost[m % 2]), ost_tot[m - 2])
                    P.inc(oev, s.activation(ostg[:, m % 2, :], po, AF.Copy))
                    P.inc(sqS, s.activation(sqst[:], po, AF.Square))
                    s.wait_ge(P.sem(kvdS), m + 1)
                    s.wait_ge(P.sem(kvfS), m)
                    P.inc(f"{tag}_kvsc", s.activation(kvt[:], kvf[:], AF.Copy, scale=bdcol[:]))
        blk.scalar(sc_body)

        def t_body(t):
            t.wait_ge(P.sem(ldx), DT * 16)
            for h in range(NH):
                hb = h % 2
                t.wait_ge(P.sem(decA), h + 1)
                if h >= 1:
                    t.wait_ge(P.sem(varS), h * NBLK)
                t.matmul(pdec[:, 0:256], C["ones1_bf"][:], qdrow[:], start=True, stop=True)
                P.inc(decP, t.matmul(pdec[:, 256:512], C["ones1_bf"][:], kdrow[:],
                                     start=True, stop=True))
                t.wait_ge(P.sem(ldw[0]), ldw_tot[h])
                for pn in range(NPAN):
                    for o in range(3):
                        i = h * 24 + pn * 3 + o
                        if i >= 3:
                            t.wait_ge(P.sem(pev), i - 2)
                        for k in range(DT):
                            last = t.matmul(ppj[o][:],
                                            wqk[:, k, o * 128:(o + 1) * 128],
                                            xr[:, k, pn * PN:(pn + 1) * PN],
                                            start=(k == 0), stop=(k == DT - 1))
                        P.inc(pj, last)
                for blki in range(NBLK):
                    m = h * NBLK + blki
                    cs = slice(blki * 256, (blki + 1) * 256)
                    if m >= 1:
                        t.wait_ge(P.sem(qkddS), m)
                    v_q = qb_[:, cs]
                    t.matmul(pqk[:, 0:256], kb_[:, blki * 256:blki * 256 + 128],
                             v_q, start=True, stop=True)
                    P.inc(qkS, t.matmul(pqk[:, 256:512],
                                        kb_[:, blki * 256 + 128:blki * 256 + 256],
                                        v_q, start=True, stop=True))
                    t.wait_ge(P.sem(kkdS), m + 1)
                    if m >= 1:
                        t.wait_ge(P.sem(trE), m)
                    t.transpose(ptr[:, 0, :], kkd[:, 0:128], C["ident_bf"][:])
                    t.transpose(ptr[:, 1, :], kkd[:, 128:256], C["ident_bf"][:])
                    t.transpose(ptr[:, 2, :], vb_[:, blki * 256:blki * 256 + 128],
                                C["ident_bf"][:])
                    P.inc(trS, t.transpose(ptr[:, 3, :],
                                           vb_[:, blki * 256 + 128:blki * 256 + 256],
                                           C["ident_bf"][:]))
                    t.wait_ge(P.sem(qqdS), m + 1)
                    t.wait_ge(P.sem(kvbS), h * (NBLK + 1) + 1 + blki)
                    if m >= 1:
                        t.wait_ge(P.sem(sqS), m)
                    if m >= 1:
                        t.wait_ge(P.sem(kvfS), m)
                    t.matmul(po, kvb[:], qqd[:], start=True, stop=False)
                    t.wait_ge(P.sem(qkddS), m + 1)
                    t.wait_ge(P.sem(trE), m + 1)
                    t.matmul(po, trv[:, 0, :], qkdd[:, 0, :], start=False, stop=False)
                    P.inc(oS, t.matmul(po, trv[:, 1, :], qkdd[:, 1, :],
                                       start=False, stop=True))
                    t.wait_ge(P.sem(sqS), m + 1)
                    t.matmul(pkvd, trk[:, 0, :], trv[:, 0, :], start=True, stop=False)
                    P.inc(kvdS, t.matmul(pkvd, trk[:, 1, :], trv[:, 1, :],
                                         start=False, stop=True))
                    t.wait_ge(P.sem(kvfS), m + 1)
                    if m >= 1:
                        t.wait_ge(P.sem(varS), m)
                    P.inc(varM, t.matmul(pvar, C["ones_bf"][:, 0:1], sqst[:],
                                         start=True, stop=True))
        blk.tensor(t_body)

        def s_out(sy):
            for h in range(NH):
                for blki in range(NBLK):
                    m = h * NBLK + blki
                    sy.wait_ge(P.sem(oev), m + 1)
                    P.inc(ost[m % 2], sy.dma_start(
                        outh_ap[h * 128:(h + 1) * 128, blki * 256:(blki + 1) * 256],
                        ostg[:, m % 2, :]), 16)
                    assert P.n(ost[m % 2]) == ost_tot[m]
            sy.wait_ge(P.sem(ost[0]), P.n(ost[0]))
            sy.wait_ge(P.sem(ost[1]), P.n(ost[1]))
        blk.sync(s_out)


def emit_lightning_proj(P, tag, xln_ap, wqkv_ap, qkv_ap):
    nc = P.nc
    sb = P.sbuf
    xr = sb(f"{tag}_xr", [128, DT, T], BF16)
    wqk = [sb(f"{tag}_wq{b}", [128, DT, 384], BF16) for b in range(2)]
    pstg = sb(f"{tag}_pstg", [128, 2, PN], BF16)
    ppj = [P.psum(f"{tag}_ppj{i}", [128, PN], F32) for i in range(3)]

    ldx, pj, pev = f"{tag}_ldx", f"{tag}_pj", f"{tag}_pev"
    ldw = [f"{tag}_lw0", f"{tag}_lw1"]
    pst = [f"{tag}_ps0", f"{tag}_ps1"]
    ldw_tot = [DT * 16 * (h // 2 + 1) for h in range(NH)]

    with nc.Block() as blk:
        def g_body(g):
            for k in range(DT):
                P.inc(ldx, g.dma_start(xr[:, k, :],
                                       xln_ap[k * 128:(k + 1) * 128, :]), 16)
            for h in range(NH):
                b = h % 2
                if h >= 2:
                    g.wait_ge(P.sem(pj), (h - 1) * 24)
                for k in range(DT):
                    P.inc(ldw[b], g.dma_start(
                        wqk[b][:, k, :],
                        wqkv_ap[k * 128:(k + 1) * 128,
                                h * 384:(h + 1) * 384]), 16)
        blk.gpsimd(g_body)

        def t_body(t):
            t.wait_ge(P.sem(ldx), DT * 16)
            for h in range(NH):
                b = h % 2
                t.wait_ge(P.sem(ldw[b]), ldw_tot[h])
                for pn in range(NPAN):
                    for o in range(3):
                        i = h * 24 + pn * 3 + o
                        if i >= 3:
                            t.wait_ge(P.sem(pev), i - 2)
                        for k in range(DT):
                            last = t.matmul(ppj[o][:],
                                            wqk[b][:, k, o * 128:(o + 1) * 128],
                                            xr[:, k, pn * PN:(pn + 1) * PN],
                                            start=(k == 0), stop=(k == DT - 1))
                        P.inc(pj, last)
        blk.tensor(t_body)

        def sc_body(s):
            for i in range(NH * 24):
                s.wait_ge(P.sem(pj), i + 1)
                if i >= 2:
                    s.wait_ge(P.sem(pst[i % 2]), 16 * (i // 2))
                P.inc(pev, s.activation(pstg[:, i % 2, :], ppj[i % 3][:],
                                        AF.Silu))
        blk.scalar(sc_body)

        def s_out(sy):
            for h in range(NH):
                for pn in range(NPAN):
                    for o in range(3):
                        i = h * 24 + pn * 3 + o
                        sy.wait_ge(P.sem(pev), i + 1)
                        P.inc(pst[i % 2], sy.dma_start(
                            bass.AP(qkv_ap.tensor,
                                    qkv_ap.offset + (o * D + h * 128) * T
                                    + pn * PN, [[T, 128], [1, PN]]),
                            pstg[:, i % 2, :]), 16)
            sy.wait_ge(P.sem(pst[0]), P.n(pst[0]))
            sy.wait_ge(P.sem(pst[1]), P.n(pst[1]))
        blk.sync(s_out)


def emit_lightning_scan(P, tag, qkv_ap, slope_ap, outh_ap):
    nc = P.nc
    sb = P.sbuf
    C = P.consts
    qbc = [sb(f"{tag}_qb{u}", [128, 2, 8 * 256], BF16) for u in range(2)]
    kbc = [sb(f"{tag}_kb{u}", [128, 2, 8 * 256], BF16) for u in range(2)]
    vbc = [sb(f"{tag}_vb{u}", [128, 2, 8 * 256], BF16) for u in range(2)]
    slope = sb(f"{tag}_slope", [128, 16], F32)
    negs = sb(f"{tag}_negs", [128, 16], F32)
    idxc = sb(f"{tag}_idxc", [128, 2, 256], F32)
    qdrow = [sb(f"{tag}_qdr{u}", [1, 256], BF16) for u in range(2)]
    kdrow = [sb(f"{tag}_kdr{u}", [1, 256], BF16) for u in range(2)]
    bdcol = [sb(f"{tag}_bd{u}", [128, 1], F32) for u in range(2)]
    qdb = [sb(f"{tag}_qdb{u}", [128, 256], BF16) for u in range(2)]
    kdb = [sb(f"{tag}_kdb{u}", [128, 256], BF16) for u in range(2)]
    ddh = [sb(f"{tag}_ddh{u}", [128, 256], BF16) for u in range(2)]
    ddl = [sb(f"{tag}_ddl{u}", [128, 256], BF16) for u in range(2)]
    qqd = [sb(f"{tag}_qqd{u}", [128, 2, 256], BF16) for u in range(2)]
    kkd = [sb(f"{tag}_kkd{u}", [128, 256], BF16) for u in range(2)]
    qkdd = [sb(f"{tag}_qkdd{u}", [128, 2, 2, 256], BF16) for u in range(2)]
    trv = [sb(f"{tag}_trv{u}", [128, 2, 2, 128], BF16) for u in range(2)]
    trk = [sb(f"{tag}_trk{u}", [128, 2, 2, 128], BF16) for u in range(2)]
    kvf = [sb(f"{tag}_kvf{u}", [128, 128], F32) for u in range(2)]
    kvb = [sb(f"{tag}_kvb{u}", [128, 128], BF16) for u in range(2)]
    ostg = sb(f"{tag}_ostg", [128, 2, 256], BF16)
    pqk = [P.psum(f"{tag}_pqk{u}", [128, 512], F32) for u in range(2)]
    pA_f = [P.psum(f"{tag}_pA{u}", [128, 512], F32) for u in range(2)]
    pA = [x[:, 0:256] for x in pA_f]
    ptr_f = [P.psum(f"{tag}_ptr{u}", [128, 8, 128], BF16) for u in range(2)]
    ptr = [x[:, 0:4, :] for x in ptr_f]
    pkv_f = [P.psum(f"{tag}_pkv{u}", [128, 512], F32) for u in range(2)]
    pkv = [x[:, 0:128] for x in pkv_f]

    NG = NH // 2

    def sfx(name, u):
        return f"{tag}_{name}{u}"
    lds, neg, idx, oev = (f"{tag}_lds", f"{tag}_neg", f"{tag}_idx",
                          f"{tag}_oev")
    ost = [f"{tag}_os0", f"{tag}_os1"]
    ost_tot = [16 * (m // 2 + 1) for m in range(NH * NBLK)]

    with nc.Block() as blk:
        def g_body(g):
            src = bass.AP(slope_ap.tensor, slope_ap.offset, [[0, 128], [1, 16]])
            P.inc(lds, g.dma_start(slope[:], src), 16)
            for grp in range(NG):
                for u in range(2):
                    h = 2 * grp + u
                    for hf in range(2):
                        if grp >= 1:
                            cnt = (grp - 1) * NBLK + 8 * (hf + 1)
                            g.wait_ge(P.sem(sfx("qkddS", u)), cnt)
                            g.wait_ge(P.sem(sfx("qkdS", u)), 2 * cnt)
                            g.wait_ge(P.sem(sfx("trS", u)), cnt)
                        for o, buf in ((0, qbc), (1, kbc), (2, vbc)):
                            P.inc(sfx("ldq", u), g.dma_start(
                                buf[u][:, hf, :],
                                bass.AP(qkv_ap.tensor,
                                        qkv_ap.offset + (o * D + h * 128) * T
                                        + hf * 2048,
                                        [[T, 128], [1, 2048]])), 16)
        blk.gpsimd(g_body)

        def v_body(v):
            v.wait_ge(P.sem(lds), 16)
            P.inc(neg, v.tensor_single_scalar(negs[:], slope[:], -1.0, ALU.mult))
            v.tensor_scalar_max(idxc[:, 0, :], C["idxhi"][:], 0.0)
            P.inc(idx, v.tensor_scalar_max(idxc[:, 1, :], C["idxlo"][:], 0.0))
            for grp in range(NG):
                for u in range(2):
                    n0 = grp * NBLK
                    v.wait_ge(P.sem(sfx("ddA", u)), grp * 2 + 2)
                    v.tensor_tensor(ddh[u][:], ddh[u][:], C["mskhi"][:], ALU.mult)
                    v.tensor_tensor(ddl[u][:], ddl[u][:], C["msklo"][:], ALU.mult)
                    if grp >= 1:
                        v.wait_ge(P.sem(sfx("oS", u)), n0)
                    v.memset(kvf[u][:], 0.0)
                    P.inc(sfx("kvbS", u), v.memset(kvb[u][:], 0.0))
                for step in range(NBLK + 1):
                    for u in range(2):
                        if step < NBLK:
                            b = step
                            n = grp * NBLK + b
                            hf = b // 8
                            cs = slice((b % 8) * 256, (b % 8) * 256 + 256)
                            v.wait_ge(P.sem(sfx("ldq", u)),
                                      48 * (2 * grp + hf + 1))
                            if step == 0:
                                v.wait_ge(P.sem(sfx("decE", u)), grp + 1)
                            if n >= 2:
                                v.wait_ge(P.sem(sfx("oS", u)), n - 1)
                            if n >= 1:
                                v.wait_ge(P.sem(sfx("trS", u)), n)
                            v.tensor_tensor(qqd[u][:, b % 2, :],
                                            qbc[u][:, hf, cs], qdb[u][:],
                                            ALU.mult)
                            P.inc(sfx("qkdS", u), v.tensor_tensor(
                                kkd[u][:], kbc[u][:, hf, cs], kdb[u][:],
                                ALU.mult), 2)
                            v.wait_ge(P.sem(sfx("qkS", u)), n + 1)
                            if n >= 2:
                                v.wait_ge(P.sem(sfx("oS", u)), n - 1)
                            v.tensor_tensor(qkdd[u][:, b % 2, 0, :],
                                            pqk[u][:, 0:256],
                                            ddh[u][:], ALU.mult)
                            P.inc(sfx("qkddS", u), v.tensor_tensor(
                                qkdd[u][:, b % 2, 1, :], pqk[u][:, 256:512],
                                ddl[u][:], ALU.mult))
                        if step >= 1:
                            n = grp * NBLK + step - 1
                            v.wait_ge(P.sem(sfx("kvdS", u)), n + 1)
                            v.wait_ge(P.sem(sfx("decA", u)), grp + 1)
                            P.inc(sfx("kvfS", u), v.scalar_tensor_tensor(
                                kvf[u][:], kvf[u][:], bdcol[u][:],
                                pkv[u][:], ALU.mult, ALU.add))
                            P.inc(sfx("kvbS", u), v.tensor_copy(kvb[u][:],
                                                                kvf[u][:]))
        blk.vector(v_body)

        def sc_body(s):
            s.wait_ge(P.sem(lds), 16)
            for grp in range(NG):
                for u in range(2):
                    h = 2 * grp + u
                    s.wait_ge(P.sem(neg), 1)
                    if grp >= 1:
                        s.wait_ge(P.sem(sfx("qkddS", u)), grp * NBLK)
                        s.wait_ge(P.sem(sfx("qkdS", u)), 2 * grp * NBLK)
                    nsc = negs[0:1, h:h + 1]
                    s.activation(qdrow[u][:], C["arr256"][:], AF.Exp, scale=nsc)
                    s.activation(kdrow[u][:], C["rarr256"][:], AF.Exp, scale=nsc)
                    P.inc(sfx("decA", u), s.activation(
                        bdcol[u][:], slope[:, h:h + 1], AF.Exp, scale=-256.0))
                    if grp == 0 and u == 0:
                        s.wait_ge(P.sem(idx), 1)
                    s.activation(ddh[u][:], idxc[:, 0, :], AF.Exp,
                                 scale=negs[:, h:h + 1])
                    P.inc(sfx("ddA", u), s.activation(
                        ddl[u][:], idxc[:, 1, :], AF.Exp,
                        scale=negs[:, h:h + 1]), 2)
                    s.wait_ge(P.sem(sfx("decP", u)), grp + 1)
                    s.activation(qdb[u][:], pqk[u][:, 0:256], AF.Copy)
                    P.inc(sfx("decE", u), s.activation(kdb[u][:],
                                                       pqk[u][:, 256:512],
                                                       AF.Copy))
                for step in range(NBLK + 1):
                    for u in range(2):
                        if step < NBLK:
                            n = grp * NBLK + step
                            s.wait_ge(P.sem(sfx("trS", u)), n + 1)
                            if n >= 2:
                                s.wait_ge(P.sem(sfx("kvdS", u)), n - 1)
                                s.wait_ge(P.sem(sfx("oS", u)), n - 1)
                            s.activation(trk[u][:, step % 2, 0, :],
                                         ptr[u][:, 0, :], AF.Copy)
                            s.activation(trk[u][:, step % 2, 1, :],
                                         ptr[u][:, 1, :], AF.Copy)
                            s.activation(trv[u][:, step % 2, 0, :],
                                         ptr[u][:, 2, :], AF.Copy)
                            P.inc(sfx("trE", u), s.activation(
                                trv[u][:, step % 2, 1, :], ptr[u][:, 3, :],
                                AF.Copy))
                        if step >= 1:
                            b = step - 1
                            n = grp * NBLK + b
                            m = 2 * n + u
                            s.wait_ge(P.sem(sfx("oS", u)), n + 1)
                            if m >= 2:
                                s.wait_ge(P.sem(ost[m % 2]), ost_tot[m - 2])
                            P.inc(oev, s.activation(ostg[:, m % 2, :],
                                                    pA[u][:], AF.Copy))
        blk.scalar(sc_body)

        def t_body(t):
            for grp in range(NG):
                for u in range(2):
                    t.wait_ge(P.sem(sfx("decA", u)), grp + 1)
                    if grp >= 1:
                        t.wait_ge(P.sem(sfx("qkddS", u)), grp * NBLK)
                    t.matmul(pqk[u][:, 0:256], C["ones1_bf"][:], qdrow[u][:],
                             start=True, stop=True)
                    P.inc(sfx("decP", u), t.matmul(pqk[u][:, 256:512],
                                                   C["ones1_bf"][:],
                                                   kdrow[u][:],
                                                   start=True, stop=True))
                for step in range(NBLK + 1):
                    for u in range(2):
                        if step < NBLK:
                            b = step
                            n = grp * NBLK + b
                            hf = b // 8
                            c0 = (b % 8) * 256
                            t.wait_ge(P.sem(sfx("ldq", u)),
                                      48 * (2 * grp + hf + 1))
                            t.wait_ge(P.sem(sfx("qkddS", u)), n)
                            if step == 0:
                                t.wait_ge(P.sem(sfx("decE", u)), grp + 1)
                            t.matmul(pqk[u][:, 0:256],
                                     kbc[u][:, hf, c0:c0 + 128],
                                     qbc[u][:, hf, c0:c0 + 256],
                                     start=True, stop=True)
                            P.inc(sfx("qkS", u), t.matmul(
                                pqk[u][:, 256:512],
                                kbc[u][:, hf, c0 + 128:c0 + 256],
                                qbc[u][:, hf, c0:c0 + 256],
                                start=True, stop=True))
                            t.wait_ge(P.sem(sfx("qkdS", u)), 2 * (n + 1))
                            if n >= 1:
                                t.wait_ge(P.sem(sfx("trE", u)), n)
                            t.transpose(ptr[u][:, 0, :], kkd[u][:, 0:128],
                                        C["ident_bf"][:])
                            t.transpose(ptr[u][:, 1, :], kkd[u][:, 128:256],
                                        C["ident_bf"][:])
                            t.transpose(ptr[u][:, 2, :],
                                        vbc[u][:, hf, c0:c0 + 128],
                                        C["ident_bf"][:])
                            P.inc(sfx("trS", u), t.transpose(
                                ptr[u][:, 3, :],
                                vbc[u][:, hf, c0 + 128:c0 + 256],
                                C["ident_bf"][:]))
                        if step >= 1:
                            b = step - 1
                            n = grp * NBLK + b
                            t.wait_ge(P.sem(sfx("trE", u)), n + 1)
                            t.wait_ge(P.sem(sfx("kvbS", u)),
                                      grp * (NBLK + 1) + 1 + b)
                            t.wait_ge(P.sem(sfx("qkddS", u)), n + 1)
                            t.wait_ge(P.sem(sfx("qkdS", u)), 2 * (n + 1))
                            m = 2 * n + u
                            if m >= 2:
                                t.wait_ge(P.sem(oev), m - 1)
                            t.matmul(pA[u][:], kvb[u][:],
                                     qqd[u][:, b % 2, :], start=True,
                                     stop=False)
                            t.matmul(pA[u][:], trv[u][:, b % 2, 0, :],
                                     qkdd[u][:, b % 2, 0, :],
                                     start=False, stop=False)
                            P.inc(sfx("oS", u), t.matmul(
                                pA[u][:], trv[u][:, b % 2, 1, :],
                                qkdd[u][:, b % 2, 1, :],
                                start=False, stop=True))
                            if n >= 1:
                                t.wait_ge(P.sem(sfx("kvfS", u)), n)
                            t.matmul(pkv[u][:], trk[u][:, b % 2, 0, :],
                                     trv[u][:, b % 2, 0, :],
                                     start=True, stop=False)
                            P.inc(sfx("kvdS", u), t.matmul(
                                pkv[u][:], trk[u][:, b % 2, 1, :],
                                trv[u][:, b % 2, 1, :],
                                start=False, stop=True))
        blk.tensor(t_body)

        def s_out(sy):
            for grp in range(NG):
                for b in range(NBLK):
                    for u in range(2):
                        h = 2 * grp + u
                        m = (grp * NBLK + b) * 2 + u
                        sy.wait_ge(P.sem(oev), m + 1)
                        P.inc(ost[m % 2], sy.dma_start(
                            outh_ap[h * 128:(h + 1) * 128,
                                    b * 256:(b + 1) * 256],
                            ostg[:, m % 2, :]), 16)
            sy.wait_ge(P.sem(ost[0]), P.n(ost[0]))
            sy.wait_ge(P.sem(ost[1]), P.n(ost[1]))
        blk.sync(s_out)


def emit_lightning_out(P, tag, xln_ap, outh_ap, var_ap, wgate_ap, normw_ap, z_ap):
    nc = P.nc
    sb = P.sbuf
    C = P.consts
    xr = sb(f"{tag}_xr", [128, DT, T], BF16)
    wgt = [sb(f"{tag}_wg{b}", [128, DT, 128], BF16) for b in range(2)]
    oh = [sb(f"{tag}_oh{b}", [128, T], BF16) for b in range(2)]
    nw = sb(f"{tag}_nw", [128, 16], F32)
    varr = sb(f"{tag}_var", [1, T], F32)
    rstd = varr
    rstdb = sb(f"{tag}_rb", [128, T], F32)
    gsb = sb(f"{tag}_gsb", [128, 2, PN], F32)
    zt = sb(f"{tag}_zt", [128, 2, PN], BF16)
    t1 = sb(f"{tag}_t1", [128, PN], F32)
    ppj = [P.psum(f"{tag}_pp{b}", [128, PN], F32) for b in range(2)]

    sqv = sb(f"{tag}_sqv", [128, 2, PN], BF16)
    pvar = [P.psum(f"{tag}_pv{b}", [128, PN], F32) for b in range(2)]
    ldx, ldn, rsd, gev, zdone = (f"{tag}_ldx", f"{tag}_ldn",
                                 f"{tag}_rsd", f"{tag}_gev", f"{tag}_zd")
    sqA, vmm, vS = f"{tag}_sqA", f"{tag}_vmm", f"{tag}_vS"
    ldvh = [f"{tag}_lv0", f"{tag}_lv1"]
    ldw = [f"{tag}_lw0", f"{tag}_lw1"]
    ldo = [f"{tag}_lo0", f"{tag}_lo1"]
    zst = [f"{tag}_zs0", f"{tag}_zs1"]
    ldw_tot = [DT * 16 * (o // 2 + 1) for o in range(DT)]
    ldo_tot = [16 * (o // 2 + 1) for o in range(DT)]
    zst_tot = [NPAN * 16 * (o // 2 + 1) for o in range(DT)]

    with nc.Block() as blk:
        def g_body(g):
            for k in range(DT):
                P.inc(ldx, g.dma_start(xr[:, k, :], xln_ap[k * 128:(k + 1) * 128, :]), 16)
            P.inc(ldn, g.dma_start(nw[:], normw_ap[:]), 16)
            for o in range(DT):
                b = o % 2
                if o >= 2:
                    g.wait_ge(P.sem(sqA), (o - 1) * NPAN)
                P.inc(ldvh[b], g.dma_start(oh[b][:],
                                           outh_ap[o * 128:(o + 1) * 128, :]), 16)
            g.wait_ge(P.sem(f"{tag}_rst"), 16)
            src2 = bass.AP(var_ap.tensor, var_ap.offset, [[0, 128], [1, T]])
            P.inc(f"{tag}_rb", g.dma_start(rstdb[:], src2), 16)
            for o in range(DT):
                b = o % 2
                if o >= 2:
                    g.wait_ge(P.sem(gev), (o - 1) * NPAN)
                for k in range(DT):
                    P.inc(ldw[b], g.dma_start(
                        wgt[b][:, k, :], wgate_ap[k * 128:(k + 1) * 128,
                                                  o * 128:(o + 1) * 128]), 16)
                assert P.n(ldw[b]) == ldw_tot[o]
                if o == 0:
                    g.wait_ge(P.sem(sqA), DT * NPAN)
                if o >= 2:
                    g.wait_ge(P.sem(zdone), (o - 1) * NPAN)
                P.inc(ldo[b], g.dma_start(oh[b][:], outh_ap[o * 128:(o + 1) * 128, :]), 16)
                assert P.n(ldo[b]) == ldo_tot[o]
        blk.gpsimd(g_body)

        def sc_body(s):
            for o in range(DT):
                b = o % 2
                s.wait_ge(P.sem(ldvh[b]), 16 * (o // 2 + 1))
                for pn in range(NPAN):
                    i = o * NPAN + pn
                    if i >= 2:
                        s.wait_ge(P.sem(vmm), i - 1)
                    P.inc(sqA, s.activation(sqv[:, i % 2, :],
                                            oh[b][:, pn * PN:(pn + 1) * PN],
                                            AF.Square))
            s.wait_ge(P.sem(vS), DT * NPAN)
            s.activation(rstd[:], varr[:], AF.Ln, scale=1.0 / D, bias=EPS_RMS)
            P.inc(rsd, s.activation(rstd[:], rstd[:], AF.Exp, scale=-0.5))
            for o in range(DT):
                for pn in range(NPAN):
                    i = o * NPAN + pn
                    s.wait_ge(P.sem(f"{tag}_pj"), i + 1)
                    if i >= 2:
                        s.wait_ge(P.sem(zdone), i - 1)
                    P.inc(gev, s.activation(gsb[:, i % 2, :], ppj[i % 2][:], AF.Sigmoid))
        blk.scalar(sc_body)

        def t_body(t):
            for i in range(DT * NPAN):
                t.wait_ge(P.sem(sqA), i + 1)
                if i >= 2:
                    t.wait_ge(P.sem(vS), i - 1)
                P.inc(vmm, t.matmul(pvar[i % 2][:], C["ones_bf"][:],
                                    sqv[:, i % 2, :], start=True, stop=True))
            t.wait_ge(P.sem(ldx), DT * 16)
            for o in range(DT):
                b = o % 2
                t.wait_ge(P.sem(ldw[b]), ldw_tot[o])
                for pn in range(NPAN):
                    i = o * NPAN + pn
                    if i >= 2:
                        t.wait_ge(P.sem(gev), i - 1)
                    for k in range(DT):
                        last = t.matmul(ppj[i % 2][:], wgt[b][:, k, :],
                                        xr[:, k, pn * PN:(pn + 1) * PN],
                                        start=(k == 0), stop=(k == DT - 1))
                    P.inc(f"{tag}_pj", last)
        blk.tensor(t_body)

        def v_body(v):
            v.memset(varr[:], 0.0)
            for i in range(DT * NPAN):
                pn = i % NPAN
                cs = slice(pn * PN, (pn + 1) * PN)
                v.wait_ge(P.sem(vmm), i + 1)
                P.inc(vS, v.tensor_tensor(varr[0:1, cs], varr[0:1, cs],
                                          pvar[i % 2][0:1, :], ALU.add))
            for o in range(DT):
                b = o % 2
                v.wait_ge(P.sem(ldo[b]), ldo_tot[o])
                v.wait_ge(P.sem(ldn), 16)
                v.wait_ge(P.sem(f"{tag}_rb"), 16)
                for pn in range(NPAN):
                    i = o * NPAN + pn
                    zb = i % 2
                    cs = slice(pn * PN, (pn + 1) * PN)
                    v.wait_ge(P.sem(gev), i + 1)
                    if i >= 2:
                        v.wait_ge(P.sem(zst[zb]), 16 * (i // 2))
                    v.tensor_tensor(t1[:], oh[b][:, cs], rstdb[:, cs], ALU.mult)
                    v.tensor_single_scalar(t1[:], t1[:], nw[:, o:o + 1], ALU.mult)
                    P.inc(zdone, v.tensor_tensor(zt[:, zb, :], t1[:], gsb[:, i % 2, :], ALU.mult))
        blk.vector(v_body)

        def s_out(sy):
            sy.wait_ge(P.sem(rsd), 1)
            P.inc(f"{tag}_rst", sy.dma_start(var_ap[:], rstd[:]), 16)
            for o in range(DT):
                for pn in range(NPAN):
                    i = o * NPAN + pn
                    zb = i % 2
                    sy.wait_ge(P.sem(zdone), i + 1)
                    P.inc(zst[zb], sy.dma_start(
                        z_ap[o * 128:(o + 1) * 128, pn * PN:(pn + 1) * PN],
                        zt[:, zb, :]), 16)
            sy.wait_ge(P.sem(zst[0]), P.n(zst[0]))
            sy.wait_ge(P.sem(zst[1]), P.n(zst[1]))
        blk.sync(s_out)


def emit_matmul_residual(P, tag, z_ap, w_ap, x_ap, xout_ap, KD, alpha=1.0,
                         bias_ap=None, contrib_ap=None):
    nc = P.nc
    sb = P.sbuf
    KT = KD // 128
    NSL4 = 4
    zr = sb(f"{tag}_zr", [128, KT, T], BF16)
    wt = [sb(f"{tag}_wt{b}", [128, KT, 128], BF16) for b in range(2)]
    bias = sb(f"{tag}_bias", [128, 16], F32)
    xt = sb(f"{tag}_xt", [128, NSL4, PN], F32)
    xo = sb(f"{tag}_xo", [128, NSL4, PN], F32)
    pmm = P.psum(f"{tag}_pm", [128, NSL4, PN], F32)

    ldz, ldb, mm, appd = f"{tag}_ldz", f"{tag}_ldb", f"{tag}_mm", f"{tag}_ap"
    ldw = [f"{tag}_lw0", f"{tag}_lw1"]
    ldx = [f"{tag}_lx{b}" for b in range(NSL4)]
    od = [f"{tag}_od{b}" for b in range(NSL4)]
    ct = sb(f"{tag}_ct", [128, NSL4, PN], BF16) if contrib_ap is not None else None
    tmpc = sb(f"{tag}_tmpc", [128, PN], F32) if contrib_ap is not None else None
    NT = DT * NPAN
    nld = 2 if contrib_ap is not None else 1
    ldw_tot = [16 * (o // 2 + 1) for o in range(DT)]
    ldx_tot = [nld * 16 * (nt // NSL4 + 1) for nt in range(NT)]
    od_tot = [16 * (nt // NSL4 + 1) for nt in range(NT)]

    with nc.Block() as blk:
        def g_body(g):
            for k in range(KT):
                P.inc(ldz, g.dma_start(zr[:, k, :], z_ap[k * 128:(k + 1) * 128, :]), 16)
            if bias_ap is not None:
                P.inc(ldb, g.dma_start(bias[:], bias_ap[:]), 16)
            def xloads(o):
                for pn in range(NPAN):
                    nt = o * NPAN + pn
                    xb = nt % NSL4
                    if nt >= NSL4 - 1:
                        g.wait_ge(P.sem(appd), nt - (NSL4 - 1))
                    cs = slice(pn * PN, (pn + 1) * PN)
                    P.inc(ldx[xb], g.dma_start(xt[:, xb, :],
                                               x_ap[o * 128:(o + 1) * 128, cs]), 16)
                    if contrib_ap is not None:
                        P.inc(ldx[xb], g.dma_start(
                            ct[:, xb, :], contrib_ap[o * 128:(o + 1) * 128, cs]), 16)
            for o in range(DT):
                b = o % 2
                if o >= 2:
                    g.wait_ge(P.sem(mm), (o - 1) * NPAN)
                src = bass.AP(w_ap.tensor, w_ap.offset + o * 128 * KT * 128,
                              [[KT * 128, 128], [1, KT * 128]])
                P.inc(ldw[b], g.dma_start(wt[b][:, :, :], src), 16)
                assert P.n(ldw[b]) == ldw_tot[o]
                if o >= 1:
                    xloads(o - 1)
            xloads(DT - 1)
        blk.gpsimd(g_body)

        def t_body(t):
            t.wait_ge(P.sem(ldz), KT * 16)
            for o in range(DT):
                b = o % 2
                t.wait_ge(P.sem(ldw[b]), ldw_tot[o])
                for pn in range(NPAN):
                    i = o * NPAN + pn
                    if i >= NSL4:
                        t.wait_ge(P.sem(appd), i - (NSL4 - 1))
                    for k in range(KT):
                        last = t.matmul(pmm[:, i % NSL4, :], wt[b][:, k, :],
                                        xr_slice(zr, k, pn),
                                        start=(k == 0), stop=(k == KT - 1))
                    P.inc(mm, last)
        blk.tensor(t_body)

        def v_body(v):
            if bias_ap is not None:
                v.wait_ge(P.sem(ldb), 16)
            for nt in range(NT):
                o, pn = divmod(nt, NPAN)
                xb = nt % NSL4
                v.wait_ge(P.sem(mm), nt + 1)
                if nt >= NSL4:
                    v.wait_ge(P.sem(od[xb]), od_tot[nt - NSL4])
                v.wait_ge(P.sem(ldx[xb]), ldx_tot[nt])
                if contrib_ap is not None:
                    v.tensor_tensor(tmpc[:], pmm[:, xb, :], ct[:, xb, :], ALU.add)
                    P.inc(appd, v.scalar_tensor_tensor(
                        xo[:, xb, :], tmpc[:], alpha, xt[:, xb, :],
                        ALU.mult, ALU.add))
                elif bias_ap is not None:
                    v.tensor_single_scalar(xo[:, xb, :], pmm[:, xb, :],
                                           bias[:, o:o + 1], ALU.add)
                    P.inc(appd, v.tensor_tensor(xo[:, xb, :], xo[:, xb, :],
                                                xt[:, xb, :], ALU.add))
                elif alpha == 1.0:
                    P.inc(appd, v.tensor_tensor(xo[:, xb, :], pmm[:, xb, :],
                                                xt[:, xb, :], ALU.add))
                else:
                    P.inc(appd, v.scalar_tensor_tensor(
                        xo[:, xb, :], pmm[:, xb, :], alpha, xt[:, xb, :],
                        ALU.mult, ALU.add))
        blk.vector(v_body)

        def sc_store(sc):
            for nt in range(NT):
                o, pn = divmod(nt, NPAN)
                xb = nt % NSL4
                sc.wait_ge(P.sem(appd), nt + 1)
                cs = slice(pn * PN, (pn + 1) * PN)
                P.inc(od[xb], sc.dma_start(xout_ap[o * 128:(o + 1) * 128, cs],
                                           xo[:, xb, :]), 16)
            for b in range(NSL4):
                sc.wait_ge(P.sem(od[b]), P.n(od[b]))
        blk.scalar(sc_store)


def xr_slice(zr, k, pn):
    return zr[:, k, pn * PN:(pn + 1) * PN]


def emit_l1ffn_core(P, tag, xln_ap, gate_w_ap, eaw_ap, atoms_ap, imp_ap,
                    midx_ap, contrib_ap, tile_blocks):
    nc = P.nc
    sb = P.sbuf
    C = P.consts
    NTI = 32
    weighted = sb(f"{tag}_wtd", [8, NS], BF16)
    rTb = sb(f"{tag}_rtb", [8, T], BF16)
    midx = sb(f"{tag}_midx", [128, 128], I32)
    rows_f = sb(f"{tag}_rows", [128, 128], F32)
    cols_i = sb(f"{tag}_cols", [128, 128], I32)
    ldm, rcd = f"{tag}_ldm", f"{tag}_rc"

    P.open_scope()
    xp = [sb(f"{tag}_xp{b}", [128, DT, PN], BF16) for b in range(2)]
    gw = sb(f"{tag}_gw", [128, DT, 8], BF16)
    lg = sb(f"{tag}_lg", [8, T], F32)
    ltok = sb(f"{tag}_lt", [128, NTI, 8], F32)
    m1a = sb(f"{tag}_m1", [128, NTI], F32)
    m2a = sb(f"{tag}_m2", [128, NTI], F32)
    i1a = sb(f"{tag}_i1", [128, NTI], F32)
    i2a = sb(f"{tag}_i2", [128, NTI], F32)
    rw1 = sb(f"{tag}_rw1", [128, NTI], F32)
    rw2 = sb(f"{tag}_rw2", [128, NTI], F32)
    scr_all = sb(f"{tag}_scrA", [128, NTI, 8], F32)
    scr2_all = sb(f"{tag}_scrB", [128, NTI, 8], F32)
    scr_e = sb(f"{tag}_scrE", [8, 4], F32)
    rw2b1 = sb(f"{tag}_rb1", [128, NTI], F32)
    rw2b2 = sb(f"{tag}_rb2", [128, NTI], F32)
    Rt = sb(f"{tag}_R", [128, NTI, 8], F32)
    eaw_s = sb(f"{tag}_eaw", [8, 64], F32)
    sm64 = sb(f"{tag}_sm64", [64, 8], BF16)
    atsl = [sb(f"{tag}_at{b}", [64, 2048], BF16) for b in range(2)]
    imsl = [sb(f"{tag}_im{b}", [8, 2048], F32) for b in range(2)]
    sgsl = sb(f"{tag}_sg", [8, 2048], F32)
    plog = P.psum(f"{tag}_plog", [8, PN], F32)
    ptr = P.psum(f"{tag}_ptr", [128, 8], F32)
    prT = P.psum(f"{tag}_prT", [8, 128], F32)
    ptr2 = P.psum(f"{tag}_ptr2", [64, 8], F32)
    pdel = P.psum(f"{tag}_pdel", [8, PN], F32)

    ldx = [f"{tag}_lx0", f"{tag}_lx1"]
    lda = [f"{tag}_la0", f"{tag}_la1"]
    (ldg, ldeaw, mmL, clipd, trS, trE, top2d, rwd, Rd, rTt, rTe, smd, smTd,
     smTe, deld, sigd, wtdd, esub, eexp, dsub, dexp) = [f"{tag}_{x}" for x in (
        "ldg", "lde", "mmL", "cl", "trS", "trE", "t2", "rw", "Rd", "rTt", "rTe",
        "sm", "sT", "sTe", "dl", "sgd", "wt", "es", "ee", "ds", "de")]
    ldx_tot = [DT * 16 * (p // 2 + 1) for p in range(NPAN)]
    lda_tot = [2 * 16 * (i // 2 + 1) for i in range(8)]

    with nc.Block() as blk:
        def g_body(g):
            P.inc(ldm, g.dma_start(midx[:], midx_ap[:]), 16)
            for k in range(DT):
                P.inc(ldg, g.dma_start(gw[:, k, :], gate_w_ap[k * 128:(k + 1) * 128, :]), 16)
            P.inc(ldeaw, g.dma_start(eaw_s[:], eaw_ap[:]), 16)
            for p in range(NPAN):
                b = p % 2
                if p >= 2:
                    g.wait_ge(P.sem(mmL), p - 1)
                for k in range(DT):
                    P.inc(ldx[b], g.dma_start(xp[b][:, k, :],
                                              xln_ap[k * 128:(k + 1) * 128,
                                                     p * PN:(p + 1) * PN]), 16)
                assert P.n(ldx[b]) == ldx_tot[p]
            for i in range(8):
                b = i % 2
                if i >= 2:
                    g.wait_ge(P.sem(wtdd), (i - 1) * 4)
                P.inc(lda[b], g.dma_start(atsl[b][:], atoms_ap[:, i * 2048:(i + 1) * 2048]), 16)
                P.inc(lda[b], g.dma_start(imsl[b][:], imp_ap[:, i * 2048:(i + 1) * 2048]), 16)
                assert P.n(lda[b]) == lda_tot[i]
        blk.gpsimd(g_body)

        def t_body(t):
            t.wait_ge(P.sem(ldg), DT * 16)
            for p in range(NPAN):
                b = p % 2
                t.wait_ge(P.sem(ldx[b]), ldx_tot[p])
                if p >= 1:
                    t.wait_ge(P.sem(clipd), p)
                for k in range(DT):
                    last = t.matmul(plog[:], gw[:, k, :], xp[b][:, k, :],
                                    start=(k == 0), stop=(k == DT - 1))
                P.inc(mmL, last)
            for ti in range(NTI):
                t.wait_ge(P.sem(clipd), (ti // 4) + 1)
                if ti >= 1:
                    t.wait_ge(P.sem(trE), ti)
                P.inc(trS, t.transpose(ptr[:], lg[0:8, ti * 128:(ti + 1) * 128],
                                       C["ident_f32"][0:8, 0:8]))
            t.wait_ge(P.sem(smd), 1)
            P.inc(smTd, t.transpose(ptr2[:], eaw_s[:], C["ident_f32"][0:8, 0:8]))
            for i in range(8):
                b = i % 2
                t.wait_ge(P.sem(lda[b]), lda_tot[i])
                t.wait_ge(P.sem(smTe), 1)
                for q in range(4):
                    idx = i * 4 + q
                    if idx >= 1:
                        t.wait_ge(P.sem(wtdd), idx)
                    P.inc(deld, t.matmul(pdel[:, 0:512], sm64[:],
                                         atsl[b][:, q * 512:(q + 1) * 512],
                                         start=True, stop=True))
            for ti in range(NTI):
                t.wait_ge(P.sem(Rd), 8)
                if ti >= 1:
                    t.wait_ge(P.sem(rTe), ti)
                P.inc(rTt, t.transpose(prT[:], Rt[:, ti, :], C["ident_f32"][:]))
        blk.tensor(t_body)

        def v_body(v):
            for p in range(NPAN):
                v.wait_ge(P.sem(mmL), p + 1)
                P.inc(clipd, v.tensor_scalar(lg[:, p * PN:(p + 1) * PN], plog[:],
                                             50.0, -50.0, ALU.min, ALU.max))
            v.wait_ge(P.sem(ldeaw), 16)
            v.wait_ge(P.sem(trE), NTI)
            v.reduce_max(scr_e[0:8, 0:1], eaw_s[:], axis=AX)
            for ti in range(NTI):
                v.reduce_max(m1a[:, ti:ti + 1], ltok[:, ti, :], axis=AX)
            mxb = bass.AP(scr_e, 0, [[4, 8], [0, 64]])
            P.inc(esub, v.tensor_tensor(eaw_s[:], eaw_s[:], mxb, ALU.subtract))
            for ti in range(NTI):
                m1b = bass.AP(m1a, ti, [[NTI, 128], [0, 8]])
                v.tensor_tensor(scr_all[:, ti, :], ltok[:, ti, :], m1b, ALU.is_equal)
            v.wait_ge(P.sem(eexp), 1)
            v.reduce_sum(scr_e[0:8, 1:2], eaw_s[:], axis=AX)
            for ti in range(NTI):
                v.scalar_tensor_tensor(scr2_all[:, ti, :], scr_all[:, ti, :],
                                       -1000.0, C["iota8"][:], ALU.mult, ALU.add)
            v.reciprocal(scr_e[0:8, 2:3], scr_e[0:8, 1:2])
            for ti in range(NTI):
                v.tensor_reduce(i1a[:, ti:ti + 1], scr2_all[:, ti, :], axis=AX,
                                op=ALU.min)
            rb = bass.AP(scr_e, 2, [[4, 8], [0, 64]])
            P.inc(smd, v.tensor_tensor(eaw_s[:], eaw_s[:], rb, ALU.mult))
            for ti in range(NTI):
                i1b = bass.AP(i1a, ti, [[NTI, 128], [0, 8]])
                v.tensor_tensor(scr_all[:, ti, :], scr2_all[:, ti, :], i1b,
                                ALU.is_equal)
            for ti in range(NTI):
                v.scalar_tensor_tensor(scr2_all[:, ti, :], scr_all[:, ti, :],
                                       -1e30, ltok[:, ti, :], ALU.mult, ALU.add)
            for ti in range(NTI):
                v.reduce_max(m2a[:, ti:ti + 1], scr2_all[:, ti, :], axis=AX)
            for ti in range(NTI):
                m2b = bass.AP(m2a, ti, [[NTI, 128], [0, 8]])
                v.tensor_tensor(scr_all[:, ti, :], scr2_all[:, ti, :], m2b,
                                ALU.is_equal)
            for ti in range(NTI):
                v.scalar_tensor_tensor(scr_all[:, ti, :], scr_all[:, ti, :],
                                       -1000.0, C["iota8"][:], ALU.mult, ALU.add)
            for ti in range(NTI):
                v.tensor_reduce(i2a[:, ti:ti + 1], scr_all[:, ti, :], axis=AX,
                                op=ALU.min)
            v.tensor_tensor(m2a[:], m2a[:], m1a[:], ALU.subtract)
            P.inc(dsub, v.tensor_scalar_add(i1a[:], i1a[:], 1000.0))
            v.tensor_scalar_add(i2a[:], i2a[:], 1000.0)
            v.wait_ge(P.sem(ldm), 16)
            v.wait_ge(P.sem(dexp), 1)
            v.tensor_scalar_add(rw2[:], m2a[:], 1.0)
            v.tensor_single_scalar(cols_i[:], midx[:], 2047, ALU.bitwise_and)
            v.tensor_single_scalar(midx[:], midx[:], 11, ALU.logical_shift_right)
            v.reciprocal(rw1[:], rw2[:])
            P.inc(rcd, v.tensor_copy(rows_f[:], midx[:]))
            P.inc(rwd, v.tensor_tensor(rw2[:], m2a[:], rw1[:], ALU.mult))
            for e in range(0, 8, 2):
                v.tensor_single_scalar(scr_all[:, 0, :][:, 0:32] if False else m2a[:],
                                       i1a[:], float(e), ALU.is_equal)
                v.tensor_single_scalar(m1a[:], i2a[:], float(e), ALU.is_equal)
                v.tensor_single_scalar(scr_all[:, 0:4, :].rearrange("p a b -> p (a b)")[:, 0:32] if False else rw2b1[:],
                                       i1a[:], float(e + 1), ALU.is_equal)
                v.tensor_single_scalar(rw2b2[:], i2a[:], float(e + 1), ALU.is_equal)
                v.tensor_tensor(m2a[:], m2a[:], rw1[:], ALU.mult)
                v.tensor_tensor(m1a[:], m1a[:], rw2[:], ALU.mult)
                v.tensor_tensor(rw2b1[:], rw2b1[:], rw1[:], ALU.mult)
                v.tensor_tensor(rw2b2[:], rw2b2[:], rw2[:], ALU.mult)
                P.inc(Rd, v.tensor_tensor(Rt[:, :, e], m2a[:], m1a[:], ALU.add))
                P.inc(Rd, v.tensor_tensor(Rt[:, :, e + 1], rw2b1[:], rw2b2[:],
                                          ALU.add))
            for i in range(8):
                for q in range(4):
                    v.wait_ge(P.sem(deld), i * 4 + q + 1)
                    v.wait_ge(P.sem(sigd), i + 1)
                    P.inc(wtdd, v.tensor_tensor(
                        weighted[:, i * 2048 + q * 512:i * 2048 + (q + 1) * 512],
                        pdel[:, 0:512], sgsl[:, q * 512:(q + 1) * 512], ALU.mult))
        blk.vector(v_body)

        def sc_body(s):
            for ti in range(NTI):
                s.wait_ge(P.sem(trS), ti + 1)
                P.inc(trE, s.activation(ltok[:, ti, :], ptr[:], AF.Copy))
            s.wait_ge(P.sem(esub), 1)
            P.inc(eexp, s.activation(eaw_s[:], eaw_s[:], AF.Exp))
            s.wait_ge(P.sem(smTd), 1)
            P.inc(smTe, s.activation(sm64[:], ptr2[:], AF.Copy))
            s.wait_ge(P.sem(dsub), 1)
            P.inc(dexp, s.activation(m2a[:], m2a[:], AF.Exp))
            for i in range(8):
                b = i % 2
                s.wait_ge(P.sem(lda[b]), lda_tot[i])
                if i >= 1:
                    s.wait_ge(P.sem(wtdd), i * 4)
                P.inc(sigd, s.activation(sgsl[:], imsl[b][:], AF.Sigmoid))
            for ti in range(NTI):
                s.wait_ge(P.sem(rTt), ti + 1)
                P.inc(rTe, s.activation(rTb[0:8, ti * 128:(ti + 1) * 128], prT[:],
                                        AF.Copy))
        blk.scalar(sc_body)
    P.close_scope()

    P.open_scope()
    HW = T // 2
    xg = [sb(f"{tag}_xg{b}", [128, HW], BF16) for b in range(4)]
    vsl = sb(f"{tag}_vsl", [128, 2, HW], BF16)
    prod = [sb(f"{tag}_pr{b}", [128, HW], BF16) for b in range(3)]
    Pt = sb(f"{tag}_Pt", [128, 3, 128], BF16)
    scrP = sb(f"{tag}_scrP", [128, 1], F32)
    csb = sb(f"{tag}_csb", [128, 2, 4, PN], BF16)
    pcon = [P.psum(f"{tag}_pc{q}", [128, PN], F32) for q in range(4)]
    pval = P.psum(f"{tag}_pv", [128, 4, PN], F32)

    gth = [f"{tag}_gt{b}" for b in range(4)]
    cstq = [f"{tag}_cs0", f"{tag}_cs1"]
    vmm, vev, prd, pbd, stile, cev = [f"{tag}_{x}" for x in
                                      ("vmm", "vev", "prd", "pbd", "stile", "cev")]

    plan = []
    n = 0
    npart = 0
    nclose = 0
    parts_through = {}
    parts_before_close = {}
    for half in range(2):
        cur = None
        present = set()
        pend = None

        def flush(pend):
            nonlocal npart, nclose, cur
            if pend is None:
                return
            pn_, pst, blocks = pend
            for B in blocks:
                present.add(B)
                if cur is not None and B != cur:
                    parts_before_close[nclose] = npart
                    plan.append(("close", nclose, cur, half))
                    nclose += 1
                cur = B
                plan.append(("part", npart, pn_, pst, B))
                npart += 1
            parts_through[pn_] = npart

        for st_ in range(128):
            plan.append(("tile", half, st_, n))
            flush(pend)
            pend = (n, st_, tile_blocks[st_])
            n += 1
        flush(pend)
        pend = None
        parts_before_close[nclose] = npart
        plan.append(("close", nclose, cur, half))
        nclose += 1
        for B in range(16):
            if B not in present:
                plan.append(("zclose", nclose, B, half))
                nclose += 1
        cur = None
    NTOT = n

    with nc.Block() as blk:
        def g_body(g):
            g.wait_ge(P.sem(rcd), 1)
            for ev in plan:
                if ev[0] == "tile":
                    _, half, st_, nn = ev
                    b = nn % 4
                    if nn >= 4:
                        g.wait_ge(P.sem(prd), nn - 3)
                    P.inc(gth[b], g.indirect_dma_start(
                        xg[b][:], None, xln_ap[:],
                        IndirectOffsetOnAxis(cols_i[:, st_:st_ + 1], 0),
                        element_offset=half * HW), 16)
        blk.gpsimd(g_body)

        groups = []
        for ev in plan:
            if ev[0] == "tile":
                groups.append(("tile", ev))
            elif groups and groups[-1][0] == "grp":
                groups[-1][1].append(ev)
            else:
                groups.append(("grp", [ev]))
        plan_pe = []
        held = None
        for g in groups:
            if g[0] == "tile":
                plan_pe.append(g[1])
                if held is not None:
                    plan_pe.extend(held)
                    held = None
            else:
                held = g[1]
        if held is not None:
            plan_pe.extend(held)

        def v_body(v):
            for ev in plan:
                if ev[0] == "tile":
                    _, half, st_, nn = ev
                    b = nn % 4
                    v.wait_ge(P.sem(gth[b]), 16 * (nn // 4 + 1))
                    v.wait_ge(P.sem(vev), nn * 2 + 2)
                    if nn >= 3:
                        v.wait_ge(P.sem(stile), parts_through[nn - 3])
                    P.inc(prd, v.tensor_tensor(prod[nn % 3][:], xg[b][:],
                                               vsl[:, nn % 2, :], ALU.mult))
                elif ev[0] == "part":
                    _, np_, nn, st_, B = ev
                    if np_ >= 3:
                        v.wait_ge(P.sem(stile), np_ - 2)
                    v.tensor_single_scalar(scrP[:], rows_f[:, st_:st_ + 1],
                                           float(128 * B), ALU.subtract)
                    rcb = bass.AP(scrP, 0, [[1, 128], [0, 128]])
                    P.inc(pbd, v.tensor_tensor(Pt[:, np_ % 3, :], rcb,
                                               C["iota128"][:], ALU.is_equal))
        blk.vector(v_body)

        def t_body(t):
            t.wait_ge(P.sem(rTe), NTI)
            bank_open = [False] * 4
            pend_cev = 0
            for ev in plan_pe:
                if ev[0] == "tile":
                    _, half, st_, nn = ev
                    for q in range(4):
                        if nn >= 1:
                            t.wait_ge(P.sem(vev), 2 * (nn - 1) + q // 2 + 1)
                        P.inc(vmm, t.matmul(
                            pval[:, q, :], weighted[:, st_ * 128:(st_ + 1) * 128],
                            rTb[0:8, half * HW + q * PN:half * HW + (q + 1) * PN],
                            start=True, stop=True))
                elif ev[0] == "part":
                    _, np_, nn, st_, B = ev
                    if pend_cev:
                        t.wait_ge(P.sem(cev), pend_cev)
                        pend_cev = 0
                    t.wait_ge(P.sem(pbd), np_ + 1)
                    t.wait_ge(P.sem(prd), nn + 1)
                    for q in range(4):
                        last = t.matmul(pcon[q][:], Pt[:, np_ % 3, :],
                                        prod[nn % 3][:, q * PN:(q + 1) * PN],
                                        start=bank_open[q] is False, stop=False,
                                        skip_group_check=True)
                        bank_open[q] = True
                    P.inc(stile, last)
                elif ev[0] == "close":
                    _, ci, B, half = ev
                    pend_cev = (ci + 1) * 4
                    bank_open = [False] * 4
        blk.tensor(t_body)

        def sc_body(s):
            for ev in plan:
                if ev[0] == "tile":
                    _, half, st_, nn = ev
                    for h in range(2):
                        s.wait_ge(P.sem(vmm), nn * 4 + 2 * h + 2)
                        if nn >= 2 and h == 0:
                            s.wait_ge(P.sem(prd), nn - 1)
                        P.inc(vev, s.activation(
                            vsl[:, nn % 2, h * 2 * PN:(h + 1) * 2 * PN],
                            pval[:, 2 * h:2 * h + 2, :], AF.Copy))
                elif ev[0] == "close":
                    _, ci, B, half = ev
                    s.wait_ge(P.sem(stile), parts_before_close[ci])
                    if ci >= 2:
                        s.wait_ge(P.sem(cstq[ci % 2]), 4 * 16 * (ci // 2))
                    for q in range(4):
                        P.inc(cev, s.activation(csb[:, ci % 2, q, :], pcon[q][:],
                                                AF.Copy))
                elif ev[0] == "zclose":
                    _, ci, B, half = ev
                    if ci >= 2:
                        s.wait_ge(P.sem(cstq[ci % 2]), 4 * 16 * (ci // 2))
                    for q in range(4):
                        P.inc(cev, s.memzero(csb[:, ci % 2, q, :]))
        blk.scalar(sc_body)

        def s_out(sy):
            for ev in plan:
                if ev[0] in ("close", "zclose"):
                    ci, B, half = ev[1], ev[2], ev[3]
                    sy.wait_ge(P.sem(cev), (ci + 1) * 4)
                    for q in range(4):
                        P.inc(cstq[ci % 2], sy.dma_start(
                            contrib_ap[B * 128:(B + 1) * 128,
                                       half * HW + q * PN:half * HW + (q + 1) * PN],
                            csb[:, ci % 2, q, :]), 16)
            sy.wait_ge(P.sem(cstq[0]), P.n(cstq[0]))
            sy.wait_ge(P.sem(cstq[1]), P.n(cstq[1]))
        blk.sync(s_out)
    P.close_scope()



def build_l1ffn(xsrc_name, xdst_name, tile_blocks):
    P = Prog("l1ffn")
    nc = P.nc
    dram = decl_consts(P)
    x_in = nc.dram_tensor(xsrc_name, [D, T], F32, kind="ExternalInput").ap()
    gw = nc.dram_tensor("gate_wT", [D, 8], F32, kind="ExternalInput").ap()
    WT = nc.dram_tensor("WT", [D, D], F32, kind="ExternalInput").ap()
    eaw = nc.dram_tensor("eaw", [8, 64], F32, kind="ExternalInput").ap()
    atoms = nc.dram_tensor("atoms_s", [64, NS], F32, kind="ExternalInput").ap()
    imp = nc.dram_tensor("imp_s", [8, NS], F32, kind="ExternalInput").ap()
    midx = nc.dram_tensor("midx2d", [128, 128], I32, kind="ExternalInput").ap()
    x_out = nc.dram_tensor(xdst_name, [D, T], F32, kind="ExternalOutput").ap()
    xln = nc.dram_tensor("xln", [D, T], BF16, kind="Internal").ap()
    contrib = nc.dram_tensor("contrib", [D, T], BF16, kind="Internal").ap()
    emit_consts(P, dram)
    P.open_scope()
    emit_ln(P, "ln", x_in, xln)
    emit_barrier(P, "b1")
    P.close_scope()
    P.open_scope()
    emit_l1ffn_core(P, "lf", xln, gw, eaw, atoms, imp, midx, contrib, tile_blocks)
    emit_barrier(P, "b2")
    P.close_scope()
    P.open_scope()
    emit_matmul_residual(P, "wl", xln, WT, x_in, x_out, D, alpha=0.5,
                         contrib_ap=contrib)
    emit_barrier(P, "b3")
    P.close_scope()
    P.close()
    ins = CONST_NAMES + [xsrc_name, "gate_wT", "WT", "eaw", "atoms_s", "imp_s",
                         "midx2d"]
    return P.nc, ins, [xdst_name]


def l1ffn_host_plan(mask_idx):
    mask_idx = np.asarray(mask_idx)
    rows = mask_idx // D
    order = np.argsort(rows, kind="stable")
    midx_s = mask_idx[order]
    rows_s = rows[order]
    tile_blocks = []
    for st in range(128):
        r = rows_s[st * 128:(st + 1) * 128]
        blocks = sorted(set(int(b) for b in (r // 128)))
        tile_blocks.append(blocks)
    midx2d = np.ascontiguousarray(midx_s.reshape(128, 128).T).astype(np.int32)
    return order, midx2d, tile_blocks


def emit_gqa_proj(P, tag, xln_ap, wq_ap, wk_ap, wv_ap, qT_ap, kv_ap):
    nc = P.nc
    sb = P.sbuf
    C = P.consts
    G = 4
    NTK = 32
    RS = 0.08838834764831845
    xr = sb(f"{tag}_xr", [128, DT, T], BF16)
    wkb = sb(f"{tag}_wkb", [128, DT, 128], BF16)
    wvb = sb(f"{tag}_wvb", [128, DT, 128], BF16)
    wq = [sb(f"{tag}_wq{b}", [128, DT, 128], BF16) for b in range(2)]
    kstg = sb(f"{tag}_kstg", [128, 2, PN], BF16)
    vstage = sb(f"{tag}_vst", [128, T], BF16)
    vtstg = sb(f"{tag}_vts", [128, 2, 128], BF16)
    qstg = sb(f"{tag}_qstg", [128, 2, PN], BF16)
    ppj = [P.psum(f"{tag}_ppj{b}", [128, PN], F32) for b in range(2)]
    ptr = P.psum(f"{tag}_ptr", [128, 128], BF16)

    ldx, ldkv = f"{tag}_ldx", f"{tag}_ldkv"
    ldq = [f"{tag}_lq0", f"{tag}_lq1"]
    kstq = [f"{tag}_ks0", f"{tag}_ks1"]
    vstq = [f"{tag}_vs0", f"{tag}_vs1"]
    qstq = [f"{tag}_qs0", f"{tag}_qs1"]
    (kvmm, kev, ksq, ktr, ktrE, qmm, qev) = [
        f"{tag}_{x}" for x in ("kvmm", "kev", "ksq", "ktr", "ktrE",
                               "qmm", "qev")]

    with nc.Block() as blk:
        def g_body(g):
            for k in range(DT):
                P.inc(ldx, g.dma_start(xr[:, k, :], xln_ap[k * 128:(k + 1) * 128, :]), 16)
            for gi in range(G):
                if gi >= 1:
                    g.wait_ge(P.sem(kvmm), gi * 2 * NPAN)
                P.inc(ldkv, g.dma_start(wkb[:, :, :], bass.AP(
                    wk_ap.tensor, wk_ap.offset + gi * 128 * DT * 128,
                    [[DT * 128, 128], [1, DT * 128]])), 16)
                P.inc(ldkv, g.dma_start(wvb[:, :, :], bass.AP(
                    wv_ap.tensor, wv_ap.offset + gi * 128 * DT * 128,
                    [[DT * 128, 128], [1, DT * 128]])), 16)
            for h in range(NH):
                b = h % 2
                if h >= 2:
                    g.wait_ge(P.sem(qmm), (h - 1) * NPAN)
                P.inc(ldq[b], g.dma_start(wq[b][:, :, :], bass.AP(
                    wq_ap.tensor, wq_ap.offset + h * 128 * DT * 128,
                    [[DT * 128, 128], [1, DT * 128]])), 16)
        blk.gpsimd(g_body)

        def t_body(t):
            t.wait_ge(P.sem(ldx), DT * 16)
            for gi in range(G):
                t.wait_ge(P.sem(ldkv), (gi + 1) * 2 * 16)
                for pn in range(NPAN):
                    gp = gi * NPAN + pn
                    if gp >= 1:
                        t.wait_ge(P.sem(kev), gp)
                    for k in range(DT):
                        last = t.matmul(ppj[0][:], wkb[:, k, :],
                                        xr[:, k, pn * PN:(pn + 1) * PN],
                                        start=(k == 0), stop=(k == DT - 1))
                    P.inc(kvmm, last)
                    for k in range(DT):
                        last = t.matmul(ppj[1][:], wvb[:, k, :],
                                        xr[:, k, pn * PN:(pn + 1) * PN],
                                        start=(k == 0), stop=(k == DT - 1))
                    P.inc(kvmm, last)

                for ti in range(NTK):
                    gt = gi * NTK + ti
                    t.wait_ge(P.sem(kev), gi * NPAN + (ti // 4) + 1)
                    if gt >= 1:
                        t.wait_ge(P.sem(ktrE), gt)
                    P.inc(ktr, t.transpose(ptr[:],
                                           vstage[:, ti * 128:(ti + 1) * 128],
                                           C["ident_bf"][:]))
            for h in range(NH):
                b = h % 2
                t.wait_ge(P.sem(ldq[b]), 16 * ((h // 2) + 1))
                for pn in range(NPAN):
                    i = h * NPAN + pn
                    if i >= 2:
                        t.wait_ge(P.sem(qev), i - 1)
                    for k in range(DT):
                        last = t.matmul(ppj[i % 2][:], wq[b][:, k, :],
                                        xr[:, k, pn * PN:(pn + 1) * PN],
                                        start=(k == 0), stop=(k == DT - 1))
                    P.inc(qmm, last)
        blk.tensor(t_body)

        def sc_body(s):
            for gi in range(G):
                for pn in range(NPAN):
                    gp = gi * NPAN + pn
                    s.wait_ge(P.sem(kvmm), gp * 2 + 1)
                    if gp >= 2:
                        s.wait_ge(P.sem(kstq[gp % 2]), 16 * (gp // 2))
                    P.inc(ksq, s.activation(kstg[:, gp % 2, :], ppj[0][:], AF.Copy))
                    s.wait_ge(P.sem(kvmm), gp * 2 + 2)
                    if gi >= 1 and pn == 0:
                        s.wait_ge(P.sem(ktrE), gi * NTK)
                    P.inc(kev, s.activation(vstage[:, pn * PN:(pn + 1) * PN],
                                            ppj[1][:], AF.Copy))
                for ti in range(NTK):
                    gt = gi * NTK + ti
                    s.wait_ge(P.sem(ktr), gt + 1)
                    if gt >= 2:
                        s.wait_ge(P.sem(vstq[gt % 2]), 16 * (gt // 2))
                    P.inc(ktrE, s.activation(vtstg[:, gt % 2, :], ptr[:], AF.Copy))
            for h in range(NH):
                for pn in range(NPAN):
                    i = h * NPAN + pn
                    s.wait_ge(P.sem(qmm), i + 1)
                    if i >= 2:
                        s.wait_ge(P.sem(qstq[i % 2]), 16 * (i // 2))
                    P.inc(qev, s.activation(qstg[:, i % 2, :], ppj[i % 2][:],
                                            AF.Copy, scale=RS))
        blk.scalar(sc_body)

        def s_out(sy):
            for gi in range(G):
                for pn in range(NPAN):
                    gp = gi * NPAN + pn
                    sy.wait_ge(P.sem(ksq), gp + 1)
                    P.inc(kstq[gp % 2], sy.dma_start(
                        kv_ap[gi * 128:(gi + 1) * 128, pn * PN:(pn + 1) * PN],
                        kstg[:, gp % 2, :]), 16)
                for ti in range(NTK):
                    gt = gi * NTK + ti
                    sy.wait_ge(P.sem(ktrE), gt + 1)
                    P.inc(vstq[gt % 2], sy.dma_start(
                        kv_ap[512 + gi * 128:512 + (gi + 1) * 128,
                              ti * 128:(ti + 1) * 128],
                        vtstg[:, gt % 2, :]), 16)
            for h in range(NH):
                for pn in range(NPAN):
                    i = h * NPAN + pn
                    sy.wait_ge(P.sem(qev), i + 1)
                    P.inc(qstq[i % 2], sy.dma_start(
                        qT_ap[h * 128:(h + 1) * 128, pn * PN:(pn + 1) * PN],
                        qstg[:, i % 2, :]), 16)
            for q_ in (kstq[0], kstq[1], vstq[0], vstq[1], qstq[0], qstq[1]):
                sy.wait_ge(P.sem(q_), P.n(q_))
        blk.sync(s_out)


def emit_gqa_attn(P, tag, qT_ap, kv_ap, ctx_ap, var_ap):
    nc = P.nc
    sb = P.sbuf
    C = P.consts
    G = 4
    NTK = 32
    NPS = 4
    NES = 8
    L = 4
    kg = sb(f"{tag}_kg", [128, 2, T], BF16)
    vg = sb(f"{tag}_vg", [128, 2, NTK, 128], BF16)
    qh = sb(f"{tag}_qh", [128, 2, T], BF16)
    exps = sb(f"{tag}_expS", [128, NES, PN], BF16)
    dacc = sb(f"{tag}_dacc", [128, 2, PN], F32)
    rrow = sb(f"{tag}_rrow", [1, 2, PN], F32)
    rbx = sb(f"{tag}_rbx", [128, 2, PN], F32)
    cstg = sb(f"{tag}_cstg", [128, 2, PN], BF16)
    pS = P.psum(f"{tag}_pS", [128, NPS, PN], F32)
    pctx = [P.psum(f"{tag}_pctx{b}", [128, PN], F32) for b in range(2)]
    prow = P.psum(f"{tag}_prow", [1, 2, PN], F32)

    ldk = [f"{tag}_lk0", f"{tag}_lk1"]
    ldqh = [f"{tag}_lh0", f"{tag}_lh1"]
    cstq = [f"{tag}_ct0", f"{tag}_ct1"]
    rstq = [f"{tag}_rs0", f"{tag}_rs1"]
    rbxq = [f"{tag}_rb0", f"{tag}_rb1"]
    (smm, expd, dad, cmm, dsm, rrd, ctxd) = [
        f"{tag}_{x}" for x in ("smm", "exp", "dad", "cmm", "dsm", "rrd", "ctd")]

    NU = NH * NPAN
    def upn(u):
        return u // NPAN, u % NPAN
    def untk(u):
        return 4 * (u % NPAN) + 4
    s_base = {}
    si = 0
    tiles = []
    for u in range(NU):
        s_base[u] = si
        for ti in range(untk(u)):
            tiles.append((u, ti))
        si += untk(u)
    s_base[NU] = si
    NT = si

    with nc.Block() as blk:
        def g_body(g):
            def rbx_load(w):
                g.wait_ge(P.sem(rstq[w % 2]), 16 * (w // 2 + 1))
                if w >= 2:
                    g.wait_ge(P.sem(ctxd), w - 1)
                src = bass.AP(var_ap.tensor, var_ap.offset + (w % 2) * PN,
                              [[0, 128], [1, PN]])
                P.inc(rbxq[w % 2], g.dma_start(rbx[:, w % 2, :], src), 16)
            for u in range(NU):
                h, pn = upn(u)
                gi = h // 4
                if pn == 0:
                    b = h % 2
                    if h >= 2:
                        g.wait_ge(P.sem(ctxd), (h - 1) * NPAN)
                    P.inc(ldqh[b], g.dma_start(qh[:, b, :],
                                               qT_ap[h * 128:(h + 1) * 128, :]), 16)
                    if h % 4 == 0:
                        gb = gi % 2
                        if gi >= 2:
                            g.wait_ge(P.sem(ctxd), 4 * (gi - 1) * NPAN)
                        P.inc(ldk[gb], g.dma_start(kg[:, gb, :],
                                                   kv_ap[gi * 128:(gi + 1) * 128, :]), 16)
                        P.inc(ldk[gb], g.dma_start(
                            vg[:, gb, :, :],
                            kv_ap[512 + gi * 128:512 + (gi + 1) * 128, :]), 16)
                if u >= 1:
                    rbx_load(u - 1)
            rbx_load(NU - 1)
        blk.gpsimd(g_body)

        def t_body(t):
            def s_mm(p):
                u, ti = tiles[p]
                h, pn = upn(u)
                gi = h // 4
                if pn == 0 and ti == 0:
                    t.wait_ge(P.sem(ldk[gi % 2]), 32 * (gi // 2 + 1))
                    t.wait_ge(P.sem(ldqh[h % 2]), 16 * (h // 2 + 1))
                if p >= NPS:
                    t.wait_ge(P.sem(expd), p // 2 - 1)
                if ti >= 4 * pn:
                    t.matmul(pS[:, p % NPS, :], C["ident_bf"][:],
                             C["cmadd"][:, ti - 4 * pn, :], start=True, stop=False)
                    P.inc(smm, t.matmul(pS[:, p % NPS, :],
                                        kg[:, gi % 2, ti * 128:(ti + 1) * 128],
                                        qh[:, h % 2, pn * PN:(pn + 1) * PN],
                                        start=False, stop=True))
                else:
                    P.inc(smm, t.matmul(pS[:, p % NPS, :],
                                        kg[:, gi % 2, ti * 128:(ti + 1) * 128],
                                        qh[:, h % 2, pn * PN:(pn + 1) * PN],
                                        start=True, stop=True))
            def ctx_mm(p):
                u, ti = tiles[p]
                h, pn = upn(u)
                gi = h // 4
                ntk = untk(u)
                t.wait_ge(P.sem(expd), p // 2 + 1)
                if ti == 0 and u >= 2:
                    t.wait_ge(P.sem(ctxd), u - 1)
                P.inc(cmm, t.matmul(pctx[u % 2][:], vg[:, gi % 2, ti, :],
                                    exps[:, p % NES, :],
                                    start=(ti == 0), stop=(ti == ntk - 1)))
            def tail_pe(w):
                t.wait_ge(P.sem(dad), s_base[w + 1])
                if w >= 2:
                    t.wait_ge(P.sem(rrd), w - 1)
                P.inc(dsm, t.matmul(prow[0:1, w % 2, :], C["ones_f32"][:],
                                    dacc[:, w % 2, :], start=True, stop=True))
            for p in range(NT):
                s_mm(p)
                if p >= L:
                    ctx_mm(p - L)
                u, ti = tiles[p]
                if u >= 1 and ti == min(4, untk(u) - 1):
                    tail_pe(u - 1)
            for p in range(NT - L, NT):
                ctx_mm(p)
            tail_pe(NU - 1)
        blk.tensor(t_body)

        def sc_body(s):
            def rrow_chain(w):
                s.wait_ge(P.sem(dsm), w + 1)
                s.activation(rrow[0:1, w % 2, :], prow[0:1, w % 2, :], AF.Ln)
                P.inc(rrd, s.activation(rrow[0:1, w % 2, :],
                                        rrow[0:1, w % 2, :], AF.Exp, scale=-1.0))
            for j in range(NT // 2):
                p = 2 * j
                s.wait_ge(P.sem(smm), p + 2)
                if p >= NES:
                    s.wait_ge(P.sem(cmm), p + 2 - NES)
                    s.wait_ge(P.sem(dad), p + 2 - NES)
                P.inc(expd, s.activation(exps[:, p % NES:p % NES + 2, :],
                                         pS[:, p % NPS:p % NPS + 2, :], AF.Exp))
                u, ti = tiles[p + 1]
                if u >= 1 and ti == min(5, untk(u) - 1):
                    rrow_chain(u - 1)
            rrow_chain(NU - 1)
        blk.scalar(sc_body)

        def v_body(v):
            def ctx_norm(w):
                v.wait_ge(P.sem(cmm), s_base[w + 1])
                v.wait_ge(P.sem(rbxq[w % 2]), 16 * (w // 2 + 1))
                if w >= 2:
                    v.wait_ge(P.sem(cstq[w % 2]), 16 * (w // 2))
                P.inc(ctxd, v.tensor_tensor(cstg[:, w % 2, :], pctx[w % 2][:],
                                            rbx[:, w % 2, :], ALU.mult))
            for p in range(NT):
                u, ti = tiles[p]
                ntk = untk(u)
                v.wait_ge(P.sem(expd), p // 2 + 1)
                if ti == 0:
                    if u >= 2:
                        v.wait_ge(P.sem(dsm), u - 1)
                    P.inc(dad, v.tensor_copy(dacc[:, u % 2, :],
                                             exps[:, p % NES, :]))
                else:
                    P.inc(dad, v.tensor_tensor(dacc[:, u % 2, :],
                                               dacc[:, u % 2, :],
                                               exps[:, p % NES, :], ALU.add))
                if u >= 1 and ti == min(10, ntk - 1):
                    ctx_norm(u - 1)
            ctx_norm(NU - 1)
        blk.vector(v_body)

        def s_out(sy):
            for u in range(NU):
                h, pn = upn(u)
                sy.wait_ge(P.sem(rrd), u + 1)
                if u >= 2:
                    sy.wait_ge(P.sem(rbxq[u % 2]), 16 * (u // 2))
                P.inc(rstq[u % 2], sy.dma_start(
                    var_ap[0:1, (u % 2) * PN:(u % 2 + 1) * PN],
                    rrow[0:1, u % 2, :]), 16)
                if u >= 1:
                    w = u - 1
                    hw, pw = upn(w)
                    sy.wait_ge(P.sem(ctxd), w + 1)
                    P.inc(cstq[w % 2], sy.dma_start(
                        ctx_ap[hw * 128:(hw + 1) * 128, pw * PN:(pw + 1) * PN],
                        cstg[:, w % 2, :]), 16)
            w = NU - 1
            hw, pw = upn(w)
            sy.wait_ge(P.sem(ctxd), w + 1)
            P.inc(cstq[w % 2], sy.dma_start(
                ctx_ap[hw * 128:(hw + 1) * 128, pw * PN:(pw + 1) * PN],
                cstg[:, w % 2, :]), 16)
            for q_ in (cstq[0], cstq[1], rstq[0], rstq[1]):
                sy.wait_ge(P.sem(q_), P.n(q_))
        blk.sync(s_out)


def build_l1attn(xsrc_name, xdst_name):
    P = Prog("l1attn")
    nc = P.nc
    dram = decl_consts(P)
    x_in = nc.dram_tensor(xsrc_name, [D, T], F32, kind="ExternalInput").ap()
    wqa = nc.dram_tensor("Wq", [D, D], F32, kind="ExternalInput").ap()
    wka = nc.dram_tensor("Wk", [D, 512], F32, kind="ExternalInput").ap()
    wva = nc.dram_tensor("Wv", [D, 512], F32, kind="ExternalInput").ap()
    woa = nc.dram_tensor("Wo", [D, D], F32, kind="ExternalInput").ap()
    bo2d = nc.dram_tensor("bo2d", [128, 16], F32, kind="ExternalInput").ap()
    x_out = nc.dram_tensor(xdst_name, [D, T], F32, kind="ExternalOutput").ap()
    xln = nc.dram_tensor("xln", [D, T], BF16, kind="Internal").ap()
    ctx = nc.dram_tensor("ctxbuf", [D, T], BF16, kind="Internal").ap()
    qT = nc.dram_tensor("qTbuf", [D, T], BF16, kind="Internal").ap()
    kv = nc.dram_tensor("kvbuf", [1024, T], BF16, kind="Internal").ap()
    varb = nc.dram_tensor("varbuf", [1, T], F32, kind="Internal").ap()
    emit_consts(P, dram)
    P.open_scope()
    emit_ln(P, "ln", x_in, xln)
    emit_barrier(P, "b1")
    P.close_scope()
    P.open_scope()
    emit_gqa_proj(P, "gp", xln, wqa, wka, wva, qT, kv)
    emit_barrier(P, "b2a")
    P.close_scope()
    P.open_scope()
    emit_gqa_attn(P, "ga", qT, kv, ctx, varb)
    emit_barrier(P, "b2")
    P.close_scope()
    P.open_scope()
    emit_matmul_residual(P, "wo2", ctx, woa, x_in, x_out, D, alpha=1.0,
                         bias_ap=bo2d)
    emit_barrier(P, "b3")
    P.close_scope()
    P.close()
    ins = CONST_NAMES + [xsrc_name, "Wq", "Wk", "Wv", "Wo", "bo2d"]
    return P.nc, ins, [xdst_name]


def build_full(tile_blocks):
    P = Prog("full")
    nc = P.nc
    dram = decl_consts(P)
    x_in = nc.dram_tensor("x_in", [D, T], F32, kind="ExternalInput").ap()
    pk_gu = nc.dram_tensor("pk_gu", [256, GU_OUT], I32, kind="ExternalInput").ap()
    sc_gu = nc.dram_tensor("sc_gu", [16, GU_OUT], F16, kind="ExternalInput").ap()
    pk_dn = nc.dram_tensor("pk_dn", [704, D], I32, kind="ExternalInput").ap()
    sc_dn = nc.dram_tensor("sc_dn", [44, D], F16, kind="ExternalInput").ap()
    wqkv = nc.dram_tensor("wqkvT", [D, 3 * D], F32, kind="ExternalInput").ap()
    slope = nc.dram_tensor("slope_row", [1, 16], F32, kind="ExternalInput").ap()
    wgate = nc.dram_tensor("wgateT", [D, D], F32, kind="ExternalInput").ap()
    wout = nc.dram_tensor("woutT", [D, D], F32, kind="ExternalInput").ap()
    normw = nc.dram_tensor("normw2d", [128, 16], F32, kind="ExternalInput").ap()
    gwT = nc.dram_tensor("gate_wT", [D, 8], F32, kind="ExternalInput").ap()
    WT = nc.dram_tensor("WT", [D, D], F32, kind="ExternalInput").ap()
    eaw = nc.dram_tensor("eaw", [8, 64], F32, kind="ExternalInput").ap()
    atoms = nc.dram_tensor("atoms_s", [64, NS], F32, kind="ExternalInput").ap()
    imp = nc.dram_tensor("imp_s", [8, NS], F32, kind="ExternalInput").ap()
    midx = nc.dram_tensor("midx2d", [128, 128], I32, kind="ExternalInput").ap()
    wqa = nc.dram_tensor("Wq", [D, D], F32, kind="ExternalInput").ap()
    wka = nc.dram_tensor("Wk", [D, 512], F32, kind="ExternalInput").ap()
    wva = nc.dram_tensor("Wv", [D, 512], F32, kind="ExternalInput").ap()
    woa = nc.dram_tensor("Wo", [D, D], F32, kind="ExternalInput").ap()
    bo2d = nc.dram_tensor("bo2d", [128, 16], F32, kind="ExternalInput").ap()
    x_out = nc.dram_tensor("x_out", [D, T], F32, kind="ExternalOutput").ap()
    wgu = nc.dram_tensor("wgu", [D, GU_OUT], BF16, kind="Internal").ap()
    wdn = nc.dram_tensor("wdn", [I_FF, D], BF16, kind="Internal").ap()
    xln = nc.dram_tensor("xln", [D, T], BF16, kind="Internal").ap()
    hbuf = nc.dram_tensor("hbuf", [I_FF, T], BF16, kind="Internal").ap()
    kv = nc.dram_tensor("kvbuf", [1024, T], BF16, kind="Internal").ap()
    contrib = nc.dram_tensor("contrib", [D, T], BF16, kind="Internal").ap()
    varb = nc.dram_tensor("varbuf", [1, T], F32, kind="Internal").ap()
    xA = nc.dram_tensor("xA", [D, T], F32, kind="Internal").ap()
    xB = nc.dram_tensor("xB", [D, T], F32, kind="Internal").ap()
    qkvb = nc.dram_tensor("qkvbuf", [3 * D, T], BF16, kind="Internal").ap()
    outh = hbuf[0:D, :]
    zbuf = hbuf[D:2 * D, :]
    ctx = hbuf[0:D, :]
    qT = hbuf[D:2 * D, :]

    emit_consts(P, dram)

    def scoped(fn, *args, **kw):
        P.open_scope()
        fn(P, *args, **kw)
        P.close_scope(recycle=True)

    scoped(emit_ln, "ln1", x_in, xln)
    scoped(emit_gateup_dq, "fa1", xln, pk_gu, sc_gu, pk_dn, sc_dn, wgu, wdn,
           hbuf)
    scoped(emit_l0ffn_down, "fb1", hbuf, wdn, x_in, xA)
    scoped(emit_ln, "ln2", xA, xln)
    scoped(emit_lightning_proj, "lp2", xln, wqkv, qkvb)
    scoped(emit_lightning_scan, "ls2", qkvb, slope, outh)
    scoped(emit_lightning_out, "lo2", xln, outh, varb, wgate, normw, zbuf)
    scoped(emit_matmul_residual, "wo2", zbuf, wout, xA, xB, D, alpha=1.0)
    scoped(emit_ln, "ln3", xB, xln)
    scoped(emit_l0ffn_gateup, "fa3", xln, wgu, hbuf)
    scoped(emit_l0ffn_down, "fb3", hbuf, wdn, xB, xA)
    scoped(emit_ln, "ln4", xA, xln)
    scoped(emit_l1ffn_core, "lf4", xln, gwT, eaw, atoms, imp, midx, contrib,
           tile_blocks)
    scoped(emit_matmul_residual, "wl4", xln, WT, xA, xB, D, alpha=0.5,
           contrib_ap=contrib)
    scoped(emit_ln, "ln5", xB, xln)
    scoped(emit_gqa_proj, "gp5", xln, wqa, wka, wva, qT, kv)
    scoped(emit_gqa_attn, "ga5", qT, kv, ctx, varb)
    scoped(emit_matmul_residual, "wo5", ctx, woa, xB, xA, D, alpha=1.0,
           bias_ap=bo2d)
    scoped(emit_ln, "ln6", xA, xln)
    scoped(emit_l1ffn_core, "lf6", xln, gwT, eaw, atoms, imp, midx, contrib,
           tile_blocks)
    scoped(emit_matmul_residual, "wl6", xln, WT, xA, x_out, D, alpha=0.5,
           contrib_ap=contrib)
    P.close()
    ins = CONST_NAMES + ["x_in", "pk_gu", "sc_gu", "pk_dn", "sc_dn", "wqkvT",
                         "slope_row", "wgateT", "woutT", "normw2d", "gate_wT",
                         "WT", "eaw", "atoms_s", "imp_s", "midx2d", "Wq", "Wk",
                         "Wv", "Wo", "bo2d"]
    return P.nc, ins, ["x_out"]


def kernel_fused(**inputs):
    shared, percore = host_prep(inputs)
    tile_blocks = shared.pop("_tile_blocks")
    state = [dict(shared, **percore[c]) for c in range(8)]
    prog = build_full(tile_blocks)
    run_launch(prog, state)
    return np.stack([np.asarray(state[c]["x_out"], dtype=np.float32).T
                     for c in range(8)])


# version 14
# speedup vs baseline: 1.0576x; 1.0576x over previous
import contextlib
import numpy as np
import ml_dtypes
import concourse.bass as bass
import concourse.mybir as mybir
from concourse.bass_utils import run_bass_kernel_spmd
from concourse.bass_core_rust import IndirectOffsetOnAxis

F32 = mybir.dt.float32
BF16 = mybir.dt.bfloat16
F16 = mybir.dt.float16
I32 = mybir.dt.int32
U16 = mybir.dt.uint16
ALU = mybir.AluOpType
AF = mybir.ActivationFunctionType
AX = mybir.AxisListType.X

D = 2048
T = 4096
DT = D // 128
PN = 512
NPAN = T // PN
I_FF = 5632
IT_FF = I_FF // 128
GU_OUT = 2 * I_FF
EPS_LN = 1e-5
EPS_RMS = 1e-6
NH = 16
HD = 128
BLK = 256
NBLK = T // BLK
NS = 16384
NE = 8
NA = 64

bf = ml_dtypes.bfloat16
FCONSTS = [EPS_LN, EPS_RMS, -0.5, 0.5, 1.0, 0.0, 2.0, -1.0, 1.0 / D, -256.0,
           0.08838834764831845, 1e-30, 3.0, 4.0, 5.0, 6.0]


class Prog:

    def __init__(self, name):
        self.nc = bass.Bass("TRN2", target_bir_lowering=False)
        self.nc.detect_race_conditions = False
        self.sems = {}
        self.cnt = {}
        self.consts = {}
        self.stack = contextlib.ExitStack()
        self.scopes = []

    def sem(self, name):
        if name not in self.sems:
            self.sems[name] = self.nc.alloc_semaphore(name)
            self.cnt[name] = 0
        return self.sems[name]

    def inc(self, name, instr, k=1):
        instr.then_inc(self.sem(name), k)
        self.cnt[name] += k
        return self.cnt[name]

    def n(self, name):
        self.sem(name)
        return self.cnt[name]

    def sbuf(self, name, shape, dtype):
        st = self.scopes[-1] if self.scopes else self.stack
        return st.enter_context(self.nc.sbuf_tensor(name, shape, dtype))

    def psum(self, name, shape, dtype=F32):
        st = self.scopes[-1] if self.scopes else self.stack
        return st.enter_context(self.nc.psum_tensor(name, shape, dtype))

    def open_scope(self):
        self.scopes.append(contextlib.ExitStack())
        self.scope_sems = getattr(self, "scope_sems", [])
        self.scope_sems.append(set(self.sems.keys()))

    def close_scope(self, recycle=False):
        self.scopes.pop().close()
        base = self.scope_sems.pop()
        if recycle:
            new = [n for n in self.sems if n not in base and n != "bar"]
            emit_barrier(self, "scx")
            self.nc.clear_and_free_semaphores([self.sems[n] for n in new])
            emit_barrier(self, "scy")
            for n in new:
                del self.sems[n]
                del self.cnt[n]

    def close(self):
        while self.scopes:
            self.close_scope()
        self.stack.close()


def emit_barrier(P, tag):
    nc = P.nc
    s = P.sem("bar")
    base = P.cnt["bar"]
    target = base + 5
    P.cnt["bar"] = target
    with nc.Block() as blk:
        for eng_name in ("sync", "tensor", "vector", "scalar", "gpsimd"):
            def body(eng):
                try:
                    eng.drain()
                except Exception:
                    pass
                eng.nop().then_inc(s, 1)
                eng.wait_ge(s, target)
            getattr(blk, eng_name)(body)


def emit_consts(P, dram):
    nc = P.nc
    C = {}
    C["ones_bf"] = P.stack.enter_context(nc.sbuf_tensor("s_ones_bf", [128, 128], BF16))
    C["ident_bf"] = P.stack.enter_context(nc.sbuf_tensor("s_ident_bf", [128, 128], BF16))
    C["ident_f32"] = P.stack.enter_context(nc.sbuf_tensor("s_ident_f32", [128, 128], F32))
    C["ones1_bf"] = P.stack.enter_context(nc.sbuf_tensor("s_ones1_bf", [1, 128], BF16))
    C["iota128"] = P.stack.enter_context(nc.sbuf_tensor("s_iota128", [128, 128], F32))
    C["arr256"] = P.stack.enter_context(nc.sbuf_tensor("s_arr256", [1, 256], F32))
    C["rarr256"] = P.stack.enter_context(nc.sbuf_tensor("s_rarr256", [1, 256], F32))
    C["idxhi"] = P.stack.enter_context(nc.sbuf_tensor("s_idxhi", [128, 256], F32))
    C["idxlo"] = P.stack.enter_context(nc.sbuf_tensor("s_idxlo", [128, 256], F32))
    C["mskhi"] = P.stack.enter_context(nc.sbuf_tensor("s_mskhi", [128, 256], F32))
    C["msklo"] = P.stack.enter_context(nc.sbuf_tensor("s_msklo", [128, 256], F32))
    C["iota8"] = P.stack.enter_context(nc.sbuf_tensor("s_iota8", [128, 8], F32))
    C["cmadd"] = P.stack.enter_context(nc.sbuf_tensor("s_cmadd", [128, 4, 512], BF16))
    C["ones_f32"] = P.stack.enter_context(nc.sbuf_tensor("s_ones_f32", [128, 1], F32))
    C["fconst"] = P.stack.enter_context(nc.sbuf_tensor("s_fconst", [128, len(FCONSTS)], F32))
    P.consts = C
    for i, v in enumerate(FCONSTS):
        nc.const_aps.register(F32, v, C["fconst"][:, i:i + 1])
    with P.nc.Block() as blk:
        def body(g):
            n0 = P.n("cst")
            for key in ("ones_bf", "ident_bf", "ident_f32", "ones1_bf", "iota128",
                        "arr256", "rarr256", "idxhi", "idxlo", "mskhi", "msklo", "iota8",
                        "cmadd", "ones_f32", "fconst"):
                P.inc("cst", g.dma_start(C[key][:], dram["c_" + key][:]), 16)
            g.wait_ge(P.sem("cst"), P.n("cst"))
        blk.gpsimd(body)
    emit_barrier(P, "consts")


def host_consts():
    out = {}
    out["c_ones_bf"] = np.ones((128, 128), dtype=bf)
    out["c_ident_bf"] = np.eye(128, dtype=bf)
    out["c_ident_f32"] = np.eye(128, dtype=np.float32)
    out["c_ones1_bf"] = np.ones((1, 128), dtype=bf)
    out["c_iota128"] = np.broadcast_to(np.arange(128, dtype=np.float32), (128, 128)).copy()
    arr = np.arange(1, 257, dtype=np.float32)
    out["c_arr256"] = arr.reshape(1, 256).copy()
    out["c_rarr256"] = (256.0 - arr).reshape(1, 256).copy()
    m = np.arange(256, dtype=np.float32)[None, :]
    nhi = np.arange(128, dtype=np.float32)[:, None]
    nlo = nhi + 128
    out["c_idxhi"] = (m - nhi).astype(np.float32)
    out["c_idxlo"] = (m - nlo).astype(np.float32)
    out["c_mskhi"] = (m >= nhi).astype(np.float32)
    out["c_msklo"] = (m >= nlo).astype(np.float32)
    out["c_iota8"] = np.broadcast_to(np.arange(8, dtype=np.float32), (128, 8)).copy()
    r_ = np.arange(128)[:, None]
    c_ = np.arange(512)[None, :]
    cm = np.where(np.stack([(c_ >= j * 128 + r_) for j in range(4)], axis=1),
                  0.0, -10000.0).astype(bf)
    out["c_cmadd"] = np.ascontiguousarray(cm)
    out["c_ones_f32"] = np.ones((128, 1), dtype=np.float32)
    out["c_fconst"] = np.broadcast_to(np.array(FCONSTS, dtype=np.float32), (128, len(FCONSTS))).copy()
    return out




def emit_dequant(P, tag, pk_ap, sc_ap, w_ap, IN, OUT, fslice=512):
    nc = P.nc
    IPT = IN // 8
    n_ip = (IPT + 127) // 128
    n_f = OUT // fslice
    sb = P.sbuf
    pks = [sb(f"{tag}_pk{b}", [128, fslice], I32) for b in range(2)]
    scs = [sb(f"{tag}_sc{b}", [128, fslice], F32) for b in range(2)]
    wt = [sb(f"{tag}_wt{b}", [128, 8, fslice], BF16) for b in range(2)]
    nib = sb(f"{tag}_nib", [128, fslice], I32)
    mm_ = sb(f"{tag}_m", [128, fslice], I32)
    t2 = sb(f"{tag}_t2", [128, fslice], I32)
    t3 = sb(f"{tag}_t3", [128, fslice], I32)
    cnd = sb(f"{tag}_cnd", [128, fslice], I32)
    bits = sb(f"{tag}_bits", [128, fslice], I32)
    sgn = sb(f"{tag}_sgn", [128, fslice], I32)
    u16 = sb(f"{tag}_u16", [128, fslice], U16)

    tiles = []
    for it in range(n_ip):
        prt = min(128, IPT - it * 128)
        for fi in range(n_f):
            tiles.append((it, fi, prt))
    NTL = len(tiles)

    ld = [f"{tag}_ld0", f"{tag}_ld1"]
    stq = [f"{tag}_st0", f"{tag}_st1"]
    vv = f"{tag}_vv"
    ld_tot = [32 * (i // 2 + 1) for i in range(NTL)]
    st_tot = [8 * 16 * (i // 2 + 1) for i in range(NTL)]

    with nc.Block() as blk:
        def g_body(g):
            for i, (it, fi, prt) in enumerate(tiles):
                b = i % 2
                if i >= 2:
                    g.wait_ge(P.sem(stq[b]), st_tot[i - 2])
                P.inc(ld[b], g.dma_start(
                    pks[b][0:prt, :], pk_ap[it * 128:it * 128 + prt,
                                            fi * fslice:(fi + 1) * fslice]), 16)
                g0 = (it * 128) // 16
                ngr = (prt + 15) // 16
                src = bass.AP(sc_ap.tensor, sc_ap.offset + g0 * OUT + fi * fslice,
                              [[OUT, ngr], [0, 16], [1, fslice]])
                P.inc(ld[b], g.dma_start(scs[b][0:prt, :], src), 16)
                assert P.n(ld[b]) == ld_tot[i]
        blk.gpsimd(g_body)

        def v_body(v):
            for i, (it, fi, prt) in enumerate(tiles):
                b = i % 2
                v.wait_ge(P.sem(ld[b]), ld_tot[i])
                if i >= 2:
                    v.wait_ge(P.sem(stq[b]), st_tot[i - 2])
                pt = pks[b][0:prt, :]
                for j in range(8):
                    v.tensor_scalar(nib[0:prt, :], pt, 4 * j, 15,
                                    ALU.logical_shift_right, ALU.bitwise_and)
                    v.tensor_single_scalar(mm_[0:prt, :], nib[0:prt, :], 7, ALU.bitwise_and)
                    v.tensor_scalar(t2[0:prt, :], mm_[0:prt, :], 64, 0x3F00, ALU.mult, ALU.add)
                    v.tensor_single_scalar(t3[0:prt, :], mm_[0:prt, :], 0x3F00, ALU.mult)
                    v.tensor_single_scalar(cnd[0:prt, :], mm_[0:prt, :], 2, ALU.is_lt)
                    v.select(bits[0:prt, :], cnd[0:prt, :], t3[0:prt, :], t2[0:prt, :])
                    v.tensor_scalar(sgn[0:prt, :], nib[0:prt, :], 8, 12,
                                    ALU.bitwise_and, ALU.logical_shift_left)
                    v.tensor_tensor(bits[0:prt, :], bits[0:prt, :], sgn[0:prt, :], ALU.add)
                    v.tensor_copy(u16[0:prt, :], bits[0:prt, :])
                    P.inc(vv, v.tensor_tensor(wt[b][0:prt, j, :],
                                              u16[0:prt, :].bitcast(BF16),
                                              scs[b][0:prt, :], ALU.mult))
        blk.vector(v_body)

        def s_body(sy):
            k = 0
            for i, (it, fi, prt) in enumerate(tiles):
                b = i % 2
                k += 8
                sy.wait_ge(P.sem(vv), k)
                for j in range(8):
                    dst = bass.AP(w_ap.tensor,
                                  w_ap.offset + (it * 1024 + j) * OUT + fi * fslice,
                                  [[8 * OUT, prt], [1, fslice]])
                    P.inc(stq[b], sy.dma_start(dst, wt[b][0:prt, j, :]), 16)
                assert P.n(stq[b]) == st_tot[i]
            sy.wait_ge(P.sem(stq[0]), P.n(stq[0]))
            sy.wait_ge(P.sem(stq[1]), P.n(stq[1]))
        blk.sync(s_body)


def emit_ln(P, tag, x_ap, xln_ap):
    nc = P.nc
    sb = P.sbuf
    C = P.consts
    xp = [sb(f"{tag}_xp{b}", [128, DT, PN], BF16) for b in range(2)]
    sq = [sb(f"{tag}_sq{b}", [128, DT, PN], BF16) for b in range(2)]
    ms = sb(f"{tag}_ms", [128, PN], F32)
    vq = sb(f"{tag}_vq", [128, PN], F32)
    var = sb(f"{tag}_var", [128, PN], F32)
    rstd = sb(f"{tag}_rstd", [128, PN], F32)
    mr = sb(f"{tag}_mr", [128, PN], F32)
    tacc = sb(f"{tag}_tacc", [128, PN], F32)
    xst = [sb(f"{tag}_xst{b}", [128, DT, PN], BF16) for b in range(2)]
    pss = [P.psum(f"{tag}_pss{b}", [128, PN], F32) for b in range(2)]
    psq = [P.psum(f"{tag}_psq{b}", [128, PN], F32) for b in range(2)]

    ld = [f"{tag}_ld0", f"{tag}_ld1"]
    od = [f"{tag}_od0", f"{tag}_od1"]
    sqd, smd, smv, rsd, appd, pfree = (
        f"{tag}_sq", f"{tag}_sm", f"{tag}_smv", f"{tag}_rs",
        f"{tag}_ap", f"{tag}_pf")
    ld_tot = [DT * 16 * (p // 2 + 1) for p in range(NPAN)]
    od_tot = [DT * 16 * (p // 2 + 1) for p in range(NPAN)]

    with nc.Block() as blk:
        def g_body(g):
            for p in range(NPAN):
                b = p % 2
                if p >= 2:
                    g.wait_ge(P.sem(appd), p - 1)
                for k in range(DT):
                    P.inc(ld[b], g.dma_start(xp[b][:, k, :],
                                             x_ap[k * 128:(k + 1) * 128,
                                                  p * PN:(p + 1) * PN]), 16)
                assert P.n(ld[b]) == ld_tot[p]
        blk.gpsimd(g_body)

        def sc_body(s):
            for p in range(NPAN):
                b = p % 2
                if p >= 2:
                    s.wait_ge(P.sem(smd), (p - 1) * 2)
                s.wait_ge(P.sem(ld[b]), ld_tot[p])
                for k in range(DT):
                    P.inc(sqd, s.activation(sq[b][:, k, :], xp[b][:, k, :],
                                            AF.Square))
                s.wait_ge(P.sem(smv), p + 1)
                s.activation(var[:], var[:], AF.Ln, bias=EPS_LN)
                P.inc(rsd, s.activation(rstd[:], var[:], AF.Exp, scale=-0.5))
        blk.scalar(sc_body)

        def v_body(v):
            for p in range(NPAN):
                b = p % 2
                v.wait_ge(P.sem(smd), p * 2 + 2)
                v.tensor_single_scalar(ms[:], pss[b][:], 1.0 / D, ALU.mult)
                P.inc(pfree, v.tensor_single_scalar(vq[:], psq[b][:], 1.0 / D, ALU.mult))
                v.tensor_tensor(tacc[:], ms[:], ms[:], ALU.mult)
                P.inc(smv, v.tensor_tensor(var[:], vq[:], tacc[:], ALU.subtract))
                v.wait_ge(P.sem(rsd), p + 1)
                if p >= 2:
                    v.wait_ge(P.sem(od[b]), od_tot[p - 2])
                v.tensor_tensor(mr[:], ms[:], rstd[:], ALU.mult)
                for k in range(DT):
                    v.tensor_tensor(tacc[:], xp[b][:, k, :], rstd[:], ALU.mult)
                    last = v.tensor_tensor(xst[b][:, k, :], tacc[:], mr[:], ALU.subtract)
                P.inc(appd, last)
        blk.vector(v_body)

        def t_body(t):
            for p in range(NPAN):
                b = p % 2
                if p >= 2:
                    t.wait_ge(P.sem(pfree), p - 1)
                t.wait_ge(P.sem(ld[b]), ld_tot[p])
                t.wait_ge(P.sem(sqd), (p + 1) * DT)
                for k in range(DT):
                    t.matmul(pss[b][:], C["ones_bf"][:], xp[b][:, k, :],
                             start=(k == 0), stop=(k == DT - 1))
                    last = t.matmul(psq[b][:], C["ones_bf"][:], sq[b][:, k, :],
                                    start=(k == 0), stop=(k == DT - 1))
                P.inc(smd, last, 2)
        blk.tensor(t_body)

        def s_out(sy):
            for p in range(NPAN):
                b = p % 2
                sy.wait_ge(P.sem(appd), p + 1)
                for k in range(DT):
                    P.inc(od[b], sy.dma_start(xln_ap[k * 128:(k + 1) * 128,
                                                     p * PN:(p + 1) * PN],
                                              xst[b][:, k, :]), 16)
                assert P.n(od[b]) == od_tot[p]
            sy.wait_ge(P.sem(od[0]), P.n(od[0]))
            sy.wait_ge(P.sem(od[1]), P.n(od[1]))
        blk.sync(s_out)


def emit_l0ffn_gateup(P, tag, xln_ap, wgu_ap, h_ap, wgu_tiled=True):
    nc = P.nc
    sb = P.sbuf
    xr = sb(f"{tag}_xr", [128, DT, T], BF16)
    wg = [sb(f"{tag}_wg{b}", [128, DT, 256], BF16) for b in range(2)]
    wu = [sb(f"{tag}_wu{b}", [128, DT, 256], BF16) for b in range(2)]
    sg = sb(f"{tag}_sg", [128, 2, PN], F32)
    ht = sb(f"{tag}_ht", [128, 2, PN], BF16)
    psg = [P.psum(f"{tag}_psg{b}", [128, PN], F32) for b in range(2)]
    psu = [P.psum(f"{tag}_psu{b}", [128, PN], F32) for b in range(2)]

    ldx = f"{tag}_ldx"
    ldw = [f"{tag}_ldw0", f"{tag}_ldw1"]
    hst = [f"{tag}_hs0", f"{tag}_hs1"]
    mmg, mmu, sgd, hdone, wfree, pfree = (
        f"{tag}_mmg", f"{tag}_mmu", f"{tag}_sgd", f"{tag}_hd", f"{tag}_wf",
        f"{tag}_pf")

    NI2 = IT_FF // 2
    NIDX = IT_FF * NPAN
    nwl = 2 if wgu_tiled else 2 * DT
    ldw_tot = [nwl * 16 * (s2 // 2 + 1) for s2 in range(NI2)]
    hst_tot = [16 * (i // 2 + 1) for i in range(NIDX)]

    with nc.Block() as blk:
        def g_body(g):
            for k in range(DT):
                P.inc(ldx, g.dma_start(xr[:, k, :], xln_ap[k * 128:(k + 1) * 128, :]), 16)
            for s2 in range(NI2):
                b = s2 % 2
                if s2 >= 2:
                    g.wait_ge(P.sem(mmu), (s2 - 1) * 2 * NPAN)
                if wgu_tiled:
                    P.inc(ldw[b], g.dma_start(wg[b][:, :, :], bass.AP(
                        wgu_ap.tensor, wgu_ap.offset + s2 * (128 * DT * 256),
                        [[DT * 256, 128], [1, DT * 256]])), 16)
                    P.inc(ldw[b], g.dma_start(wu[b][:, :, :], bass.AP(
                        wgu_ap.tensor, wgu_ap.offset + (22 + s2) * (128 * DT * 256),
                        [[DT * 256, 128], [1, DT * 256]])), 16)
                else:
                    for k in range(DT):
                        P.inc(ldw[b], g.dma_start(
                            wg[b][:, k, :],
                            wgu_ap[k * 128:(k + 1) * 128,
                                   s2 * 256:(s2 + 1) * 256]), 16)
                        P.inc(ldw[b], g.dma_start(
                            wu[b][:, k, :],
                            wgu_ap[k * 128:(k + 1) * 128,
                                   I_FF + s2 * 256:I_FF + (s2 + 1) * 256]), 16)
                assert P.n(ldw[b]) == ldw_tot[s2]
        blk.gpsimd(g_body)

        def t_body(t):
            t.wait_ge(P.sem(ldx), DT * 16)
            for s2 in range(NI2):
                b = s2 % 2
                t.wait_ge(P.sem(ldw[b]), ldw_tot[s2])
                for ii in range(2):
                    for p in range(NPAN):
                        idx = (s2 * 2 + ii) * NPAN + p
                        pb = idx % 2
                        if idx >= 2:
                            t.wait_ge(P.sem(hdone), idx - 1)
                        for k in range(DT):
                            last = t.matmul(psg[pb][:], wg[b][:, k, ii * 128:(ii + 1) * 128],
                                            xr[:, k, p * PN:(p + 1) * PN],
                                            start=(k == 0), stop=(k == DT - 1))
                        P.inc(mmg, last)
                        for k in range(DT):
                            last = t.matmul(psu[pb][:], wu[b][:, k, ii * 128:(ii + 1) * 128],
                                            xr[:, k, p * PN:(p + 1) * PN],
                                            start=(k == 0), stop=(k == DT - 1))
                        P.inc(mmu, last)
        blk.tensor(t_body)

        def sc_body(s):
            for idx in range(NIDX):
                pb = idx % 2
                if idx >= 2:
                    s.wait_ge(P.sem(hdone), idx - 1)
                s.wait_ge(P.sem(mmg), idx + 1)
                P.inc(sgd, s.activation(sg[:, pb, :], psg[pb][:], AF.Silu))
        blk.scalar(sc_body)

        def v_body(v):
            for idx in range(NIDX):
                pb = idx % 2
                if idx >= 2:
                    v.wait_ge(P.sem(hst[pb]), hst_tot[idx - 2])
                v.wait_ge(P.sem(sgd), idx + 1)
                v.wait_ge(P.sem(mmu), idx + 1)
                P.inc(hdone, v.tensor_tensor(ht[:, pb, :], sg[:, pb, :], psu[pb][:],
                                             ALU.mult))
        blk.vector(v_body)

        def s_out(sy):
            for idx in range(NIDX):
                i = idx // NPAN
                p = idx % NPAN
                pb = idx % 2
                sy.wait_ge(P.sem(hdone), idx + 1)
                dst = bass.AP(h_ap.tensor, h_ap.offset + p * 128 * IT_FF * PN
                              + i * PN, [[IT_FF * PN, 128], [1, PN]])
                P.inc(hst[pb], sy.dma_start(dst, ht[:, pb, :]), 16)
                assert P.n(hst[pb]) == hst_tot[idx]
            sy.wait_ge(P.sem(hst[0]), P.n(hst[0]))
            sy.wait_ge(P.sem(hst[1]), P.n(hst[1]))
        blk.sync(s_out)


def emit_gateup_dq(P, tag, xln_ap, pkg_ap, scg_ap, pkd_ap, scd_ap, wgu_ap,
                   wdn_ap, h_ap):
    nc = P.nc
    sb = P.sbuf
    C = P.consts
    FS = 512
    tiles = []
    for fp in range(11):
        for fi in (fp, 11 + fp):
            for it in range(2):
                tiles.append(("g", it, fi, 128))
    for it in range(6):
        prt = 128 if it < 5 else 64
        for fi in range(4):
            tiles.append(("d", it, fi, prt))
    NTL = len(tiles)
    NGT = 44

    pks = [sb(f"{tag}_pk{b}", [128, FS], I32) for b in range(2)]
    scs = [sb(f"{tag}_sc{b}", [128, FS], F32) for b in range(2)]
    wt = sb(f"{tag}_wt", [128, 8, FS], BF16)
    nib = sb(f"{tag}_nib", [128, FS], I32)
    mm_ = sb(f"{tag}_m", [128, FS], I32)
    t2 = sb(f"{tag}_t2", [128, FS], I32)
    bits = sb(f"{tag}_bits", [128, FS], I32)
    sgn = sb(f"{tag}_sgn", [128, FS], I32)
    u16 = sb(f"{tag}_u16", [128, FS], U16)

    xr = sb(f"{tag}_xr", [128, DT, T], BF16)
    wg = [sb(f"{tag}_wg{b}", [128, DT, 128], BF16) for b in range(2)]
    wu = [sb(f"{tag}_wu{b}", [128, DT, 128], BF16) for b in range(2)]
    sg = sb(f"{tag}_sg", [128, 2, PN], BF16)
    ht = sb(f"{tag}_ht", [128, 2, PN], BF16)
    psg = [P.psum(f"{tag}_psg{b}", [128, PN], F32) for b in range(2)]
    psu = [P.psum(f"{tag}_psu{b}", [128, PN], F32) for b in range(2)]

    ld = [f"{tag}_ld0", f"{tag}_ld1"]
    ldw = [f"{tag}_lw0", f"{tag}_lw1"]
    hst = [f"{tag}_hs0", f"{tag}_hs1"]
    (stq, vv, ldx, mmg, mmu, sgd, hdone) = [
        f"{tag}_{x}" for x in ("stq", "vv", "ldx", "mmg", "mmu", "sg", "hd")]
    NSL = 44
    NIDX = NSL * NPAN
    hst_tot = [16 * (i // 2 + 1) for i in range(NIDX)]


    ndma = [16 for _ in tiles]
    st_cum = []
    acc = 0
    for x in ndma:
        acc += 16 * x
        st_cum.append(acc)

    with nc.Block() as blk:
        def g_body(g):
            def store_tile(i):
                kind, it, fi, prt = tiles[i]
                g.wait_ge(P.sem(vv), 8 * (i + 1))
                if kind == "g":
                    for j in range(8):
                        for half in range(2):
                            s2 = 2 * fi + half if fi < 11 else \
                                22 + 2 * (fi - 11) + half
                            dst = bass.AP(
                                wgu_ap.tensor,
                                wgu_ap.offset + s2 * (128 * DT * 256)
                                + j * DT * 256 + 8 * it * 256,
                                [[256, 8], [8 * DT * 256, 16], [1, 256]])
                            P.inc(stq, g.dma_start(
                                dst, wt[0:prt, j,
                                        half * 256:(half + 1) * 256]), 16)
                else:
                    for j in range(8):
                        for half in range(2):
                            fp2 = 2 * fi + half
                            dst = bass.AP(
                                wdn_ap.tensor,
                                wdn_ap.offset + fp2 * 128 * IT_FF * 256
                                + j * IT_FF * 256 + 8 * it * 256,
                                [[256, prt // 16], [8 * IT_FF * 256, 16],
                                 [1, 256]])
                            P.inc(stq, g.dma_start(
                                dst, wt[0:prt, j,
                                        half * 256:(half + 1) * 256]), 16)
            for i, (kind, it, fi, prt) in enumerate(tiles):
                b = i % 2
                if i >= 2:
                    g.wait_ge(P.sem(stq), st_cum[i - 2])
                pk_ap = pkg_ap if kind == "g" else pkd_ap
                sc_ap = scg_ap if kind == "g" else scd_ap
                OUT = GU_OUT if kind == "g" else D
                P.inc(ld[b], g.dma_start(
                    pks[b][0:prt, :], pk_ap[it * 128:it * 128 + prt,
                                            fi * FS:(fi + 1) * FS]), 16)
                g0 = (it * 128) // 16
                ngr = (prt + 15) // 16
                src = bass.AP(sc_ap.tensor, sc_ap.offset + g0 * OUT + fi * FS,
                              [[OUT, ngr], [0, 16], [1, FS]])
                P.inc(ld[b], g.dma_start(scs[b][0:prt, :], src), 16)
                if i >= 1:
                    store_tile(i - 1)
            store_tile(NTL - 1)
            g.wait_ge(P.sem(stq), st_cum[-1])
        blk.gpsimd(g_body)

        def s_body(sy):
            for k in range(DT):
                P.inc(ldx, sy.dma_start(xr[:, k, :],
                                        xln_ap[k * 128:(k + 1) * 128, :]), 16)
            for fp in range(11):
                sy.wait_ge(P.sem(stq), st_cum[4 * fp + 3])
                for s2 in range(4 * fp, 4 * fp + 4):
                    b = s2 % 2
                    if s2 >= 2:
                        sy.wait_ge(P.sem(mmu), (s2 - 1) * NPAN)
                    half = s2 % 2
                    P.inc(ldw[b], sy.dma_start(wg[b][:, :, :], bass.AP(
                        wgu_ap.tensor,
                        wgu_ap.offset + (s2 // 2) * (128 * DT * 256) + half * 128,
                        [[DT * 256, 128], [256, DT], [1, 128]])), 16)
                    P.inc(ldw[b], sy.dma_start(wu[b][:, :, :], bass.AP(
                        wgu_ap.tensor,
                        wgu_ap.offset + (22 + s2 // 2) * (128 * DT * 256)
                        + half * 128,
                        [[DT * 256, 128], [256, DT], [1, 128]])), 16)
        blk.sync(s_body)

        def t_body(t):
            t.wait_ge(P.sem(ldx), DT * 16)
            for s2 in range(NSL):
                b = s2 % 2
                t.wait_ge(P.sem(ldw[b]), 2 * 16 * (s2 // 2 + 1))
                for p in range(NPAN):
                    idx = s2 * NPAN + p
                    pb = idx % 2
                    if idx >= 2:
                        t.wait_ge(P.sem(hdone), idx - 1)
                    for k in range(DT):
                        last = t.matmul(psg[pb][:], wg[b][:, k, :],
                                        xr[:, k, p * PN:(p + 1) * PN],
                                        start=(k == 0), stop=(k == DT - 1))
                    P.inc(mmg, last)
                    for k in range(DT):
                        last = t.matmul(psu[pb][:], wu[b][:, k, :],
                                        xr[:, k, p * PN:(p + 1) * PN],
                                        start=(k == 0), stop=(k == DT - 1))
                    P.inc(mmu, last)
        blk.tensor(t_body)

        def sc_body(s):
            def hstore(j):
                pb = j % 2
                s.wait_ge(P.sem(hdone), j + 1)
                i_t = j // NPAN
                p = j % NPAN
                dst = bass.AP(h_ap.tensor, h_ap.offset + p * 128 * IT_FF * PN
                              + i_t * PN, [[IT_FF * PN, 128], [1, PN]])
                P.inc(hst[pb], s.dma_start(dst, ht[:, pb, :]), 16)
            for idx in range(NIDX):
                pb = idx % 2
                if idx >= 2:
                    s.wait_ge(P.sem(hdone), idx - 1)
                s.wait_ge(P.sem(mmg), idx + 1)
                P.inc(sgd, s.activation(sg[:, pb, :], psg[pb][:], AF.Silu))
                if idx >= 1:
                    hstore(idx - 1)
            hstore(NIDX - 1)
            s.wait_ge(P.sem(hst[0]), P.n(hst[0]))
            s.wait_ge(P.sem(hst[1]), P.n(hst[1]))
        blk.scalar(sc_body)

        def v_body(v):
            def plane(pl):
                i, j = pl // 8, pl % 8
                kind, it, fi, prt = tiles[i]
                b = i % 2
                if j == 0:
                    v.wait_ge(P.sem(ld[b]), 32 * (i // 2 + 1))
                    if i >= 1:
                        v.wait_ge(P.sem(stq), st_cum[i - 1])
                pt = pks[b][0:prt, :]
                v.tensor_scalar(nib[0:prt, :], pt, 4 * j, 15,
                                ALU.logical_shift_right, ALU.bitwise_and)
                v.tensor_single_scalar(mm_[0:prt, :], nib[0:prt, :], 7, ALU.bitwise_and)
                v.tensor_scalar(t2[0:prt, :], mm_[0:prt, :], 0x3EC0, 0x3F00,
                                ALU.mult, ALU.min)
                v.scalar_tensor_tensor(bits[0:prt, :], mm_[0:prt, :], 64,
                                       t2[0:prt, :], ALU.mult, ALU.add)
                v.tensor_scalar(sgn[0:prt, :], nib[0:prt, :], 8, 12,
                                ALU.bitwise_and, ALU.logical_shift_left)
                v.tensor_tensor(bits[0:prt, :], bits[0:prt, :], sgn[0:prt, :], ALU.add)
                v.tensor_copy(u16[0:prt, :], bits[0:prt, :])
                P.inc(vv, v.tensor_tensor(wt[0:prt, j, :],
                                          u16[0:prt, :].bitcast(BF16),
                                          scs[b][0:prt, :], ALU.mult))
            pl = 0
            NPL = NTL * 8
            for idx in range(NIDX):
                pace = min(NPL, 32 + (idx * 512) // 352 + 2)
                while pl < pace:
                    plane(pl)
                    pl += 1
                pb = idx % 2
                if idx >= 2:
                    v.wait_ge(P.sem(hst[pb]), hst_tot[idx - 2])
                v.wait_ge(P.sem(sgd), idx + 1)
                v.wait_ge(P.sem(mmu), idx + 1)
                P.inc(hdone, v.tensor_tensor(ht[:, pb, :], sg[:, pb, :],
                                             psu[pb][:], ALU.mult))
            while pl < NPL:
                plane(pl)
                pl += 1
        blk.vector(v_body)


def emit_l0ffn_down(P, tag, h_ap, wdn_ap, x_ap, xout_ap, stage=3,
                    wdn_tiled=True):
    nc = P.nc
    sb = P.sbuf
    C = P.consts
    hp = [sb(f"{tag}_hp{b}", [128, IT_FF, PN], BF16) for b in range(2)]
    wd = [sb(f"{tag}_wd{b}", [128, IT_FF, 256], BF16) for b in range(2)]
    ysb = sb(f"{tag}_ysb", [128, DT, PN], F32)
    ysq = sb(f"{tag}_ysq", [128, DT, PN], BF16)
    rec = sb(f"{tag}_rec", [128, PN], F32)
    tmp = sb(f"{tag}_tmp", [128, PN], F32)
    xt = sb(f"{tag}_xt", [128, 2, PN], F32)
    xo = sb(f"{tag}_xo", [128, 2, PN], F32)
    psd = [P.psum(f"{tag}_psd{b}", [128, PN], F32) for b in range(4)]
    pss = [P.psum(f"{tag}_pss0", [128, PN], F32)]

    ldh = [f"{tag}_lh0", f"{tag}_lh1"]
    ldw = [f"{tag}_lw0", f"{tag}_lw1"]
    ldx = [f"{tag}_lx0", f"{tag}_lx1"]
    od = [f"{tag}_od0", f"{tag}_od1"]
    mmd, sqd, smd, smv, recd, appd, wfr, pfree, hfree = (
        f"{tag}_mmd", f"{tag}_sq", f"{tag}_sm", f"{tag}_smv", f"{tag}_rc",
        f"{tag}_ap", f"{tag}_wfr", f"{tag}_pf", f"{tag}_hf")

    NFP = DT // 2
    NW = NPAN * NFP
    NT = NPAN * DT
    ldh_tot = [16 * (p // 2 + 1) for p in range(NPAN)]
    nwd = 1 if wdn_tiled else IT_FF
    ldw_tot = [nwd * 16 * (ns // 2 + 1) for ns in range(NW)]
    ldx_tot = [16 * (nt // 2 + 1) for nt in range(NT)]
    od_tot = [16 * (nt // 2 + 1) for nt in range(NT)]

    with nc.Block() as blk:
        def g_body(g):
            def xtxo(p):
                for f in range(DT):
                    nt = p * DT + f
                    xb = nt % 2
                    if nt >= 2:
                        g.wait_ge(P.sem(appd), nt - 1)
                    cs = slice(p * PN, (p + 1) * PN)
                    if stage >= 3:
                        P.inc(ldx[xb], g.dma_start(xt[:, xb, :],
                                                   x_ap[f * 128:(f + 1) * 128, cs]), 16)
                    g.wait_ge(P.sem(appd), nt + 1)
                    P.inc(od[xb], g.dma_start(xout_ap[f * 128:(f + 1) * 128, cs],
                                              xo[:, xb, :]), 16)
            for p in range(NPAN):
                b = p % 2
                if p >= 2:
                    g.wait_ge(P.sem(mmd), (p - 1) * NFP)
                src = bass.AP(h_ap.tensor, h_ap.offset + p * 128 * IT_FF * PN,
                              [[IT_FF * PN, 128], [1, IT_FF * PN]])
                P.inc(ldh[b], g.dma_start(hp[b][:, :, :], src), 16)
                assert P.n(ldh[b]) == ldh_tot[p]
                if p >= 1:
                    xtxo(p - 1)
            xtxo(NPAN - 1)
            g.wait_ge(P.sem(od[0]), P.n(od[0]))
            g.wait_ge(P.sem(od[1]), P.n(od[1]))
        blk.gpsimd(g_body)

        def w_body(sy2):
            ns = 0
            for p in range(NPAN):
                for fp_ in range(NFP):
                    b = ns % 2
                    if ns >= 2:
                        sy2.wait_ge(P.sem(mmd), ns - 1)
                    if wdn_tiled:
                        src = bass.AP(wdn_ap.tensor,
                                      wdn_ap.offset + fp_ * 128 * IT_FF * 256,
                                      [[IT_FF * 256, 128], [1, IT_FF * 256]])
                        P.inc(ldw[b], sy2.dma_start(wd[b][:, :, :], src), 16)
                    else:
                        for i in range(IT_FF):
                            P.inc(ldw[b], sy2.dma_start(
                                wd[b][:, i, :],
                                wdn_ap[i * 128:(i + 1) * 128,
                                       fp_ * 256:(fp_ + 1) * 256]), 16)
                    ns += 1
        blk.sync(w_body)

        def t_body(t):
            ns = 0
            for p in range(NPAN):
                b = p % 2
                t.wait_ge(P.sem(ldh[b]), ldh_tot[p])
                for fp_ in range(NFP):
                    pb = (fp_ % 2) * 2
                    idx = p * NFP + fp_
                    wb = ns % 2
                    t.wait_ge(P.sem(ldw[wb]), ldw_tot[ns])
                    if idx >= 2:
                        t.wait_ge(P.sem(sqd), 2 * (idx - 1))
                    for i in range(IT_FF):
                        t.matmul(psd[pb][:], wd[wb][:, i, 0:128], hp[b][:, i, :],
                                 start=(i == 0), stop=(i == IT_FF - 1))
                        last = t.matmul(psd[pb + 1][:], wd[wb][:, i, 128:256], hp[b][:, i, :],
                                        start=(i == 0), stop=(i == IT_FF - 1))
                    P.inc(mmd, last)
                    ns += 1
                    if stage >= 2 and fp_ >= 1:
                        if fp_ == 1 and p >= 1:
                            t.wait_ge(P.sem(smv), p)
                        for q in range(2):
                            f = fp_ * 2 - 2 + q
                            t.wait_ge(P.sem(sqd), p * DT + f + 1)
                            last = t.matmul(pss[0][:], C["ones_bf"][:], ysq[:, f, :],
                                            start=(f == 0), stop=False)
                        P.inc(smd, last)
                if stage >= 2:
                    for q in range(2):
                        f = DT - 2 + q
                        t.wait_ge(P.sem(sqd), p * DT + f + 1)
                        last = t.matmul(pss[0][:], C["ones_bf"][:], ysq[:, f, :],
                                        start=False, stop=(f == DT - 1))
                    P.inc(smd, last)
                P.inc(hfree, t.nop())
        blk.tensor(t_body)

        def sc_body(s):
            for p in range(NPAN):
                for fp_ in range(NFP):
                    idx = p * NFP + fp_
                    pb = (fp_ % 2) * 2
                    s.wait_ge(P.sem(mmd), idx + 1)
                    for q in range(2):
                        f = fp_ * 2 + q
                        s.activation(ysb[:, f, :], psd[pb + q][:], AF.Identity)
                        P.inc(sqd, s.activation(ysq[:, f, :], ysb[:, f, :], AF.Square))
                if stage >= 2:
                    s.wait_ge(P.sem(smv), p + 1)
                    s.activation(rec[:], rec[:], AF.Ln, bias=EPS_RMS)
                    P.inc(recd, s.activation(rec[:], rec[:], AF.Exp, scale=-0.5))
        blk.scalar(sc_body)

        def v_body(v):
            nt = 0
            for p in range(NPAN):
                if stage >= 2:
                    v.wait_ge(P.sem(smd), (p + 1) * NFP)
                    P.inc(smv, v.tensor_single_scalar(rec[:], pss[0][:], 1.0 / D, ALU.mult))
                    v.wait_ge(P.sem(recd), p + 1)
                else:
                    v.wait_ge(P.sem(sqd), (p + 1) * DT)
                for f in range(DT):
                    xb = nt % 2
                    if nt >= 2:
                        v.wait_ge(P.sem(od[xb]), od_tot[nt - 2])
                    if stage >= 3:
                        v.wait_ge(P.sem(ldx[xb]), ldx_tot[nt])
                        v.tensor_tensor(tmp[:], ysb[:, f, :], rec[:], ALU.mult)
                        P.inc(appd, v.scalar_tensor_tensor(
                            xo[:, xb, :], tmp[:], 0.5, xt[:, xb, :], ALU.mult, ALU.add))
                    else:
                        P.inc(appd, v.tensor_copy(xo[:, xb, :], ysb[:, f, :]))
                    nt += 1
        blk.vector(v_body)


CONST_NAMES = ["c_ones_bf", "c_ident_bf", "c_ident_f32", "c_ones1_bf", "c_iota128",
               "c_arr256", "c_rarr256", "c_idxhi", "c_idxlo", "c_mskhi", "c_msklo",
               "c_iota8", "c_cmadd", "c_ones_f32", "c_fconst"]
CONST_SHAPES = {
    "c_ones_bf": ([128, 128], BF16), "c_ident_bf": ([128, 128], BF16),
    "c_ident_f32": ([128, 128], F32), "c_ones1_bf": ([1, 128], BF16),
    "c_iota128": ([128, 128], F32), "c_arr256": ([1, 256], F32),
    "c_rarr256": ([1, 256], F32), "c_idxhi": ([128, 256], F32),
    "c_idxlo": ([128, 256], F32), "c_mskhi": ([128, 256], F32),
    "c_msklo": ([128, 256], F32), "c_iota8": ([128, 8], F32),
    "c_cmadd": ([128, 4, 512], BF16), "c_ones_f32": ([128, 1], F32),
    "c_fconst": ([128, 16], F32),
}


def decl_consts(P):
    dram = {}
    for n in CONST_NAMES:
        shape, dt_ = CONST_SHAPES[n]
        dram[n] = P.nc.dram_tensor(n, shape, dt_, kind="ExternalInput").ap()
    return dram


def build_dequant():
    P = Prog("dequant")
    nc = P.nc
    dram = decl_consts(P)
    pk_gu = nc.dram_tensor("pk_gu", [256, GU_OUT], I32, kind="ExternalInput").ap()
    sc_gu = nc.dram_tensor("sc_gu", [16, GU_OUT], F16, kind="ExternalInput").ap()
    pk_dn = nc.dram_tensor("pk_dn", [704, D], I32, kind="ExternalInput").ap()
    sc_dn = nc.dram_tensor("sc_dn", [44, D], F16, kind="ExternalInput").ap()
    wgu = nc.dram_tensor("wgu", [D, GU_OUT], BF16, kind="ExternalOutput").ap()
    wdn = nc.dram_tensor("wdn", [I_FF, D], BF16, kind="ExternalOutput").ap()
    emit_consts(P, dram)
    P.open_scope()
    emit_dequant(P, "dqg", pk_gu, sc_gu, wgu, D, GU_OUT, 512)
    emit_barrier(P, "dq1")
    P.close_scope()
    P.open_scope()
    emit_dequant(P, "dqd", pk_dn, sc_dn, wdn, I_FF, D, 512)
    emit_barrier(P, "dq2")
    P.close_scope()
    P.close()
    ins = CONST_NAMES + ["pk_gu", "sc_gu", "pk_dn", "sc_dn"]
    return P.nc, ins, ["wgu", "wdn"]


def build_l0ffn(xsrc_name, xdst_name):
    P = Prog("l0ffn")
    nc = P.nc
    dram = decl_consts(P)
    x_in = nc.dram_tensor(xsrc_name, [D, T], F32, kind="ExternalInput").ap()
    wgu = nc.dram_tensor("wgu", [D, GU_OUT], BF16, kind="ExternalInput").ap()
    wdn = nc.dram_tensor("wdn", [I_FF, D], BF16, kind="ExternalInput").ap()
    x_out = nc.dram_tensor(xdst_name, [D, T], F32, kind="ExternalOutput").ap()
    xln = nc.dram_tensor("xln", [D, T], BF16, kind="Internal").ap()
    hbuf = nc.dram_tensor("hbuf", [I_FF, T], BF16, kind="Internal").ap()
    emit_consts(P, dram)
    P.open_scope()
    emit_ln(P, "ln", x_in, xln)
    emit_barrier(P, "b1")
    P.close_scope()
    P.open_scope()
    emit_l0ffn_gateup(P, "fa", xln, wgu, hbuf, wgu_tiled=False)
    emit_barrier(P, "b2")
    P.close_scope()
    P.open_scope()
    emit_l0ffn_down(P, "fb", hbuf, wdn, x_in, x_out, wdn_tiled=False)
    emit_barrier(P, "b3")
    P.close_scope()
    P.close()
    ins = CONST_NAMES + [xsrc_name, "wgu", "wdn"]
    return P.nc, ins, [xdst_name]


def build_l0attn(xsrc_name, xdst_name):
    P = Prog("l0attn")
    nc = P.nc
    dram = decl_consts(P)
    x_in = nc.dram_tensor(xsrc_name, [D, T], F32, kind="ExternalInput").ap()
    wqkv = nc.dram_tensor("wqkvT", [D, 3 * D], F32, kind="ExternalInput").ap()
    slope = nc.dram_tensor("slope_row", [1, 16], F32, kind="ExternalInput").ap()
    wgate = nc.dram_tensor("wgateT", [D, D], F32, kind="ExternalInput").ap()
    wout = nc.dram_tensor("woutT", [D, D], F32, kind="ExternalInput").ap()
    normw = nc.dram_tensor("normw2d", [128, 16], F32, kind="ExternalInput").ap()
    x_out = nc.dram_tensor(xdst_name, [D, T], F32, kind="ExternalOutput").ap()
    xln = nc.dram_tensor("xln", [D, T], BF16, kind="Internal").ap()
    hbuf = nc.dram_tensor("hbuf", [2 * D, T], BF16, kind="Internal").ap()
    var = nc.dram_tensor("varbuf", [1, T], F32, kind="Internal").ap()
    qkvb = nc.dram_tensor("qkvbuf", [3 * D, T], BF16, kind="Internal").ap()
    outh = hbuf[0:D, :]
    z = hbuf[D:2 * D, :]
    emit_consts(P, dram)
    P.open_scope()
    emit_ln(P, "ln", x_in, xln)
    emit_barrier(P, "b1")
    P.close_scope()
    P.open_scope()
    emit_lightning_proj(P, "lp", xln, wqkv, qkvb)
    emit_barrier(P, "b2a")
    P.close_scope()
    P.open_scope()
    emit_lightning_scan(P, "ls", qkvb, slope, outh)
    emit_barrier(P, "b2")
    P.close_scope()
    P.open_scope()
    emit_lightning_out(P, "lo", xln, outh, var, wgate, normw, z)
    emit_barrier(P, "b3")
    P.close_scope()
    P.open_scope()
    emit_matmul_residual(P, "wo", z, wout, x_in, x_out, D, alpha=1.0)
    emit_barrier(P, "b4")
    P.close_scope()
    P.close()
    ins = CONST_NAMES + [xsrc_name, "wqkvT", "slope_row", "wgateT", "woutT", "normw2d"]
    return P.nc, ins, [xdst_name]


def host_prep(inputs):
    shared = dict(host_consts())
    shared["pk_gu"] = np.ascontiguousarray(np.asarray(inputs["L0_ffn_gate_up_packed"]).T)
    shared["sc_gu"] = np.ascontiguousarray(np.asarray(inputs["L0_ffn_gate_up_scales"]))
    shared["pk_dn"] = np.ascontiguousarray(np.asarray(inputs["L0_ffn_down_packed"]).T)
    shared["sc_dn"] = np.ascontiguousarray(np.asarray(inputs["L0_ffn_down_scales"]))
    shared["wqkvT"] = np.ascontiguousarray(np.asarray(inputs["L0_attn_w_qkv"]).T)
    shared["slope_row"] = np.ascontiguousarray(
        np.asarray(inputs["L0_attn_slope_rate"]).reshape(1, 16))
    def tile_w(wT):
        KI, KO = wT.shape[0] // 128, wT.shape[1] // 128
        return np.ascontiguousarray(
            wT.reshape(KI, 128, KO, 128).transpose(2, 1, 0, 3)).reshape(wT.shape)
    shared["wgateT"] = np.ascontiguousarray(np.asarray(inputs["L0_attn_w_gate"]).T)
    shared["woutT"] = tile_w(np.asarray(inputs["L0_attn_w_out"]).T)
    shared["normw2d"] = np.ascontiguousarray(
        np.asarray(inputs["L0_attn_norm_w"]).reshape(16, 128).T)
    order, midx2d, tile_blocks = l1ffn_host_plan(inputs["L1_ffn_mask_idx"])
    shared["gate_wT"] = np.ascontiguousarray(np.asarray(inputs["L1_ffn_gate_w"]).T)
    shared["WT"] = tile_w(np.asarray(inputs["L1_ffn_W"]).T)
    shared["eaw"] = np.ascontiguousarray(np.asarray(inputs["L1_ffn_expert_atom_weights"]))
    shared["atoms_s"] = np.ascontiguousarray(np.asarray(inputs["L1_ffn_atoms"])[:, order])
    shared["imp_s"] = np.ascontiguousarray(np.asarray(inputs["L1_ffn_importance"])[:, order])
    shared["midx2d"] = midx2d
    shared["_tile_blocks"] = tile_blocks
    shared["Wq"] = tile_w(np.asarray(inputs["L1_attn_Wq"]))
    shared["Wk"] = tile_w(np.asarray(inputs["L1_attn_Wk"]))
    shared["Wv"] = tile_w(np.asarray(inputs["L1_attn_Wv"]))
    shared["Wo"] = tile_w(np.asarray(inputs["L1_attn_Wo"]))
    shared["bo2d"] = np.ascontiguousarray(
        np.asarray(inputs["L1_attn_bo"]).reshape(16, 128).T)
    x = np.asarray(inputs["L0_attn_hidden_states"])
    percore = [{"x_in": np.ascontiguousarray(x[c].T)} for c in range(8)]
    return shared, percore


def run_launch(builder_out, state):
    nc, innames, outnames = builder_out
    in_maps = [{k: state[c][k] for k in innames} for c in range(len(state))]
    res = run_bass_kernel_spmd(nc, in_maps, core_ids=list(range(len(state))))
    for c in range(len(state)):
        for k in outnames:
            state[c][k] = np.asarray(res.results[c][k])
    return state


def kernel(**inputs):
    import os as _os
    if _os.environ.get("MKK_UNFUSED"):
        return kernel_unfused(**inputs)
    return kernel_fused(**inputs)


def kernel_unfused(**inputs):
    shared, percore = host_prep(inputs)
    tile_blocks = shared.pop("_tile_blocks")
    state = [dict(shared, **percore[c]) for c in range(8)]
    for c in range(8):
        state[c]["xa"] = state[c]["x_in"]
    pg_dq = build_dequant()
    pg_ffn0 = build_l0ffn("xa", "xb")
    pg_attn0 = build_l0attn("xa", "xb")
    pg_ffn1 = build_l1ffn("xa", "xb", tile_blocks)
    pg_attn1 = build_l1attn("xa", "xb")
    run_launch(pg_dq, state)
    for prog in (pg_ffn0, pg_attn0, pg_ffn0, pg_ffn1, pg_attn1, pg_ffn1):
        run_launch(prog, state)
        for c in range(8):
            state[c]["xa"] = state[c]["xb"]
    y = np.stack([np.asarray(state[c]["xb"], dtype=np.float32).T for c in range(8)])
    return y


def emit_lightning(P, tag, xln_ap, wqkv_ap, slope_ap, outh_ap, var_ap):
    nc = P.nc
    sb = P.sbuf
    C = P.consts
    xr = sb(f"{tag}_xr", [128, DT, T], BF16)
    wqk = sb(f"{tag}_wq", [128, DT, 384], BF16)
    qb_ = sb(f"{tag}_qb", [128, T], BF16)
    kb_ = sb(f"{tag}_kb", [128, T], BF16)
    vb_ = sb(f"{tag}_vb", [128, T], BF16)
    slope = sb(f"{tag}_slope", [128, 16], F32)
    negs = sb(f"{tag}_negs", [128, 16], F32)
    qdrow = sb(f"{tag}_qdr", [1, 256], BF16)
    kdrow = sb(f"{tag}_kdr", [1, 256], BF16)
    bdcol = sb(f"{tag}_bd", [128, 1], F32)
    qdb = sb(f"{tag}_qdb", [128, 256], BF16)
    kdb = sb(f"{tag}_kdb", [128, 256], BF16)
    idxc = sb(f"{tag}_idxc", [128, 2, 256], F32)
    ddh = sb(f"{tag}_ddh", [128, 256], BF16)
    ddl = sb(f"{tag}_ddl", [128, 256], BF16)
    qqd = sb(f"{tag}_qqd", [128, 256], BF16)
    kkd = sb(f"{tag}_kkd", [128, 256], BF16)
    qkdd = sb(f"{tag}_qkdd", [128, 2, 256], BF16)
    trv = sb(f"{tag}_trv", [128, 2, 128], BF16)
    trk = sb(f"{tag}_trk", [128, 2, 128], BF16)
    kvf = sb(f"{tag}_kvf", [128, 128], F32)
    kvt = sb(f"{tag}_kvt", [128, 128], F32)
    kvb = sb(f"{tag}_kvb", [128, 128], BF16)
    ostg = sb(f"{tag}_ostg", [128, 2, 256], BF16)
    sqst = sb(f"{tag}_sq", [128, 256], BF16)
    vacc = sb(f"{tag}_vacc", [1, T], F32)
    ppj = [P.psum(f"{tag}_ppj{i}", [128, PN], F32) for i in range(3)]
    pqk = P.psum(f"{tag}_pqk", [128, 512], F32)
    pA = P.psum(f"{tag}_pA", [128, 512], F32)
    po = pA[:, 0:256]
    pkvd = pA[:, 256:384]
    ptr = P.psum(f"{tag}_ptr", [128, 4, 128], BF16)
    pdec = P.psum(f"{tag}_pdec", [128, 512], F32)
    pvar = pdec[0:1, 0:256]

    (ldx, lds, neg, decA, decP, decE, ddA, ddV, pj, pev, qkS, qqdS, kkdS, qkddS,
     trS, trE, oS, oev, sqS, kvdS, kvfS, kvbS, varM, varS) = [
        f"{tag}_{x}" for x in
        ("ldx", "lds", "neg", "dA", "dP", "dE", "ddA", "ddV", "pj", "pev", "qk",
         "qqd", "kkd", "qkdd", "tr", "trE", "o", "oev", "sqS", "kvd", "kvf",
         "kvb", "vM", "vS")]
    ldw = [f"{tag}_lw0", f"{tag}_lw1"]
    ost = [f"{tag}_os0", f"{tag}_os1"]
    ldw_tot = [DT * 16 * (h + 1) for h in range(NH)]
    ost_tot = [16 * (m // 2 + 1) for m in range(NH * NBLK)]

    with nc.Block() as blk:
        def g_body(g):
            for k in range(DT):
                P.inc(ldx, g.dma_start(xr[:, k, :], xln_ap[k * 128:(k + 1) * 128, :]), 16)
            src = bass.AP(slope_ap.tensor, slope_ap.offset, [[0, 128], [1, 16]])
            P.inc(lds, g.dma_start(slope[:], src), 16)
            for h in range(NH):
                if h >= 1:
                    g.wait_ge(P.sem(pj), h * 24)
                for k in range(DT):
                    P.inc(ldw[0], g.dma_start(
                        wqk[:, k, :],
                        wqkv_ap[k * 128:(k + 1) * 128, h * 384:(h + 1) * 384]), 16)
            g.wait_ge(P.sem(varS), NH * NBLK)
            P.inc(f"{tag}_vout", g.dma_start(var_ap[:], vacc[:]), 16)
            g.wait_ge(P.sem(f"{tag}_vout"), 16)
        blk.gpsimd(g_body)

        def v_body(v):
            v.wait_ge(P.sem(lds), 16)
            P.inc(neg, v.tensor_single_scalar(negs[:], slope[:], -1.0, ALU.mult))
            v.tensor_scalar_max(idxc[:, 0, :], C["idxhi"][:], 0.0)
            P.inc(f"{tag}_idx", v.tensor_scalar_max(idxc[:, 1, :], C["idxlo"][:], 0.0))
            v.memset(vacc[:], 0.0)
            for h in range(NH):
                hb = h % 2
                v.wait_ge(P.sem(ddA), h * 2 + 2)
                v.tensor_tensor(ddh[:], ddh[:], C["mskhi"][:], ALU.mult)
                P.inc(ddV, v.tensor_tensor(ddl[:], ddl[:], C["msklo"][:], ALU.mult))
                v.memset(kvf[:], 0.0)
                P.inc(kvbS, v.memset(kvb[:], 0.0))
                for blki in range(NBLK):
                    m = h * NBLK + blki
                    cs = slice(blki * 256, (blki + 1) * 256)
                    v.wait_ge(P.sem(pev), h * 24 + 24)
                    v.wait_ge(P.sem(decE), h + 1)
                    P.inc(qqdS, v.tensor_tensor(qqd[:], qb_[:, cs], qdb[:], ALU.mult))
                    P.inc(kkdS, v.tensor_tensor(kkd[:], kb_[:, cs], kdb[:], ALU.mult))
                    v.wait_ge(P.sem(qkS), m + 1)
                    v.tensor_tensor(qkdd[:, 0, :], pqk[:, 0:256], ddh[:], ALU.mult)
                    P.inc(qkddS, v.tensor_tensor(qkdd[:, 1, :], pqk[:, 256:512], ddl[:], ALU.mult))
                    v.wait_ge(P.sem(f"{tag}_kvsc"), m + 1)
                    P.inc(kvfS, v.tensor_tensor(kvf[:], kvt[:], pkvd, ALU.add))
                    P.inc(kvbS, v.tensor_copy(kvb[:], kvf[:]))
                    v.wait_ge(P.sem(varM), m + 1)
                    P.inc(varS, v.tensor_tensor(vacc[0:1, cs], vacc[0:1, cs], pvar, ALU.add))
        blk.vector(v_body)

        def sc_body(s):
            s.wait_ge(P.sem(lds), 16)
            for h in range(NH):
                hb = h % 2
                s.wait_ge(P.sem(neg), 1)
                if h >= 1:
                    s.wait_ge(P.sem(qkddS), h * NBLK)
                    s.wait_ge(P.sem(qqdS), h * NBLK)
                    s.wait_ge(P.sem(kkdS), h * NBLK)
                nsc = negs[:, h:h + 1]
                s.activation(qdrow[:], C["arr256"][:], AF.Exp, scale=negs[0:1, h:h + 1])
                s.activation(kdrow[:], C["rarr256"][:], AF.Exp, scale=negs[0:1, h:h + 1])
                P.inc(decA, s.activation(bdcol[:], slope[:, h:h + 1], AF.Exp, scale=-256.0))
                if h == 0:
                    s.wait_ge(P.sem(f"{tag}_idx"), 1)
                s.activation(ddh[:], idxc[:, 0, :], AF.Exp, scale=nsc)
                P.inc(ddA, s.activation(ddl[:], idxc[:, 1, :], AF.Exp, scale=nsc), 2)
                s.wait_ge(P.sem(decP), h + 1)
                s.activation(qdb[:], pdec[:, 0:256], AF.Copy)
                P.inc(decE, s.activation(kdb[:], pdec[:, 256:512], AF.Copy))
                for pn in range(NPAN):
                    for o in range(3):
                        i = h * 24 + pn * 3 + o
                        s.wait_ge(P.sem(pj), i + 1)
                        dst = (qb_, kb_, vb_)[o][:, pn * PN:(pn + 1) * PN]
                        if h >= 1 and pn == 0 and o == 0:
                            s.wait_ge(P.sem(varS), h * NBLK)
                        P.inc(pev, s.activation(dst, ppj[o][:], AF.Silu))
                for blki in range(NBLK):
                    m = h * NBLK + blki
                    s.wait_ge(P.sem(trS), m + 1)
                    s.activation(trk[:, 0, :], ptr[:, 0, :], AF.Copy)
                    s.activation(trk[:, 1, :], ptr[:, 1, :], AF.Copy)
                    s.activation(trv[:, 0, :], ptr[:, 2, :], AF.Copy)
                    P.inc(trE, s.activation(trv[:, 1, :], ptr[:, 3, :], AF.Copy))
                    s.wait_ge(P.sem(oS), m + 1)
                    if m >= 2:
                        s.wait_ge(P.sem(ost[m % 2]), ost_tot[m - 2])
                    P.inc(oev, s.activation(ostg[:, m % 2, :], po, AF.Copy))
                    P.inc(sqS, s.activation(sqst[:], po, AF.Square))
                    s.wait_ge(P.sem(kvdS), m + 1)
                    s.wait_ge(P.sem(kvfS), m)
                    P.inc(f"{tag}_kvsc", s.activation(kvt[:], kvf[:], AF.Copy, scale=bdcol[:]))
        blk.scalar(sc_body)

        def t_body(t):
            t.wait_ge(P.sem(ldx), DT * 16)
            for h in range(NH):
                hb = h % 2
                t.wait_ge(P.sem(decA), h + 1)
                if h >= 1:
                    t.wait_ge(P.sem(varS), h * NBLK)
                t.matmul(pdec[:, 0:256], C["ones1_bf"][:], qdrow[:], start=True, stop=True)
                P.inc(decP, t.matmul(pdec[:, 256:512], C["ones1_bf"][:], kdrow[:],
                                     start=True, stop=True))
                t.wait_ge(P.sem(ldw[0]), ldw_tot[h])
                for pn in range(NPAN):
                    for o in range(3):
                        i = h * 24 + pn * 3 + o
                        if i >= 3:
                            t.wait_ge(P.sem(pev), i - 2)
                        for k in range(DT):
                            last = t.matmul(ppj[o][:],
                                            wqk[:, k, o * 128:(o + 1) * 128],
                                            xr[:, k, pn * PN:(pn + 1) * PN],
                                            start=(k == 0), stop=(k == DT - 1))
                        P.inc(pj, last)
                for blki in range(NBLK):
                    m = h * NBLK + blki
                    cs = slice(blki * 256, (blki + 1) * 256)
                    if m >= 1:
                        t.wait_ge(P.sem(qkddS), m)
                    v_q = qb_[:, cs]
                    t.matmul(pqk[:, 0:256], kb_[:, blki * 256:blki * 256 + 128],
                             v_q, start=True, stop=True)
                    P.inc(qkS, t.matmul(pqk[:, 256:512],
                                        kb_[:, blki * 256 + 128:blki * 256 + 256],
                                        v_q, start=True, stop=True))
                    t.wait_ge(P.sem(kkdS), m + 1)
                    if m >= 1:
                        t.wait_ge(P.sem(trE), m)
                    t.transpose(ptr[:, 0, :], kkd[:, 0:128], C["ident_bf"][:])
                    t.transpose(ptr[:, 1, :], kkd[:, 128:256], C["ident_bf"][:])
                    t.transpose(ptr[:, 2, :], vb_[:, blki * 256:blki * 256 + 128],
                                C["ident_bf"][:])
                    P.inc(trS, t.transpose(ptr[:, 3, :],
                                           vb_[:, blki * 256 + 128:blki * 256 + 256],
                                           C["ident_bf"][:]))
                    t.wait_ge(P.sem(qqdS), m + 1)
                    t.wait_ge(P.sem(kvbS), h * (NBLK + 1) + 1 + blki)
                    if m >= 1:
                        t.wait_ge(P.sem(sqS), m)
                    if m >= 1:
                        t.wait_ge(P.sem(kvfS), m)
                    t.matmul(po, kvb[:], qqd[:], start=True, stop=False)
                    t.wait_ge(P.sem(qkddS), m + 1)
                    t.wait_ge(P.sem(trE), m + 1)
                    t.matmul(po, trv[:, 0, :], qkdd[:, 0, :], start=False, stop=False)
                    P.inc(oS, t.matmul(po, trv[:, 1, :], qkdd[:, 1, :],
                                       start=False, stop=True))
                    t.wait_ge(P.sem(sqS), m + 1)
                    t.matmul(pkvd, trk[:, 0, :], trv[:, 0, :], start=True, stop=False)
                    P.inc(kvdS, t.matmul(pkvd, trk[:, 1, :], trv[:, 1, :],
                                         start=False, stop=True))
                    t.wait_ge(P.sem(kvfS), m + 1)
                    if m >= 1:
                        t.wait_ge(P.sem(varS), m)
                    P.inc(varM, t.matmul(pvar, C["ones_bf"][:, 0:1], sqst[:],
                                         start=True, stop=True))
        blk.tensor(t_body)

        def s_out(sy):
            for h in range(NH):
                for blki in range(NBLK):
                    m = h * NBLK + blki
                    sy.wait_ge(P.sem(oev), m + 1)
                    P.inc(ost[m % 2], sy.dma_start(
                        outh_ap[h * 128:(h + 1) * 128, blki * 256:(blki + 1) * 256],
                        ostg[:, m % 2, :]), 16)
                    assert P.n(ost[m % 2]) == ost_tot[m]
            sy.wait_ge(P.sem(ost[0]), P.n(ost[0]))
            sy.wait_ge(P.sem(ost[1]), P.n(ost[1]))
        blk.sync(s_out)


def emit_lightning_proj(P, tag, xln_ap, wqkv_ap, qkv_ap):
    nc = P.nc
    sb = P.sbuf
    xr = sb(f"{tag}_xr", [128, DT, T], BF16)
    wqk = [sb(f"{tag}_wq{b}", [128, DT, 384], BF16) for b in range(2)]
    pstg = sb(f"{tag}_pstg", [128, 2, PN], BF16)
    ppj = [P.psum(f"{tag}_ppj{i}", [128, PN], F32) for i in range(3)]

    ldx, pj, pev = f"{tag}_ldx", f"{tag}_pj", f"{tag}_pev"
    ldw = [f"{tag}_lw0", f"{tag}_lw1"]
    pst = [f"{tag}_ps0", f"{tag}_ps1"]
    ldw_tot = [DT * 16 * (h // 2 + 1) for h in range(NH)]

    with nc.Block() as blk:
        def g_body(g):
            for k in range(DT):
                P.inc(ldx, g.dma_start(xr[:, k, :],
                                       xln_ap[k * 128:(k + 1) * 128, :]), 16)
            for h in range(NH):
                b = h % 2
                if h >= 2:
                    g.wait_ge(P.sem(pj), (h - 1) * 24)
                for k in range(DT):
                    P.inc(ldw[b], g.dma_start(
                        wqk[b][:, k, :],
                        wqkv_ap[k * 128:(k + 1) * 128,
                                h * 384:(h + 1) * 384]), 16)
        blk.gpsimd(g_body)

        def t_body(t):
            t.wait_ge(P.sem(ldx), DT * 16)
            for h in range(NH):
                b = h % 2
                t.wait_ge(P.sem(ldw[b]), ldw_tot[h])
                for pn in range(NPAN):
                    for o in range(3):
                        i = h * 24 + pn * 3 + o
                        if i >= 3:
                            t.wait_ge(P.sem(pev), i - 2)
                        for k in range(DT):
                            last = t.matmul(ppj[o][:],
                                            wqk[b][:, k, o * 128:(o + 1) * 128],
                                            xr[:, k, pn * PN:(pn + 1) * PN],
                                            start=(k == 0), stop=(k == DT - 1))
                        P.inc(pj, last)
        blk.tensor(t_body)

        def sc_body(s):
            for i in range(NH * 24):
                s.wait_ge(P.sem(pj), i + 1)
                if i >= 2:
                    s.wait_ge(P.sem(pst[i % 2]), 16 * (i // 2))
                P.inc(pev, s.activation(pstg[:, i % 2, :], ppj[i % 3][:],
                                        AF.Silu))
        blk.scalar(sc_body)

        def s_out(sy):
            for h in range(NH):
                for pn in range(NPAN):
                    for o in range(3):
                        i = h * 24 + pn * 3 + o
                        sy.wait_ge(P.sem(pev), i + 1)
                        P.inc(pst[i % 2], sy.dma_start(
                            bass.AP(qkv_ap.tensor,
                                    qkv_ap.offset + (o * D + h * 128) * T
                                    + pn * PN, [[T, 128], [1, PN]]),
                            pstg[:, i % 2, :]), 16)
            sy.wait_ge(P.sem(pst[0]), P.n(pst[0]))
            sy.wait_ge(P.sem(pst[1]), P.n(pst[1]))
        blk.sync(s_out)


def emit_lightning_scan(P, tag, qkv_ap, slope_ap, outh_ap):
    nc = P.nc
    sb = P.sbuf
    C = P.consts
    qbc = [sb(f"{tag}_qb{u}", [128, 2, 8 * 256], BF16) for u in range(2)]
    kbc = [sb(f"{tag}_kb{u}", [128, 2, 8 * 256], BF16) for u in range(2)]
    vbc = [sb(f"{tag}_vb{u}", [128, 2, 8 * 256], BF16) for u in range(2)]
    slope = sb(f"{tag}_slope", [128, 16], F32)
    negs = sb(f"{tag}_negs", [128, 16], F32)
    idxc = sb(f"{tag}_idxc", [128, 2, 256], F32)
    qdrow = [sb(f"{tag}_qdr{u}", [1, 256], BF16) for u in range(2)]
    kdrow = [sb(f"{tag}_kdr{u}", [1, 256], BF16) for u in range(2)]
    bdcol = [sb(f"{tag}_bd{u}", [128, 1], F32) for u in range(2)]
    qdb = [sb(f"{tag}_qdb{u}", [128, 256], BF16) for u in range(2)]
    kdb = [sb(f"{tag}_kdb{u}", [128, 256], BF16) for u in range(2)]
    ddh = [sb(f"{tag}_ddh{u}", [128, 256], BF16) for u in range(2)]
    ddl = [sb(f"{tag}_ddl{u}", [128, 256], BF16) for u in range(2)]
    qqd = [sb(f"{tag}_qqd{u}", [128, 2, 256], BF16) for u in range(2)]
    kkd = [sb(f"{tag}_kkd{u}", [128, 256], BF16) for u in range(2)]
    qkdd = [sb(f"{tag}_qkdd{u}", [128, 2, 2, 256], BF16) for u in range(2)]
    trv = [sb(f"{tag}_trv{u}", [128, 2, 2, 128], BF16) for u in range(2)]
    trk = [sb(f"{tag}_trk{u}", [128, 2, 2, 128], BF16) for u in range(2)]
    kvf = [sb(f"{tag}_kvf{u}", [128, 128], F32) for u in range(2)]
    kvb = [sb(f"{tag}_kvb{u}", [128, 128], BF16) for u in range(2)]
    ostg = sb(f"{tag}_ostg", [128, 2, 256], BF16)
    pqk = [P.psum(f"{tag}_pqk{u}", [128, 512], F32) for u in range(2)]
    pA_f = [P.psum(f"{tag}_pA{u}", [128, 512], F32) for u in range(2)]
    pA = [x[:, 0:256] for x in pA_f]
    ptr_f = [P.psum(f"{tag}_ptr{u}", [128, 8, 128], BF16) for u in range(2)]
    ptr = [x[:, 0:4, :] for x in ptr_f]
    pkv_f = [P.psum(f"{tag}_pkv{u}", [128, 512], F32) for u in range(2)]
    pkv = [x[:, 0:128] for x in pkv_f]

    NG = NH // 2

    def sfx(name, u):
        return f"{tag}_{name}{u}"
    lds, neg, idx, oev = (f"{tag}_lds", f"{tag}_neg", f"{tag}_idx",
                          f"{tag}_oev")
    ost = [f"{tag}_os0", f"{tag}_os1"]
    ost_tot = [16 * (m // 2 + 1) for m in range(NH * NBLK)]

    with nc.Block() as blk:
        def g_body(g):
            src = bass.AP(slope_ap.tensor, slope_ap.offset, [[0, 128], [1, 16]])
            P.inc(lds, g.dma_start(slope[:], src), 16)
            for grp in range(NG):
                for u in range(2):
                    h = 2 * grp + u
                    for hf in range(2):
                        if grp >= 1:
                            cnt = (grp - 1) * NBLK + 8 * (hf + 1)
                            g.wait_ge(P.sem(sfx("qkddS", u)), cnt)
                            g.wait_ge(P.sem(sfx("qkdS", u)), 2 * cnt)
                            g.wait_ge(P.sem(sfx("trS", u)), cnt)
                        for o, buf in ((0, qbc), (1, kbc), (2, vbc)):
                            P.inc(sfx("ldq", u), g.dma_start(
                                buf[u][:, hf, :],
                                bass.AP(qkv_ap.tensor,
                                        qkv_ap.offset + (o * D + h * 128) * T
                                        + hf * 2048,
                                        [[T, 128], [1, 2048]])), 16)
        blk.gpsimd(g_body)

        def v_body(v):
            v.wait_ge(P.sem(lds), 16)
            P.inc(neg, v.tensor_single_scalar(negs[:], slope[:], -1.0, ALU.mult))
            v.tensor_scalar_max(idxc[:, 0, :], C["idxhi"][:], 0.0)
            P.inc(idx, v.tensor_scalar_max(idxc[:, 1, :], C["idxlo"][:], 0.0))
            for grp in range(NG):
                for u in range(2):
                    n0 = grp * NBLK
                    v.wait_ge(P.sem(sfx("ddA", u)), grp * 2 + 2)
                    v.tensor_tensor(ddh[u][:], ddh[u][:], C["mskhi"][:], ALU.mult)
                    v.tensor_tensor(ddl[u][:], ddl[u][:], C["msklo"][:], ALU.mult)
                    if grp >= 1:
                        v.wait_ge(P.sem(sfx("oS", u)), n0)
                    v.memset(kvf[u][:], 0.0)
                    P.inc(sfx("kvbS", u), v.memset(kvb[u][:], 0.0))
                for step in range(NBLK + 1):
                    for u in range(2):
                        if step < NBLK:
                            b = step
                            n = grp * NBLK + b
                            hf = b // 8
                            cs = slice((b % 8) * 256, (b % 8) * 256 + 256)
                            v.wait_ge(P.sem(sfx("ldq", u)),
                                      48 * (2 * grp + hf + 1))
                            if step == 0:
                                v.wait_ge(P.sem(sfx("decE", u)), grp + 1)
                            if n >= 2:
                                v.wait_ge(P.sem(sfx("oS", u)), n - 1)
                            if n >= 1:
                                v.wait_ge(P.sem(sfx("trS", u)), n)
                            v.tensor_tensor(qqd[u][:, b % 2, :],
                                            qbc[u][:, hf, cs], qdb[u][:],
                                            ALU.mult)
                            P.inc(sfx("qkdS", u), v.tensor_tensor(
                                kkd[u][:], kbc[u][:, hf, cs], kdb[u][:],
                                ALU.mult), 2)
                            v.wait_ge(P.sem(sfx("qkS", u)), n + 1)
                            if n >= 2:
                                v.wait_ge(P.sem(sfx("oS", u)), n - 1)
                            v.tensor_tensor(qkdd[u][:, b % 2, 0, :],
                                            pqk[u][:, 0:256],
                                            ddh[u][:], ALU.mult)
                            P.inc(sfx("qkddS", u), v.tensor_tensor(
                                qkdd[u][:, b % 2, 1, :], pqk[u][:, 256:512],
                                ddl[u][:], ALU.mult))
                        if step >= 1:
                            n = grp * NBLK + step - 1
                            v.wait_ge(P.sem(sfx("kvdS", u)), n + 1)
                            v.wait_ge(P.sem(sfx("decA", u)), grp + 1)
                            P.inc(sfx("kvfS", u), v.scalar_tensor_tensor(
                                kvf[u][:], kvf[u][:], bdcol[u][:],
                                pkv[u][:], ALU.mult, ALU.add))
                            P.inc(sfx("kvbS", u), v.tensor_copy(kvb[u][:],
                                                                kvf[u][:]))
        blk.vector(v_body)

        def sc_body(s):
            s.wait_ge(P.sem(lds), 16)
            for grp in range(NG):
                for u in range(2):
                    h = 2 * grp + u
                    s.wait_ge(P.sem(neg), 1)
                    if grp >= 1:
                        s.wait_ge(P.sem(sfx("qkddS", u)), grp * NBLK)
                        s.wait_ge(P.sem(sfx("qkdS", u)), 2 * grp * NBLK)
                    nsc = negs[0:1, h:h + 1]
                    s.activation(qdrow[u][:], C["arr256"][:], AF.Exp, scale=nsc)
                    s.activation(kdrow[u][:], C["rarr256"][:], AF.Exp, scale=nsc)
                    P.inc(sfx("decA", u), s.activation(
                        bdcol[u][:], slope[:, h:h + 1], AF.Exp, scale=-256.0))
                    if grp == 0 and u == 0:
                        s.wait_ge(P.sem(idx), 1)
                    s.activation(ddh[u][:], idxc[:, 0, :], AF.Exp,
                                 scale=negs[:, h:h + 1])
                    P.inc(sfx("ddA", u), s.activation(
                        ddl[u][:], idxc[:, 1, :], AF.Exp,
                        scale=negs[:, h:h + 1]), 2)
                    s.wait_ge(P.sem(sfx("decP", u)), grp + 1)
                    s.activation(qdb[u][:], pqk[u][:, 0:256], AF.Copy)
                    P.inc(sfx("decE", u), s.activation(kdb[u][:],
                                                       pqk[u][:, 256:512],
                                                       AF.Copy))
                for step in range(NBLK + 1):
                    for u in range(2):
                        if step < NBLK:
                            n = grp * NBLK + step
                            s.wait_ge(P.sem(sfx("trS", u)), n + 1)
                            if n >= 2:
                                s.wait_ge(P.sem(sfx("kvdS", u)), n - 1)
                                s.wait_ge(P.sem(sfx("oS", u)), n - 1)
                            s.activation(trk[u][:, step % 2, 0, :],
                                         ptr[u][:, 0, :], AF.Copy)
                            s.activation(trk[u][:, step % 2, 1, :],
                                         ptr[u][:, 1, :], AF.Copy)
                            s.activation(trv[u][:, step % 2, 0, :],
                                         ptr[u][:, 2, :], AF.Copy)
                            P.inc(sfx("trE", u), s.activation(
                                trv[u][:, step % 2, 1, :], ptr[u][:, 3, :],
                                AF.Copy))
                        if step >= 1:
                            b = step - 1
                            n = grp * NBLK + b
                            m = 2 * n + u
                            s.wait_ge(P.sem(sfx("oS", u)), n + 1)
                            if m >= 2:
                                s.wait_ge(P.sem(ost[m % 2]), ost_tot[m - 2])
                            P.inc(oev, s.activation(ostg[:, m % 2, :],
                                                    pA[u][:], AF.Copy))
        blk.scalar(sc_body)

        def t_body(t):
            for grp in range(NG):
                for u in range(2):
                    t.wait_ge(P.sem(sfx("decA", u)), grp + 1)
                    if grp >= 1:
                        t.wait_ge(P.sem(sfx("qkddS", u)), grp * NBLK)
                    t.matmul(pqk[u][:, 0:256], C["ones1_bf"][:], qdrow[u][:],
                             start=True, stop=True)
                    P.inc(sfx("decP", u), t.matmul(pqk[u][:, 256:512],
                                                   C["ones1_bf"][:],
                                                   kdrow[u][:],
                                                   start=True, stop=True))
                for step in range(NBLK + 1):
                    for u in range(2):
                        if step < NBLK:
                            b = step
                            n = grp * NBLK + b
                            hf = b // 8
                            c0 = (b % 8) * 256
                            t.wait_ge(P.sem(sfx("ldq", u)),
                                      48 * (2 * grp + hf + 1))
                            t.wait_ge(P.sem(sfx("qkddS", u)), n)
                            if step == 0:
                                t.wait_ge(P.sem(sfx("decE", u)), grp + 1)
                            t.matmul(pqk[u][:, 0:256],
                                     kbc[u][:, hf, c0:c0 + 128],
                                     qbc[u][:, hf, c0:c0 + 256],
                                     start=True, stop=True)
                            P.inc(sfx("qkS", u), t.matmul(
                                pqk[u][:, 256:512],
                                kbc[u][:, hf, c0 + 128:c0 + 256],
                                qbc[u][:, hf, c0:c0 + 256],
                                start=True, stop=True))
                            t.wait_ge(P.sem(sfx("qkdS", u)), 2 * (n + 1))
                            if n >= 1:
                                t.wait_ge(P.sem(sfx("trE", u)), n)
                            t.transpose(ptr[u][:, 0, :], kkd[u][:, 0:128],
                                        C["ident_bf"][:])
                            t.transpose(ptr[u][:, 1, :], kkd[u][:, 128:256],
                                        C["ident_bf"][:])
                            t.transpose(ptr[u][:, 2, :],
                                        vbc[u][:, hf, c0:c0 + 128],
                                        C["ident_bf"][:])
                            P.inc(sfx("trS", u), t.transpose(
                                ptr[u][:, 3, :],
                                vbc[u][:, hf, c0 + 128:c0 + 256],
                                C["ident_bf"][:]))
                        if step >= 1:
                            b = step - 1
                            n = grp * NBLK + b
                            t.wait_ge(P.sem(sfx("trE", u)), n + 1)
                            t.wait_ge(P.sem(sfx("kvbS", u)),
                                      grp * (NBLK + 1) + 1 + b)
                            t.wait_ge(P.sem(sfx("qkddS", u)), n + 1)
                            t.wait_ge(P.sem(sfx("qkdS", u)), 2 * (n + 1))
                            m = 2 * n + u
                            if m >= 2:
                                t.wait_ge(P.sem(oev), m - 1)
                            t.matmul(pA[u][:], kvb[u][:],
                                     qqd[u][:, b % 2, :], start=True,
                                     stop=False)
                            t.matmul(pA[u][:], trv[u][:, b % 2, 0, :],
                                     qkdd[u][:, b % 2, 0, :],
                                     start=False, stop=False)
                            P.inc(sfx("oS", u), t.matmul(
                                pA[u][:], trv[u][:, b % 2, 1, :],
                                qkdd[u][:, b % 2, 1, :],
                                start=False, stop=True))
                            if n >= 1:
                                t.wait_ge(P.sem(sfx("kvfS", u)), n)
                            t.matmul(pkv[u][:], trk[u][:, b % 2, 0, :],
                                     trv[u][:, b % 2, 0, :],
                                     start=True, stop=False)
                            P.inc(sfx("kvdS", u), t.matmul(
                                pkv[u][:], trk[u][:, b % 2, 1, :],
                                trv[u][:, b % 2, 1, :],
                                start=False, stop=True))
        blk.tensor(t_body)

        def s_out(sy):
            for grp in range(NG):
                for b in range(NBLK):
                    for u in range(2):
                        h = 2 * grp + u
                        m = (grp * NBLK + b) * 2 + u
                        sy.wait_ge(P.sem(oev), m + 1)
                        P.inc(ost[m % 2], sy.dma_start(
                            outh_ap[h * 128:(h + 1) * 128,
                                    b * 256:(b + 1) * 256],
                            ostg[:, m % 2, :]), 16)
            sy.wait_ge(P.sem(ost[0]), P.n(ost[0]))
            sy.wait_ge(P.sem(ost[1]), P.n(ost[1]))
        blk.sync(s_out)


def emit_lightning_out(P, tag, xln_ap, outh_ap, var_ap, wgate_ap, normw_ap, z_ap):
    nc = P.nc
    sb = P.sbuf
    C = P.consts
    xr = sb(f"{tag}_xr", [128, DT, T], BF16)
    wgt = [sb(f"{tag}_wg{b}", [128, DT, 128], BF16) for b in range(2)]
    oh = [sb(f"{tag}_oh{b}", [128, T], BF16) for b in range(2)]
    nw = sb(f"{tag}_nw", [128, 16], F32)
    varr = sb(f"{tag}_var", [1, T], F32)
    rstd = varr
    rstdb = sb(f"{tag}_rb", [128, T], F32)
    gsb = sb(f"{tag}_gsb", [128, 2, PN], F32)
    zt = sb(f"{tag}_zt", [128, 2, PN], BF16)
    t1 = sb(f"{tag}_t1", [128, PN], F32)
    ppj = [P.psum(f"{tag}_pp{b}", [128, PN], F32) for b in range(2)]

    sqv = sb(f"{tag}_sqv", [128, 2, PN], BF16)
    pvar = [P.psum(f"{tag}_pv{b}", [128, PN], F32) for b in range(2)]
    ldx, ldn, rsd, gev, zdone = (f"{tag}_ldx", f"{tag}_ldn",
                                 f"{tag}_rsd", f"{tag}_gev", f"{tag}_zd")
    sqA, vmm, vS = f"{tag}_sqA", f"{tag}_vmm", f"{tag}_vS"
    ldvh = [f"{tag}_lv0", f"{tag}_lv1"]
    ldw = [f"{tag}_lw0", f"{tag}_lw1"]
    ldo = [f"{tag}_lo0", f"{tag}_lo1"]
    zst = [f"{tag}_zs0", f"{tag}_zs1"]
    ldw_tot = [DT * 16 * (o // 2 + 1) for o in range(DT)]
    ldo_tot = [16 * (o // 2 + 1) for o in range(DT)]
    zst_tot = [NPAN * 16 * (o // 2 + 1) for o in range(DT)]

    with nc.Block() as blk:
        def g_body(g):
            for k in range(DT):
                P.inc(ldx, g.dma_start(xr[:, k, :], xln_ap[k * 128:(k + 1) * 128, :]), 16)
            P.inc(ldn, g.dma_start(nw[:], normw_ap[:]), 16)
            for o in range(DT):
                b = o % 2
                if o >= 2:
                    g.wait_ge(P.sem(sqA), (o - 1) * NPAN)
                P.inc(ldvh[b], g.dma_start(oh[b][:],
                                           outh_ap[o * 128:(o + 1) * 128, :]), 16)
            g.wait_ge(P.sem(f"{tag}_rst"), 16)
            src2 = bass.AP(var_ap.tensor, var_ap.offset, [[0, 128], [1, T]])
            P.inc(f"{tag}_rb", g.dma_start(rstdb[:], src2), 16)
            for o in range(DT):
                b = o % 2
                if o >= 2:
                    g.wait_ge(P.sem(gev), (o - 1) * NPAN)
                for k in range(DT):
                    P.inc(ldw[b], g.dma_start(
                        wgt[b][:, k, :], wgate_ap[k * 128:(k + 1) * 128,
                                                  o * 128:(o + 1) * 128]), 16)
                assert P.n(ldw[b]) == ldw_tot[o]
                if o == 0:
                    g.wait_ge(P.sem(sqA), DT * NPAN)
                if o >= 2:
                    g.wait_ge(P.sem(zdone), (o - 1) * NPAN)
                P.inc(ldo[b], g.dma_start(oh[b][:], outh_ap[o * 128:(o + 1) * 128, :]), 16)
                assert P.n(ldo[b]) == ldo_tot[o]
        blk.gpsimd(g_body)

        def sc_body(s):
            for o in range(DT):
                b = o % 2
                s.wait_ge(P.sem(ldvh[b]), 16 * (o // 2 + 1))
                for pn in range(NPAN):
                    i = o * NPAN + pn
                    if i >= 2:
                        s.wait_ge(P.sem(vmm), i - 1)
                    P.inc(sqA, s.activation(sqv[:, i % 2, :],
                                            oh[b][:, pn * PN:(pn + 1) * PN],
                                            AF.Square))
            s.wait_ge(P.sem(vS), DT * NPAN)
            s.activation(rstd[:], varr[:], AF.Ln, scale=1.0 / D, bias=EPS_RMS)
            P.inc(rsd, s.activation(rstd[:], rstd[:], AF.Exp, scale=-0.5))
            for o in range(DT):
                for pn in range(NPAN):
                    i = o * NPAN + pn
                    s.wait_ge(P.sem(f"{tag}_pj"), i + 1)
                    if i >= 2:
                        s.wait_ge(P.sem(zdone), i - 1)
                    P.inc(gev, s.activation(gsb[:, i % 2, :], ppj[i % 2][:], AF.Sigmoid))
        blk.scalar(sc_body)

        def t_body(t):
            for i in range(DT * NPAN):
                t.wait_ge(P.sem(sqA), i + 1)
                if i >= 2:
                    t.wait_ge(P.sem(vS), i - 1)
                P.inc(vmm, t.matmul(pvar[i % 2][:], C["ones_bf"][:],
                                    sqv[:, i % 2, :], start=True, stop=True))
            t.wait_ge(P.sem(ldx), DT * 16)
            for o in range(DT):
                b = o % 2
                t.wait_ge(P.sem(ldw[b]), ldw_tot[o])
                for pn in range(NPAN):
                    i = o * NPAN + pn
                    if i >= 2:
                        t.wait_ge(P.sem(gev), i - 1)
                    for k in range(DT):
                        last = t.matmul(ppj[i % 2][:], wgt[b][:, k, :],
                                        xr[:, k, pn * PN:(pn + 1) * PN],
                                        start=(k == 0), stop=(k == DT - 1))
                    P.inc(f"{tag}_pj", last)
        blk.tensor(t_body)

        def v_body(v):
            v.memset(varr[:], 0.0)
            for i in range(DT * NPAN):
                pn = i % NPAN
                cs = slice(pn * PN, (pn + 1) * PN)
                v.wait_ge(P.sem(vmm), i + 1)
                P.inc(vS, v.tensor_tensor(varr[0:1, cs], varr[0:1, cs],
                                          pvar[i % 2][0:1, :], ALU.add))
            for o in range(DT):
                b = o % 2
                v.wait_ge(P.sem(ldo[b]), ldo_tot[o])
                v.wait_ge(P.sem(ldn), 16)
                v.wait_ge(P.sem(f"{tag}_rb"), 16)
                for pn in range(NPAN):
                    i = o * NPAN + pn
                    zb = i % 2
                    cs = slice(pn * PN, (pn + 1) * PN)
                    v.wait_ge(P.sem(gev), i + 1)
                    if i >= 2:
                        v.wait_ge(P.sem(zst[zb]), 16 * (i // 2))
                    v.tensor_tensor(t1[:], oh[b][:, cs], rstdb[:, cs], ALU.mult)
                    v.tensor_single_scalar(t1[:], t1[:], nw[:, o:o + 1], ALU.mult)
                    P.inc(zdone, v.tensor_tensor(zt[:, zb, :], t1[:], gsb[:, i % 2, :], ALU.mult))
        blk.vector(v_body)

        def s_out(sy):
            sy.wait_ge(P.sem(rsd), 1)
            P.inc(f"{tag}_rst", sy.dma_start(var_ap[:], rstd[:]), 16)
            for o in range(DT):
                for pn in range(NPAN):
                    i = o * NPAN + pn
                    zb = i % 2
                    sy.wait_ge(P.sem(zdone), i + 1)
                    P.inc(zst[zb], sy.dma_start(
                        z_ap[o * 128:(o + 1) * 128, pn * PN:(pn + 1) * PN],
                        zt[:, zb, :]), 16)
            sy.wait_ge(P.sem(zst[0]), P.n(zst[0]))
            sy.wait_ge(P.sem(zst[1]), P.n(zst[1]))
        blk.sync(s_out)


def emit_matmul_residual(P, tag, z_ap, w_ap, x_ap, xout_ap, KD, alpha=1.0,
                         bias_ap=None, contrib_ap=None):
    nc = P.nc
    sb = P.sbuf
    KT = KD // 128
    NSL4 = 4
    zr = sb(f"{tag}_zr", [128, KT, T], BF16)
    wt = [sb(f"{tag}_wt{b}", [128, KT, 128], BF16) for b in range(2)]
    bias = sb(f"{tag}_bias", [128, 16], F32)
    xt = sb(f"{tag}_xt", [128, NSL4, PN], F32)
    xo = sb(f"{tag}_xo", [128, NSL4, PN], F32)
    pmm = P.psum(f"{tag}_pm", [128, NSL4, PN], F32)

    ldz, ldb, mm, appd = f"{tag}_ldz", f"{tag}_ldb", f"{tag}_mm", f"{tag}_ap"
    ldw = [f"{tag}_lw0", f"{tag}_lw1"]
    ldx = [f"{tag}_lx{b}" for b in range(NSL4)]
    od = [f"{tag}_od{b}" for b in range(NSL4)]
    ct = sb(f"{tag}_ct", [128, NSL4, PN], BF16) if contrib_ap is not None else None
    tmpc = sb(f"{tag}_tmpc", [128, PN], F32) if contrib_ap is not None else None
    NT = DT * NPAN
    nld = 2 if contrib_ap is not None else 1
    ldw_tot = [16 * (o // 2 + 1) for o in range(DT)]
    ldx_tot = [nld * 16 * (nt // NSL4 + 1) for nt in range(NT)]
    od_tot = [16 * (nt // NSL4 + 1) for nt in range(NT)]

    with nc.Block() as blk:
        def g_body(g):
            for k in range(KT):
                P.inc(ldz, g.dma_start(zr[:, k, :], z_ap[k * 128:(k + 1) * 128, :]), 16)
            if bias_ap is not None:
                P.inc(ldb, g.dma_start(bias[:], bias_ap[:]), 16)
            def xloads(o):
                for pn in range(NPAN):
                    nt = o * NPAN + pn
                    xb = nt % NSL4
                    if nt >= NSL4 - 1:
                        g.wait_ge(P.sem(appd), nt - (NSL4 - 1))
                    cs = slice(pn * PN, (pn + 1) * PN)
                    P.inc(ldx[xb], g.dma_start(xt[:, xb, :],
                                               x_ap[o * 128:(o + 1) * 128, cs]), 16)
                    if contrib_ap is not None:
                        P.inc(ldx[xb], g.dma_start(
                            ct[:, xb, :], contrib_ap[o * 128:(o + 1) * 128, cs]), 16)
            for o in range(DT):
                b = o % 2
                if o >= 2:
                    g.wait_ge(P.sem(mm), (o - 1) * NPAN)
                src = bass.AP(w_ap.tensor, w_ap.offset + o * 128 * KT * 128,
                              [[KT * 128, 128], [1, KT * 128]])
                P.inc(ldw[b], g.dma_start(wt[b][:, :, :], src), 16)
                assert P.n(ldw[b]) == ldw_tot[o]
                if o >= 1:
                    xloads(o - 1)
            xloads(DT - 1)
        blk.gpsimd(g_body)

        def t_body(t):
            t.wait_ge(P.sem(ldz), KT * 16)
            for o in range(DT):
                b = o % 2
                t.wait_ge(P.sem(ldw[b]), ldw_tot[o])
                for pn in range(NPAN):
                    i = o * NPAN + pn
                    if i >= NSL4:
                        t.wait_ge(P.sem(appd), i - (NSL4 - 1))
                    for k in range(KT):
                        last = t.matmul(pmm[:, i % NSL4, :], wt[b][:, k, :],
                                        xr_slice(zr, k, pn),
                                        start=(k == 0), stop=(k == KT - 1))
                    P.inc(mm, last)
        blk.tensor(t_body)

        def v_body(v):
            if bias_ap is not None:
                v.wait_ge(P.sem(ldb), 16)
            for nt in range(NT):
                o, pn = divmod(nt, NPAN)
                xb = nt % NSL4
                v.wait_ge(P.sem(mm), nt + 1)
                if nt >= NSL4:
                    v.wait_ge(P.sem(od[xb]), od_tot[nt - NSL4])
                v.wait_ge(P.sem(ldx[xb]), ldx_tot[nt])
                if contrib_ap is not None:
                    v.tensor_tensor(tmpc[:], pmm[:, xb, :], ct[:, xb, :], ALU.add)
                    P.inc(appd, v.scalar_tensor_tensor(
                        xo[:, xb, :], tmpc[:], alpha, xt[:, xb, :],
                        ALU.mult, ALU.add))
                elif bias_ap is not None:
                    v.tensor_single_scalar(xo[:, xb, :], pmm[:, xb, :],
                                           bias[:, o:o + 1], ALU.add)
                    P.inc(appd, v.tensor_tensor(xo[:, xb, :], xo[:, xb, :],
                                                xt[:, xb, :], ALU.add))
                elif alpha == 1.0:
                    P.inc(appd, v.tensor_tensor(xo[:, xb, :], pmm[:, xb, :],
                                                xt[:, xb, :], ALU.add))
                else:
                    P.inc(appd, v.scalar_tensor_tensor(
                        xo[:, xb, :], pmm[:, xb, :], alpha, xt[:, xb, :],
                        ALU.mult, ALU.add))
        blk.vector(v_body)

        def sc_store(sc):
            for nt in range(NT):
                o, pn = divmod(nt, NPAN)
                xb = nt % NSL4
                sc.wait_ge(P.sem(appd), nt + 1)
                cs = slice(pn * PN, (pn + 1) * PN)
                P.inc(od[xb], sc.dma_start(xout_ap[o * 128:(o + 1) * 128, cs],
                                           xo[:, xb, :]), 16)
            for b in range(NSL4):
                sc.wait_ge(P.sem(od[b]), P.n(od[b]))
        blk.scalar(sc_store)


def xr_slice(zr, k, pn):
    return zr[:, k, pn * PN:(pn + 1) * PN]


def emit_l1ffn_core(P, tag, xln_ap, gate_w_ap, eaw_ap, atoms_ap, imp_ap,
                    midx_ap, contrib_ap, tile_blocks):
    nc = P.nc
    sb = P.sbuf
    C = P.consts
    NTI = 32
    weighted = sb(f"{tag}_wtd", [8, NS], BF16)
    rTb = sb(f"{tag}_rtb", [8, T], BF16)
    midx = sb(f"{tag}_midx", [128, 128], I32)
    rows_f = sb(f"{tag}_rows", [128, 128], F32)
    cols_i = sb(f"{tag}_cols", [128, 128], I32)
    ldm, rcd = f"{tag}_ldm", f"{tag}_rc"

    P.open_scope()
    xp = [sb(f"{tag}_xp{b}", [128, DT, PN], BF16) for b in range(2)]
    gw = sb(f"{tag}_gw", [128, DT, 8], BF16)
    lg = sb(f"{tag}_lg", [8, T], F32)
    ltok = sb(f"{tag}_lt", [128, NTI, 8], F32)
    m1a = sb(f"{tag}_m1", [128, NTI], F32)
    m2a = sb(f"{tag}_m2", [128, NTI], F32)
    i1a = sb(f"{tag}_i1", [128, NTI], F32)
    i2a = sb(f"{tag}_i2", [128, NTI], F32)
    rw1 = sb(f"{tag}_rw1", [128, NTI], F32)
    rw2 = sb(f"{tag}_rw2", [128, NTI], F32)
    scr_all = sb(f"{tag}_scrA", [128, NTI, 8], F32)
    scr2_all = sb(f"{tag}_scrB", [128, NTI, 8], F32)
    scr_e = sb(f"{tag}_scrE", [8, 4], F32)
    rw2b1 = sb(f"{tag}_rb1", [128, NTI], F32)
    rw2b2 = sb(f"{tag}_rb2", [128, NTI], F32)
    Rt = sb(f"{tag}_R", [128, NTI, 8], F32)
    eaw_s = sb(f"{tag}_eaw", [8, 64], F32)
    sm64 = sb(f"{tag}_sm64", [64, 8], BF16)
    atsl = [sb(f"{tag}_at{b}", [64, 2048], BF16) for b in range(2)]
    imsl = [sb(f"{tag}_im{b}", [8, 2048], F32) for b in range(2)]
    sgsl = sb(f"{tag}_sg", [8, 2048], F32)
    plog = P.psum(f"{tag}_plog", [8, PN], F32)
    ptr = P.psum(f"{tag}_ptr", [128, 8], F32)
    prT = P.psum(f"{tag}_prT", [8, 128], F32)
    ptr2 = P.psum(f"{tag}_ptr2", [64, 8], F32)
    pdel = P.psum(f"{tag}_pdel", [8, PN], F32)

    ldx = [f"{tag}_lx0", f"{tag}_lx1"]
    lda = [f"{tag}_la0", f"{tag}_la1"]
    (ldg, ldeaw, mmL, clipd, trS, trE, top2d, rwd, Rd, rTt, rTe, smd, smTd,
     smTe, deld, sigd, wtdd, esub, eexp, dsub, dexp) = [f"{tag}_{x}" for x in (
        "ldg", "lde", "mmL", "cl", "trS", "trE", "t2", "rw", "Rd", "rTt", "rTe",
        "sm", "sT", "sTe", "dl", "sgd", "wt", "es", "ee", "ds", "de")]
    ldx_tot = [DT * 16 * (p // 2 + 1) for p in range(NPAN)]
    lda_tot = [2 * 16 * (i // 2 + 1) for i in range(8)]

    with nc.Block() as blk:
        def g_body(g):
            P.inc(ldm, g.dma_start(midx[:], midx_ap[:]), 16)
            for k in range(DT):
                P.inc(ldg, g.dma_start(gw[:, k, :], gate_w_ap[k * 128:(k + 1) * 128, :]), 16)
            P.inc(ldeaw, g.dma_start(eaw_s[:], eaw_ap[:]), 16)
            for p in range(NPAN):
                b = p % 2
                if p >= 2:
                    g.wait_ge(P.sem(mmL), p - 1)
                for k in range(DT):
                    P.inc(ldx[b], g.dma_start(xp[b][:, k, :],
                                              xln_ap[k * 128:(k + 1) * 128,
                                                     p * PN:(p + 1) * PN]), 16)
                assert P.n(ldx[b]) == ldx_tot[p]
            for i in range(8):
                b = i % 2
                if i >= 2:
                    g.wait_ge(P.sem(wtdd), (i - 1) * 4)
                P.inc(lda[b], g.dma_start(atsl[b][:], atoms_ap[:, i * 2048:(i + 1) * 2048]), 16)
                P.inc(lda[b], g.dma_start(imsl[b][:], imp_ap[:, i * 2048:(i + 1) * 2048]), 16)
                assert P.n(lda[b]) == lda_tot[i]
        blk.gpsimd(g_body)

        def t_body(t):
            t.wait_ge(P.sem(ldg), DT * 16)
            for p in range(NPAN):
                b = p % 2
                t.wait_ge(P.sem(ldx[b]), ldx_tot[p])
                if p >= 1:
                    t.wait_ge(P.sem(clipd), p)
                for k in range(DT):
                    last = t.matmul(plog[:], gw[:, k, :], xp[b][:, k, :],
                                    start=(k == 0), stop=(k == DT - 1))
                P.inc(mmL, last)
            for ti in range(NTI):
                t.wait_ge(P.sem(clipd), (ti // 4) + 1)
                if ti >= 1:
                    t.wait_ge(P.sem(trE), ti)
                P.inc(trS, t.transpose(ptr[:], lg[0:8, ti * 128:(ti + 1) * 128],
                                       C["ident_f32"][0:8, 0:8]))
            t.wait_ge(P.sem(smd), 1)
            P.inc(smTd, t.transpose(ptr2[:], eaw_s[:], C["ident_f32"][0:8, 0:8]))
            for i in range(8):
                b = i % 2
                t.wait_ge(P.sem(lda[b]), lda_tot[i])
                t.wait_ge(P.sem(smTe), 1)
                for q in range(4):
                    idx = i * 4 + q
                    if idx >= 1:
                        t.wait_ge(P.sem(wtdd), idx)
                    P.inc(deld, t.matmul(pdel[:, 0:512], sm64[:],
                                         atsl[b][:, q * 512:(q + 1) * 512],
                                         start=True, stop=True))
            for ti in range(NTI):
                t.wait_ge(P.sem(Rd), 8)
                if ti >= 1:
                    t.wait_ge(P.sem(rTe), ti)
                P.inc(rTt, t.transpose(prT[:], Rt[:, ti, :], C["ident_f32"][:]))
        blk.tensor(t_body)

        def v_body(v):
            for p in range(NPAN):
                v.wait_ge(P.sem(mmL), p + 1)
                P.inc(clipd, v.tensor_scalar(lg[:, p * PN:(p + 1) * PN], plog[:],
                                             50.0, -50.0, ALU.min, ALU.max))
            v.wait_ge(P.sem(ldeaw), 16)
            v.wait_ge(P.sem(trE), NTI)
            v.reduce_max(scr_e[0:8, 0:1], eaw_s[:], axis=AX)
            for ti in range(NTI):
                v.reduce_max(m1a[:, ti:ti + 1], ltok[:, ti, :], axis=AX)
            mxb = bass.AP(scr_e, 0, [[4, 8], [0, 64]])
            P.inc(esub, v.tensor_tensor(eaw_s[:], eaw_s[:], mxb, ALU.subtract))
            for ti in range(NTI):
                m1b = bass.AP(m1a, ti, [[NTI, 128], [0, 8]])
                v.tensor_tensor(scr_all[:, ti, :], ltok[:, ti, :], m1b, ALU.is_equal)
            v.wait_ge(P.sem(eexp), 1)
            v.reduce_sum(scr_e[0:8, 1:2], eaw_s[:], axis=AX)
            for ti in range(NTI):
                v.scalar_tensor_tensor(scr2_all[:, ti, :], scr_all[:, ti, :],
                                       -1000.0, C["iota8"][:], ALU.mult, ALU.add)
            v.reciprocal(scr_e[0:8, 2:3], scr_e[0:8, 1:2])
            for ti in range(NTI):
                v.tensor_reduce(i1a[:, ti:ti + 1], scr2_all[:, ti, :], axis=AX,
                                op=ALU.min)
            rb = bass.AP(scr_e, 2, [[4, 8], [0, 64]])
            P.inc(smd, v.tensor_tensor(eaw_s[:], eaw_s[:], rb, ALU.mult))
            for ti in range(NTI):
                i1b = bass.AP(i1a, ti, [[NTI, 128], [0, 8]])
                v.tensor_tensor(scr_all[:, ti, :], scr2_all[:, ti, :], i1b,
                                ALU.is_equal)
            for ti in range(NTI):
                v.scalar_tensor_tensor(scr2_all[:, ti, :], scr_all[:, ti, :],
                                       -1e30, ltok[:, ti, :], ALU.mult, ALU.add)
            for ti in range(NTI):
                v.reduce_max(m2a[:, ti:ti + 1], scr2_all[:, ti, :], axis=AX)
            for ti in range(NTI):
                m2b = bass.AP(m2a, ti, [[NTI, 128], [0, 8]])
                v.tensor_tensor(scr_all[:, ti, :], scr2_all[:, ti, :], m2b,
                                ALU.is_equal)
            for ti in range(NTI):
                v.scalar_tensor_tensor(scr_all[:, ti, :], scr_all[:, ti, :],
                                       -1000.0, C["iota8"][:], ALU.mult, ALU.add)
            for ti in range(NTI):
                v.tensor_reduce(i2a[:, ti:ti + 1], scr_all[:, ti, :], axis=AX,
                                op=ALU.min)
            v.tensor_tensor(m2a[:], m2a[:], m1a[:], ALU.subtract)
            P.inc(dsub, v.tensor_scalar_add(i1a[:], i1a[:], 1000.0))
            v.tensor_scalar_add(i2a[:], i2a[:], 1000.0)
            v.wait_ge(P.sem(ldm), 16)
            v.wait_ge(P.sem(dexp), 1)
            v.tensor_scalar_add(rw2[:], m2a[:], 1.0)
            v.tensor_single_scalar(cols_i[:], midx[:], 2047, ALU.bitwise_and)
            v.tensor_single_scalar(midx[:], midx[:], 11, ALU.logical_shift_right)
            v.reciprocal(rw1[:], rw2[:])
            P.inc(rcd, v.tensor_copy(rows_f[:], midx[:]))
            P.inc(rwd, v.tensor_tensor(rw2[:], m2a[:], rw1[:], ALU.mult))
            for e in range(0, 8, 2):
                v.tensor_single_scalar(scr_all[:, 0, :][:, 0:32] if False else m2a[:],
                                       i1a[:], float(e), ALU.is_equal)
                v.tensor_single_scalar(m1a[:], i2a[:], float(e), ALU.is_equal)
                v.tensor_single_scalar(scr_all[:, 0:4, :].rearrange("p a b -> p (a b)")[:, 0:32] if False else rw2b1[:],
                                       i1a[:], float(e + 1), ALU.is_equal)
                v.tensor_single_scalar(rw2b2[:], i2a[:], float(e + 1), ALU.is_equal)
                v.tensor_tensor(m2a[:], m2a[:], rw1[:], ALU.mult)
                v.tensor_tensor(m1a[:], m1a[:], rw2[:], ALU.mult)
                v.tensor_tensor(rw2b1[:], rw2b1[:], rw1[:], ALU.mult)
                v.tensor_tensor(rw2b2[:], rw2b2[:], rw2[:], ALU.mult)
                P.inc(Rd, v.tensor_tensor(Rt[:, :, e], m2a[:], m1a[:], ALU.add))
                P.inc(Rd, v.tensor_tensor(Rt[:, :, e + 1], rw2b1[:], rw2b2[:],
                                          ALU.add))
            for i in range(8):
                for q in range(4):
                    v.wait_ge(P.sem(deld), i * 4 + q + 1)
                    v.wait_ge(P.sem(sigd), i + 1)
                    P.inc(wtdd, v.tensor_tensor(
                        weighted[:, i * 2048 + q * 512:i * 2048 + (q + 1) * 512],
                        pdel[:, 0:512], sgsl[:, q * 512:(q + 1) * 512], ALU.mult))
        blk.vector(v_body)

        def sc_body(s):
            for ti in range(NTI):
                s.wait_ge(P.sem(trS), ti + 1)
                P.inc(trE, s.activation(ltok[:, ti, :], ptr[:], AF.Copy))
            s.wait_ge(P.sem(esub), 1)
            P.inc(eexp, s.activation(eaw_s[:], eaw_s[:], AF.Exp))
            s.wait_ge(P.sem(smTd), 1)
            P.inc(smTe, s.activation(sm64[:], ptr2[:], AF.Copy))
            s.wait_ge(P.sem(dsub), 1)
            P.inc(dexp, s.activation(m2a[:], m2a[:], AF.Exp))
            for i in range(8):
                b = i % 2
                s.wait_ge(P.sem(lda[b]), lda_tot[i])
                if i >= 1:
                    s.wait_ge(P.sem(wtdd), i * 4)
                P.inc(sigd, s.activation(sgsl[:], imsl[b][:], AF.Sigmoid))
            for ti in range(NTI):
                s.wait_ge(P.sem(rTt), ti + 1)
                P.inc(rTe, s.activation(rTb[0:8, ti * 128:(ti + 1) * 128], prT[:],
                                        AF.Copy))
        blk.scalar(sc_body)
    P.close_scope()

    P.open_scope()
    HW = T // 2
    xg = [sb(f"{tag}_xg{b}", [128, HW], BF16) for b in range(4)]
    vsl = sb(f"{tag}_vsl", [128, 2, HW], BF16)
    prod = [sb(f"{tag}_pr{b}", [128, HW], BF16) for b in range(3)]
    Pt = sb(f"{tag}_Pt", [128, 3, 128], BF16)
    scrP = sb(f"{tag}_scrP", [128, 1], F32)
    csb = sb(f"{tag}_csb", [128, 2, 4, PN], BF16)
    pcon = [P.psum(f"{tag}_pc{q}", [128, PN], F32) for q in range(4)]
    pval = P.psum(f"{tag}_pv", [128, 4, PN], F32)

    gth = [f"{tag}_gt{b}" for b in range(4)]
    cstq = [f"{tag}_cs0", f"{tag}_cs1"]
    vmm, vev, prd, pbd, stile, cev = [f"{tag}_{x}" for x in
                                      ("vmm", "vev", "prd", "pbd", "stile", "cev")]

    plan = []
    n = 0
    npart = 0
    nclose = 0
    parts_through = {}
    parts_before_close = {}
    for half in range(2):
        cur = None
        present = set()
        pend = None

        def flush(pend):
            nonlocal npart, nclose, cur
            if pend is None:
                return
            pn_, pst, blocks = pend
            for B in blocks:
                present.add(B)
                if cur is not None and B != cur:
                    parts_before_close[nclose] = npart
                    plan.append(("close", nclose, cur, half))
                    nclose += 1
                cur = B
                plan.append(("part", npart, pn_, pst, B))
                npart += 1
            parts_through[pn_] = npart

        for st_ in range(128):
            plan.append(("tile", half, st_, n))
            flush(pend)
            pend = (n, st_, tile_blocks[st_])
            n += 1
        flush(pend)
        pend = None
        parts_before_close[nclose] = npart
        plan.append(("close", nclose, cur, half))
        nclose += 1
        for B in range(16):
            if B not in present:
                plan.append(("zclose", nclose, B, half))
                nclose += 1
        cur = None
    NTOT = n

    with nc.Block() as blk:
        def g_body(g):
            g.wait_ge(P.sem(rcd), 1)
            for ev in plan:
                if ev[0] == "tile":
                    _, half, st_, nn = ev
                    b = nn % 4
                    if nn >= 4:
                        g.wait_ge(P.sem(prd), nn - 3)
                    P.inc(gth[b], g.indirect_dma_start(
                        xg[b][:], None, xln_ap[:],
                        IndirectOffsetOnAxis(cols_i[:, st_:st_ + 1], 0),
                        element_offset=half * HW), 16)
        blk.gpsimd(g_body)

        groups = []
        for ev in plan:
            if ev[0] == "tile":
                groups.append(("tile", ev))
            elif groups and groups[-1][0] == "grp":
                groups[-1][1].append(ev)
            else:
                groups.append(("grp", [ev]))
        plan_pe = []
        held = None
        for g in groups:
            if g[0] == "tile":
                plan_pe.append(g[1])
                if held is not None:
                    plan_pe.extend(held)
                    held = None
            else:
                held = g[1]
        if held is not None:
            plan_pe.extend(held)

        def v_body(v):
            for ev in plan:
                if ev[0] == "tile":
                    _, half, st_, nn = ev
                    b = nn % 4
                    v.wait_ge(P.sem(gth[b]), 16 * (nn // 4 + 1))
                    v.wait_ge(P.sem(vev), nn * 2 + 2)
                    if nn >= 3:
                        v.wait_ge(P.sem(stile), parts_through[nn - 3])
                    P.inc(prd, v.tensor_tensor(prod[nn % 3][:], xg[b][:],
                                               vsl[:, nn % 2, :], ALU.mult))
                elif ev[0] == "part":
                    _, np_, nn, st_, B = ev
                    if np_ >= 3:
                        v.wait_ge(P.sem(stile), np_ - 2)
                    v.tensor_single_scalar(scrP[:], rows_f[:, st_:st_ + 1],
                                           float(128 * B), ALU.subtract)
                    rcb = bass.AP(scrP, 0, [[1, 128], [0, 128]])
                    P.inc(pbd, v.tensor_tensor(Pt[:, np_ % 3, :], rcb,
                                               C["iota128"][:], ALU.is_equal))
        blk.vector(v_body)

        def t_body(t):
            t.wait_ge(P.sem(rTe), NTI)
            bank_open = [False] * 4
            pend_cev = 0
            for ev in plan_pe:
                if ev[0] == "tile":
                    _, half, st_, nn = ev
                    for q in range(4):
                        if nn >= 1:
                            t.wait_ge(P.sem(vev), 2 * (nn - 1) + q // 2 + 1)
                        P.inc(vmm, t.matmul(
                            pval[:, q, :], weighted[:, st_ * 128:(st_ + 1) * 128],
                            rTb[0:8, half * HW + q * PN:half * HW + (q + 1) * PN],
                            start=True, stop=True))
                elif ev[0] == "part":
                    _, np_, nn, st_, B = ev
                    if pend_cev:
                        t.wait_ge(P.sem(cev), pend_cev)
                        pend_cev = 0
                    t.wait_ge(P.sem(pbd), np_ + 1)
                    t.wait_ge(P.sem(prd), nn + 1)
                    for q in range(4):
                        last = t.matmul(pcon[q][:], Pt[:, np_ % 3, :],
                                        prod[nn % 3][:, q * PN:(q + 1) * PN],
                                        start=bank_open[q] is False, stop=False,
                                        skip_group_check=True)
                        bank_open[q] = True
                    P.inc(stile, last)
                elif ev[0] == "close":
                    _, ci, B, half = ev
                    pend_cev = (ci + 1) * 4
                    bank_open = [False] * 4
        blk.tensor(t_body)

        def sc_body(s):
            for ev in plan:
                if ev[0] == "tile":
                    _, half, st_, nn = ev
                    for h in range(2):
                        s.wait_ge(P.sem(vmm), nn * 4 + 2 * h + 2)
                        if nn >= 2 and h == 0:
                            s.wait_ge(P.sem(prd), nn - 1)
                        P.inc(vev, s.activation(
                            vsl[:, nn % 2, h * 2 * PN:(h + 1) * 2 * PN],
                            pval[:, 2 * h:2 * h + 2, :], AF.Copy))
                elif ev[0] == "close":
                    _, ci, B, half = ev
                    s.wait_ge(P.sem(stile), parts_before_close[ci])
                    if ci >= 2:
                        s.wait_ge(P.sem(cstq[ci % 2]), 4 * 16 * (ci // 2))
                    for q in range(4):
                        P.inc(cev, s.activation(csb[:, ci % 2, q, :], pcon[q][:],
                                                AF.Copy))
                elif ev[0] == "zclose":
                    _, ci, B, half = ev
                    if ci >= 2:
                        s.wait_ge(P.sem(cstq[ci % 2]), 4 * 16 * (ci // 2))
                    for q in range(4):
                        P.inc(cev, s.memzero(csb[:, ci % 2, q, :]))
        blk.scalar(sc_body)

        def s_out(sy):
            for ev in plan:
                if ev[0] in ("close", "zclose"):
                    ci, B, half = ev[1], ev[2], ev[3]
                    sy.wait_ge(P.sem(cev), (ci + 1) * 4)
                    for q in range(4):
                        P.inc(cstq[ci % 2], sy.dma_start(
                            contrib_ap[B * 128:(B + 1) * 128,
                                       half * HW + q * PN:half * HW + (q + 1) * PN],
                            csb[:, ci % 2, q, :]), 16)
            sy.wait_ge(P.sem(cstq[0]), P.n(cstq[0]))
            sy.wait_ge(P.sem(cstq[1]), P.n(cstq[1]))
        blk.sync(s_out)
    P.close_scope()



def build_l1ffn(xsrc_name, xdst_name, tile_blocks):
    P = Prog("l1ffn")
    nc = P.nc
    dram = decl_consts(P)
    x_in = nc.dram_tensor(xsrc_name, [D, T], F32, kind="ExternalInput").ap()
    gw = nc.dram_tensor("gate_wT", [D, 8], F32, kind="ExternalInput").ap()
    WT = nc.dram_tensor("WT", [D, D], F32, kind="ExternalInput").ap()
    eaw = nc.dram_tensor("eaw", [8, 64], F32, kind="ExternalInput").ap()
    atoms = nc.dram_tensor("atoms_s", [64, NS], F32, kind="ExternalInput").ap()
    imp = nc.dram_tensor("imp_s", [8, NS], F32, kind="ExternalInput").ap()
    midx = nc.dram_tensor("midx2d", [128, 128], I32, kind="ExternalInput").ap()
    x_out = nc.dram_tensor(xdst_name, [D, T], F32, kind="ExternalOutput").ap()
    xln = nc.dram_tensor("xln", [D, T], BF16, kind="Internal").ap()
    contrib = nc.dram_tensor("contrib", [D, T], BF16, kind="Internal").ap()
    emit_consts(P, dram)
    P.open_scope()
    emit_ln(P, "ln", x_in, xln)
    emit_barrier(P, "b1")
    P.close_scope()
    P.open_scope()
    emit_l1ffn_core(P, "lf", xln, gw, eaw, atoms, imp, midx, contrib, tile_blocks)
    emit_barrier(P, "b2")
    P.close_scope()
    P.open_scope()
    emit_matmul_residual(P, "wl", xln, WT, x_in, x_out, D, alpha=0.5,
                         contrib_ap=contrib)
    emit_barrier(P, "b3")
    P.close_scope()
    P.close()
    ins = CONST_NAMES + [xsrc_name, "gate_wT", "WT", "eaw", "atoms_s", "imp_s",
                         "midx2d"]
    return P.nc, ins, [xdst_name]


def l1ffn_host_plan(mask_idx):
    mask_idx = np.asarray(mask_idx)
    rows = mask_idx // D
    order = np.argsort(rows, kind="stable")
    midx_s = mask_idx[order]
    rows_s = rows[order]
    tile_blocks = []
    for st in range(128):
        r = rows_s[st * 128:(st + 1) * 128]
        blocks = sorted(set(int(b) for b in (r // 128)))
        tile_blocks.append(blocks)
    midx2d = np.ascontiguousarray(midx_s.reshape(128, 128).T).astype(np.int32)
    return order, midx2d, tile_blocks


def emit_gqa_proj(P, tag, xln_ap, wq_ap, wk_ap, wv_ap, qT_ap, kv_ap):
    nc = P.nc
    sb = P.sbuf
    C = P.consts
    G = 4
    NTK = 32
    RS = 0.08838834764831845
    xr = sb(f"{tag}_xr", [128, DT, T], BF16)
    wkb = sb(f"{tag}_wkb", [128, DT, 128], BF16)
    wvb = sb(f"{tag}_wvb", [128, DT, 128], BF16)
    wq = [sb(f"{tag}_wq{b}", [128, DT, 128], BF16) for b in range(2)]
    kstg = sb(f"{tag}_kstg", [128, 2, PN], BF16)
    vstage = sb(f"{tag}_vst", [128, T], BF16)
    vtstg = sb(f"{tag}_vts", [128, 2, 128], BF16)
    qstg = sb(f"{tag}_qstg", [128, 2, PN], BF16)
    ppj = [P.psum(f"{tag}_ppj{b}", [128, PN], F32) for b in range(2)]
    ptr = P.psum(f"{tag}_ptr", [128, 128], BF16)

    ldx, ldkv = f"{tag}_ldx", f"{tag}_ldkv"
    ldq = [f"{tag}_lq0", f"{tag}_lq1"]
    kstq = [f"{tag}_ks0", f"{tag}_ks1"]
    vstq = [f"{tag}_vs0", f"{tag}_vs1"]
    qstq = [f"{tag}_qs0", f"{tag}_qs1"]
    (kvmm, kev, ksq, ktr, ktrE, qmm, qev) = [
        f"{tag}_{x}" for x in ("kvmm", "kev", "ksq", "ktr", "ktrE",
                               "qmm", "qev")]

    with nc.Block() as blk:
        def g_body(g):
            for k in range(DT):
                P.inc(ldx, g.dma_start(xr[:, k, :], xln_ap[k * 128:(k + 1) * 128, :]), 16)
            for gi in range(G):
                if gi >= 1:
                    g.wait_ge(P.sem(kvmm), gi * 2 * NPAN)
                P.inc(ldkv, g.dma_start(wkb[:, :, :], bass.AP(
                    wk_ap.tensor, wk_ap.offset + gi * 128 * DT * 128,
                    [[DT * 128, 128], [1, DT * 128]])), 16)
                P.inc(ldkv, g.dma_start(wvb[:, :, :], bass.AP(
                    wv_ap.tensor, wv_ap.offset + gi * 128 * DT * 128,
                    [[DT * 128, 128], [1, DT * 128]])), 16)
            for h in range(NH):
                b = h % 2
                if h >= 2:
                    g.wait_ge(P.sem(qmm), (h - 1) * NPAN)
                P.inc(ldq[b], g.dma_start(wq[b][:, :, :], bass.AP(
                    wq_ap.tensor, wq_ap.offset + h * 128 * DT * 128,
                    [[DT * 128, 128], [1, DT * 128]])), 16)
        blk.gpsimd(g_body)

        def t_body(t):
            t.wait_ge(P.sem(ldx), DT * 16)
            for gi in range(G):
                t.wait_ge(P.sem(ldkv), (gi + 1) * 2 * 16)
                for pn in range(NPAN):
                    gp = gi * NPAN + pn
                    if gp >= 1:
                        t.wait_ge(P.sem(kev), gp)
                    for k in range(DT):
                        last = t.matmul(ppj[0][:], wkb[:, k, :],
                                        xr[:, k, pn * PN:(pn + 1) * PN],
                                        start=(k == 0), stop=(k == DT - 1))
                    P.inc(kvmm, last)
                    for k in range(DT):
                        last = t.matmul(ppj[1][:], wvb[:, k, :],
                                        xr[:, k, pn * PN:(pn + 1) * PN],
                                        start=(k == 0), stop=(k == DT - 1))
                    P.inc(kvmm, last)

                for ti in range(NTK):
                    gt = gi * NTK + ti
                    t.wait_ge(P.sem(kev), gi * NPAN + (ti // 4) + 1)
                    if gt >= 1:
                        t.wait_ge(P.sem(ktrE), gt)
                    P.inc(ktr, t.transpose(ptr[:],
                                           vstage[:, ti * 128:(ti + 1) * 128],
                                           C["ident_bf"][:]))
            for h in range(NH):
                b = h % 2
                t.wait_ge(P.sem(ldq[b]), 16 * ((h // 2) + 1))
                for pn in range(NPAN):
                    i = h * NPAN + pn
                    if i >= 2:
                        t.wait_ge(P.sem(qev), i - 1)
                    for k in range(DT):
                        last = t.matmul(ppj[i % 2][:], wq[b][:, k, :],
                                        xr[:, k, pn * PN:(pn + 1) * PN],
                                        start=(k == 0), stop=(k == DT - 1))
                    P.inc(qmm, last)
        blk.tensor(t_body)

        def sc_body(s):
            for gi in range(G):
                for pn in range(NPAN):
                    gp = gi * NPAN + pn
                    s.wait_ge(P.sem(kvmm), gp * 2 + 1)
                    if gp >= 2:
                        s.wait_ge(P.sem(kstq[gp % 2]), 16 * (gp // 2))
                    P.inc(ksq, s.activation(kstg[:, gp % 2, :], ppj[0][:], AF.Copy))
                    s.wait_ge(P.sem(kvmm), gp * 2 + 2)
                    if gi >= 1 and pn == 0:
                        s.wait_ge(P.sem(ktrE), gi * NTK)
                    P.inc(kev, s.activation(vstage[:, pn * PN:(pn + 1) * PN],
                                            ppj[1][:], AF.Copy))
                for ti in range(NTK):
                    gt = gi * NTK + ti
                    s.wait_ge(P.sem(ktr), gt + 1)
                    if gt >= 2:
                        s.wait_ge(P.sem(vstq[gt % 2]), 16 * (gt // 2))
                    P.inc(ktrE, s.activation(vtstg[:, gt % 2, :], ptr[:], AF.Copy))
            for h in range(NH):
                for pn in range(NPAN):
                    i = h * NPAN + pn
                    s.wait_ge(P.sem(qmm), i + 1)
                    if i >= 2:
                        s.wait_ge(P.sem(qstq[i % 2]), 16 * (i // 2))
                    P.inc(qev, s.activation(qstg[:, i % 2, :], ppj[i % 2][:],
                                            AF.Copy, scale=RS))
        blk.scalar(sc_body)

        def s_out(sy):
            for gi in range(G):
                for pn in range(NPAN):
                    gp = gi * NPAN + pn
                    sy.wait_ge(P.sem(ksq), gp + 1)
                    P.inc(kstq[gp % 2], sy.dma_start(
                        kv_ap[gi * 128:(gi + 1) * 128, pn * PN:(pn + 1) * PN],
                        kstg[:, gp % 2, :]), 16)
                for ti in range(NTK):
                    gt = gi * NTK + ti
                    sy.wait_ge(P.sem(ktrE), gt + 1)
                    P.inc(vstq[gt % 2], sy.dma_start(
                        kv_ap[512 + gi * 128:512 + (gi + 1) * 128,
                              ti * 128:(ti + 1) * 128],
                        vtstg[:, gt % 2, :]), 16)
            for h in range(NH):
                for pn in range(NPAN):
                    i = h * NPAN + pn
                    sy.wait_ge(P.sem(qev), i + 1)
                    P.inc(qstq[i % 2], sy.dma_start(
                        qT_ap[h * 128:(h + 1) * 128, pn * PN:(pn + 1) * PN],
                        qstg[:, i % 2, :]), 16)
            for q_ in (kstq[0], kstq[1], vstq[0], vstq[1], qstq[0], qstq[1]):
                sy.wait_ge(P.sem(q_), P.n(q_))
        blk.sync(s_out)


def emit_gqa_attn(P, tag, qT_ap, kv_ap, ctx_ap, var_ap):
    nc = P.nc
    sb = P.sbuf
    C = P.consts
    G = 4
    NTK = 32
    NPS = 4
    NES = 8
    L = 4
    kg = sb(f"{tag}_kg", [128, 2, T], BF16)
    vg = sb(f"{tag}_vg", [128, 2, NTK, 128], BF16)
    qh = sb(f"{tag}_qh", [128, 2, T], BF16)
    exps = sb(f"{tag}_expS", [128, NES, PN], BF16)
    dacc = sb(f"{tag}_dacc", [128, 2, PN], F32)
    rrow = sb(f"{tag}_rrow", [1, 2, PN], F32)
    rbx = sb(f"{tag}_rbx", [128, 2, PN], F32)
    cstg = sb(f"{tag}_cstg", [128, 2, PN], BF16)
    pS = P.psum(f"{tag}_pS", [128, NPS, PN], F32)
    pctx = [P.psum(f"{tag}_pctx{b}", [128, PN], F32) for b in range(2)]
    prow = P.psum(f"{tag}_prow", [1, 2, PN], F32)

    ldk = [f"{tag}_lk0", f"{tag}_lk1"]
    ldqh = [f"{tag}_lh0", f"{tag}_lh1"]
    cstq = [f"{tag}_ct0", f"{tag}_ct1"]
    rstq = [f"{tag}_rs0", f"{tag}_rs1"]
    rbxq = [f"{tag}_rb0", f"{tag}_rb1"]
    (smm, expd, dad, cmm, dsm, rrd, ctxd) = [
        f"{tag}_{x}" for x in ("smm", "exp", "dad", "cmm", "dsm", "rrd", "ctd")]

    NU = NH * NPAN
    def upn(u):
        return u // NPAN, u % NPAN
    def untk(u):
        return 4 * (u % NPAN) + 4
    s_base = {}
    si = 0
    tiles = []
    for u in range(NU):
        s_base[u] = si
        for ti in range(untk(u)):
            tiles.append((u, ti))
        si += untk(u)
    s_base[NU] = si
    NT = si

    with nc.Block() as blk:
        def g_body(g):
            def rbx_load(w):
                g.wait_ge(P.sem(rstq[w % 2]), 16 * (w // 2 + 1))
                if w >= 2:
                    g.wait_ge(P.sem(ctxd), w - 1)
                src = bass.AP(var_ap.tensor, var_ap.offset + (w % 2) * PN,
                              [[0, 128], [1, PN]])
                P.inc(rbxq[w % 2], g.dma_start(rbx[:, w % 2, :], src), 16)
            for u in range(NU):
                h, pn = upn(u)
                gi = h // 4
                if pn == 0:
                    b = h % 2
                    if h >= 2:
                        g.wait_ge(P.sem(ctxd), (h - 1) * NPAN)
                    P.inc(ldqh[b], g.dma_start(qh[:, b, :],
                                               qT_ap[h * 128:(h + 1) * 128, :]), 16)
                    if h % 4 == 0:
                        gb = gi % 2
                        if gi >= 2:
                            g.wait_ge(P.sem(ctxd), 4 * (gi - 1) * NPAN)
                        P.inc(ldk[gb], g.dma_start(kg[:, gb, :],
                                                   kv_ap[gi * 128:(gi + 1) * 128, :]), 16)
                        P.inc(ldk[gb], g.dma_start(
                            vg[:, gb, :, :],
                            kv_ap[512 + gi * 128:512 + (gi + 1) * 128, :]), 16)
                if u >= 1:
                    rbx_load(u - 1)
            rbx_load(NU - 1)
        blk.gpsimd(g_body)

        def t_body(t):
            def s_mm(p):
                u, ti = tiles[p]
                h, pn = upn(u)
                gi = h // 4
                if pn == 0 and ti == 0:
                    t.wait_ge(P.sem(ldk[gi % 2]), 32 * (gi // 2 + 1))
                    t.wait_ge(P.sem(ldqh[h % 2]), 16 * (h // 2 + 1))
                if p >= NPS:
                    t.wait_ge(P.sem(expd), p // 2 - 1)
                if ti >= 4 * pn:
                    t.matmul(pS[:, p % NPS, :], C["ident_bf"][:],
                             C["cmadd"][:, ti - 4 * pn, :], start=True, stop=False)
                    P.inc(smm, t.matmul(pS[:, p % NPS, :],
                                        kg[:, gi % 2, ti * 128:(ti + 1) * 128],
                                        qh[:, h % 2, pn * PN:(pn + 1) * PN],
                                        start=False, stop=True))
                else:
                    P.inc(smm, t.matmul(pS[:, p % NPS, :],
                                        kg[:, gi % 2, ti * 128:(ti + 1) * 128],
                                        qh[:, h % 2, pn * PN:(pn + 1) * PN],
                                        start=True, stop=True))
            def ctx_mm(p):
                u, ti = tiles[p]
                h, pn = upn(u)
                gi = h // 4
                ntk = untk(u)
                t.wait_ge(P.sem(expd), p // 2 + 1)
                if ti == 0 and u >= 2:
                    t.wait_ge(P.sem(ctxd), u - 1)
                P.inc(cmm, t.matmul(pctx[u % 2][:], vg[:, gi % 2, ti, :],
                                    exps[:, p % NES, :],
                                    start=(ti == 0), stop=(ti == ntk - 1)))
            def tail_pe(w):
                t.wait_ge(P.sem(dad), s_base[w + 1])
                if w >= 2:
                    t.wait_ge(P.sem(rrd), w - 1)
                P.inc(dsm, t.matmul(prow[0:1, w % 2, :], C["ones_f32"][:],
                                    dacc[:, w % 2, :], start=True, stop=True))
            for p in range(NT):
                s_mm(p)
                if p >= L:
                    ctx_mm(p - L)
                u, ti = tiles[p]
                if u >= 1 and ti == min(4, untk(u) - 1):
                    tail_pe(u - 1)
            for p in range(NT - L, NT):
                ctx_mm(p)
            tail_pe(NU - 1)
        blk.tensor(t_body)

        def sc_body(s):
            def rrow_chain(w):
                s.wait_ge(P.sem(dsm), w + 1)
                s.activation(rrow[0:1, w % 2, :], prow[0:1, w % 2, :], AF.Ln)
                P.inc(rrd, s.activation(rrow[0:1, w % 2, :],
                                        rrow[0:1, w % 2, :], AF.Exp, scale=-1.0))
            for j in range(NT // 2):
                p = 2 * j
                s.wait_ge(P.sem(smm), p + 2)
                if p >= NES:
                    s.wait_ge(P.sem(cmm), p + 2 - NES)
                    s.wait_ge(P.sem(dad), p + 2 - NES)
                P.inc(expd, s.activation(exps[:, p % NES:p % NES + 2, :],
                                         pS[:, p % NPS:p % NPS + 2, :], AF.Exp))
                u, ti = tiles[p + 1]
                if u >= 1 and ti == min(5, untk(u) - 1):
                    rrow_chain(u - 1)
            rrow_chain(NU - 1)
        blk.scalar(sc_body)

        def v_body(v):
            def ctx_norm(w):
                v.wait_ge(P.sem(cmm), s_base[w + 1])
                v.wait_ge(P.sem(rbxq[w % 2]), 16 * (w // 2 + 1))
                if w >= 2:
                    v.wait_ge(P.sem(cstq[w % 2]), 16 * (w // 2))
                P.inc(ctxd, v.tensor_tensor(cstg[:, w % 2, :], pctx[w % 2][:],
                                            rbx[:, w % 2, :], ALU.mult))
            for p in range(NT):
                u, ti = tiles[p]
                ntk = untk(u)
                v.wait_ge(P.sem(expd), p // 2 + 1)
                if ti == 0:
                    if u >= 2:
                        v.wait_ge(P.sem(dsm), u - 1)
                    P.inc(dad, v.tensor_copy(dacc[:, u % 2, :],
                                             exps[:, p % NES, :]))
                else:
                    P.inc(dad, v.tensor_tensor(dacc[:, u % 2, :],
                                               dacc[:, u % 2, :],
                                               exps[:, p % NES, :], ALU.add))
                if u >= 1 and ti == min(10, ntk - 1):
                    ctx_norm(u - 1)
            ctx_norm(NU - 1)
        blk.vector(v_body)

        def s_out(sy):
            for u in range(NU):
                h, pn = upn(u)
                sy.wait_ge(P.sem(rrd), u + 1)
                if u >= 2:
                    sy.wait_ge(P.sem(rbxq[u % 2]), 16 * (u // 2))
                P.inc(rstq[u % 2], sy.dma_start(
                    var_ap[0:1, (u % 2) * PN:(u % 2 + 1) * PN],
                    rrow[0:1, u % 2, :]), 16)
                if u >= 1:
                    w = u - 1
                    hw, pw = upn(w)
                    sy.wait_ge(P.sem(ctxd), w + 1)
                    P.inc(cstq[w % 2], sy.dma_start(
                        ctx_ap[hw * 128:(hw + 1) * 128, pw * PN:(pw + 1) * PN],
                        cstg[:, w % 2, :]), 16)
            w = NU - 1
            hw, pw = upn(w)
            sy.wait_ge(P.sem(ctxd), w + 1)
            P.inc(cstq[w % 2], sy.dma_start(
                ctx_ap[hw * 128:(hw + 1) * 128, pw * PN:(pw + 1) * PN],
                cstg[:, w % 2, :]), 16)
            for q_ in (cstq[0], cstq[1], rstq[0], rstq[1]):
                sy.wait_ge(P.sem(q_), P.n(q_))
        blk.sync(s_out)


def build_l1attn(xsrc_name, xdst_name):
    P = Prog("l1attn")
    nc = P.nc
    dram = decl_consts(P)
    x_in = nc.dram_tensor(xsrc_name, [D, T], F32, kind="ExternalInput").ap()
    wqa = nc.dram_tensor("Wq", [D, D], F32, kind="ExternalInput").ap()
    wka = nc.dram_tensor("Wk", [D, 512], F32, kind="ExternalInput").ap()
    wva = nc.dram_tensor("Wv", [D, 512], F32, kind="ExternalInput").ap()
    woa = nc.dram_tensor("Wo", [D, D], F32, kind="ExternalInput").ap()
    bo2d = nc.dram_tensor("bo2d", [128, 16], F32, kind="ExternalInput").ap()
    x_out = nc.dram_tensor(xdst_name, [D, T], F32, kind="ExternalOutput").ap()
    xln = nc.dram_tensor("xln", [D, T], BF16, kind="Internal").ap()
    ctx = nc.dram_tensor("ctxbuf", [D, T], BF16, kind="Internal").ap()
    qT = nc.dram_tensor("qTbuf", [D, T], BF16, kind="Internal").ap()
    kv = nc.dram_tensor("kvbuf", [1024, T], BF16, kind="Internal").ap()
    varb = nc.dram_tensor("varbuf", [1, T], F32, kind="Internal").ap()
    emit_consts(P, dram)
    P.open_scope()
    emit_ln(P, "ln", x_in, xln)
    emit_barrier(P, "b1")
    P.close_scope()
    P.open_scope()
    emit_gqa_proj(P, "gp", xln, wqa, wka, wva, qT, kv)
    emit_barrier(P, "b2a")
    P.close_scope()
    P.open_scope()
    emit_gqa_attn(P, "ga", qT, kv, ctx, varb)
    emit_barrier(P, "b2")
    P.close_scope()
    P.open_scope()
    emit_matmul_residual(P, "wo2", ctx, woa, x_in, x_out, D, alpha=1.0,
                         bias_ap=bo2d)
    emit_barrier(P, "b3")
    P.close_scope()
    P.close()
    ins = CONST_NAMES + [xsrc_name, "Wq", "Wk", "Wv", "Wo", "bo2d"]
    return P.nc, ins, [xdst_name]


def build_full(tile_blocks):
    P = Prog("full")
    nc = P.nc
    dram = decl_consts(P)
    x_in = nc.dram_tensor("x_in", [D, T], F32, kind="ExternalInput").ap()
    pk_gu = nc.dram_tensor("pk_gu", [256, GU_OUT], I32, kind="ExternalInput").ap()
    sc_gu = nc.dram_tensor("sc_gu", [16, GU_OUT], F16, kind="ExternalInput").ap()
    pk_dn = nc.dram_tensor("pk_dn", [704, D], I32, kind="ExternalInput").ap()
    sc_dn = nc.dram_tensor("sc_dn", [44, D], F16, kind="ExternalInput").ap()
    wqkv = nc.dram_tensor("wqkvT", [D, 3 * D], F32, kind="ExternalInput").ap()
    slope = nc.dram_tensor("slope_row", [1, 16], F32, kind="ExternalInput").ap()
    wgate = nc.dram_tensor("wgateT", [D, D], F32, kind="ExternalInput").ap()
    wout = nc.dram_tensor("woutT", [D, D], F32, kind="ExternalInput").ap()
    normw = nc.dram_tensor("normw2d", [128, 16], F32, kind="ExternalInput").ap()
    gwT = nc.dram_tensor("gate_wT", [D, 8], F32, kind="ExternalInput").ap()
    WT = nc.dram_tensor("WT", [D, D], F32, kind="ExternalInput").ap()
    eaw = nc.dram_tensor("eaw", [8, 64], F32, kind="ExternalInput").ap()
    atoms = nc.dram_tensor("atoms_s", [64, NS], F32, kind="ExternalInput").ap()
    imp = nc.dram_tensor("imp_s", [8, NS], F32, kind="ExternalInput").ap()
    midx = nc.dram_tensor("midx2d", [128, 128], I32, kind="ExternalInput").ap()
    wqa = nc.dram_tensor("Wq", [D, D], F32, kind="ExternalInput").ap()
    wka = nc.dram_tensor("Wk", [D, 512], F32, kind="ExternalInput").ap()
    wva = nc.dram_tensor("Wv", [D, 512], F32, kind="ExternalInput").ap()
    woa = nc.dram_tensor("Wo", [D, D], F32, kind="ExternalInput").ap()
    bo2d = nc.dram_tensor("bo2d", [128, 16], F32, kind="ExternalInput").ap()
    x_out = nc.dram_tensor("x_out", [D, T], F32, kind="ExternalOutput").ap()
    wgu = nc.dram_tensor("wgu", [D, GU_OUT], BF16, kind="Internal").ap()
    wdn = nc.dram_tensor("wdn", [I_FF, D], BF16, kind="Internal").ap()
    xln = nc.dram_tensor("xln", [D, T], BF16, kind="Internal").ap()
    hbuf = nc.dram_tensor("hbuf", [I_FF, T], BF16, kind="Internal").ap()
    kv = nc.dram_tensor("kvbuf", [1024, T], BF16, kind="Internal").ap()
    contrib = nc.dram_tensor("contrib", [D, T], BF16, kind="Internal").ap()
    varb = nc.dram_tensor("varbuf", [1, T], F32, kind="Internal").ap()
    xA = nc.dram_tensor("xA", [D, T], F32, kind="Internal").ap()
    xB = nc.dram_tensor("xB", [D, T], F32, kind="Internal").ap()
    qkvb = nc.dram_tensor("qkvbuf", [3 * D, T], BF16, kind="Internal").ap()
    outh = hbuf[0:D, :]
    zbuf = hbuf[D:2 * D, :]
    ctx = hbuf[0:D, :]
    qT = hbuf[D:2 * D, :]

    emit_consts(P, dram)

    def scoped(fn, *args, **kw):
        P.open_scope()
        fn(P, *args, **kw)
        P.close_scope(recycle=True)

    scoped(emit_ln, "ln1", x_in, xln)
    scoped(emit_gateup_dq, "fa1", xln, pk_gu, sc_gu, pk_dn, sc_dn, wgu, wdn,
           hbuf)
    scoped(emit_l0ffn_down, "fb1", hbuf, wdn, x_in, xA)
    scoped(emit_ln, "ln2", xA, xln)
    scoped(emit_lightning_proj, "lp2", xln, wqkv, qkvb)
    scoped(emit_lightning_scan, "ls2", qkvb, slope, outh)
    scoped(emit_lightning_out, "lo2", xln, outh, varb, wgate, normw, zbuf)
    scoped(emit_matmul_residual, "wo2", zbuf, wout, xA, xB, D, alpha=1.0)
    scoped(emit_ln, "ln3", xB, xln)
    scoped(emit_l0ffn_gateup, "fa3", xln, wgu, hbuf)
    scoped(emit_l0ffn_down, "fb3", hbuf, wdn, xB, xA)
    scoped(emit_ln, "ln4", xA, xln)
    scoped(emit_l1ffn_core, "lf4", xln, gwT, eaw, atoms, imp, midx, contrib,
           tile_blocks)
    scoped(emit_matmul_residual, "wl4", xln, WT, xA, xB, D, alpha=0.5,
           contrib_ap=contrib)
    scoped(emit_ln, "ln5", xB, xln)
    scoped(emit_gqa_proj, "gp5", xln, wqa, wka, wva, qT, kv)
    scoped(emit_gqa_attn, "ga5", qT, kv, ctx, varb)
    scoped(emit_matmul_residual, "wo5", ctx, woa, xB, xA, D, alpha=1.0,
           bias_ap=bo2d)
    scoped(emit_ln, "ln6", xA, xln)
    scoped(emit_l1ffn_core, "lf6", xln, gwT, eaw, atoms, imp, midx, contrib,
           tile_blocks)
    scoped(emit_matmul_residual, "wl6", xln, WT, xA, x_out, D, alpha=0.5,
           contrib_ap=contrib)
    P.close()
    ins = CONST_NAMES + ["x_in", "pk_gu", "sc_gu", "pk_dn", "sc_dn", "wqkvT",
                         "slope_row", "wgateT", "woutT", "normw2d", "gate_wT",
                         "WT", "eaw", "atoms_s", "imp_s", "midx2d", "Wq", "Wk",
                         "Wv", "Wo", "bo2d"]
    return P.nc, ins, ["x_out"]


def kernel_fused(**inputs):
    shared, percore = host_prep(inputs)
    tile_blocks = shared.pop("_tile_blocks")
    state = [dict(shared, **percore[c]) for c in range(8)]
    prog = build_full(tile_blocks)
    run_launch(prog, state)
    return np.stack([np.asarray(state[c]["x_out"], dtype=np.float32).T
                     for c in range(8)])
